# version 02
# baseline (speedup 1.0000x reference)
import math

import numpy as np
import ml_dtypes

import concourse.bass as bass
import concourse.mybir as mybir
from concourse.bass_utils import run_bass_kernel_spmd

F32 = mybir.dt.float32
BF16 = mybir.dt.bfloat16
AF = mybir.ActivationFunctionType
ALU = mybir.AluOpType
AX = mybir.AxisListType

D = 4096
S = 4096
NB = 4
NH = 32
NKV = 8
HD = 128
KVD = NKV * HD
NE = 8
OWN = 2048
HALF = 1024
NQC = 4
NT_HALF = 8
RSCALE = float(1.0 / math.sqrt(HD))
LN_EPS = 1e-5

OWN_QCS = ([0, 3, 4, 7], [1, 2, 5, 6])
NK_UNIF = [8, 16, 24, 32]
SLOTS_P = [[0, 4], [0, 1, 4, 5], [0, 1, 2, 4, 5, 6], [0, 1, 2, 3, 4, 5, 6, 7]]

W_OC = 256
WO_OC = 256

B_PROJS = ("q", "k", "v")
B_ROPE = True
B_QNORM = True
A_TRANSPOSE = True
SBUF_TOP = 229344



def _tile_lhsT(wt, oc):
    f, o = wt.shape
    return np.ascontiguousarray(
        wt.reshape(f // 128, 128, o // oc, oc).transpose(2, 1, 0, 3).reshape(o // oc, 128, -1)
    )


def _rope_tables(positions):
    pos = np.asarray(positions, dtype=np.float64)[None, :]
    half = np.arange(64, dtype=np.float64)
    freqs = np.exp(-2.0 * half * (math.log(10000.0) / HD))[:, None]
    ang = pos * freqs
    cos = np.concatenate([np.cos(ang), np.cos(ang)], axis=0).astype(np.float32)
    sin = np.concatenate([np.sin(ang), np.sin(ang)], axis=0).astype(np.float32)
    return np.ascontiguousarray(cos), np.ascontiguousarray(sin)


def _masks_for(hg):
    own = OWN_QCS[hg]
    part = OWN_QCS[1 - hg]
    g_order = own + part
    out = np.zeros((NQC, 128, 4096), dtype=np.float32)
    kp = np.arange(128)[:, None]
    qf = np.arange(512)[None, :]
    for p in range(NQC):
        gq = own[p]
        for mi, slot in enumerate((p, p + 4)):
            g = g_order[slot]
            for j0 in range(4):
                if g < gq:
                    m = np.ones((128, 512), np.float32)
                elif g > gq:
                    m = np.zeros((128, 512), np.float32)
                else:
                    m = ((kp + j0 * 128) <= qf).astype(np.float32)
                out[p, :, mi * 2048 + j0 * 512: mi * 2048 + (j0 + 1) * 512] = m
    return np.ascontiguousarray(out.astype(ml_dtypes.bfloat16))


def _rope_perm_T():
    p = np.zeros((128, 128), np.float32)
    for dd in range(64):
        p[dd, dd + 64] = -1.0
        p[dd + 64, dd] = 1.0
    return np.ascontiguousarray(p.T.astype(ml_dtypes.bfloat16))



class Buf:
    __slots__ = ("name", "wtok", "rtoks")

    def __init__(self, name, carry=()):
        self.name = name
        self.wtok = None
        self.rtoks = list(carry)


class Sched:
    ENGINES = ("tensor", "vector", "scalar", "gpsimd", "sync")

    def __init__(self, nc):
        self.nc = nc
        self.ops = {e: [] for e in self.ENGINES}
        self.sem = {}
        self.cnt = {}
        self.waited = {e: {} for e in self.ENGINES}
        for e in ("tensor", "vector", "scalar", "gpsimd"):
            self.sem[e] = nc.alloc_semaphore(f"sem_{e}")
            self.cnt[e] = 0
        self._sn = 0

    def dma_slot(self, name):
        self.sem[name] = self.nc.alloc_semaphore(f"ds_{name}")
        self.cnt[name] = 0
        return name

    def _wait(self, eng, tok):
        if tok is None:
            return
        if eng == "tensor" and tok[0] == "tensor":
            return
        prev = self.waited[eng].get(tok[0], 0)
        if tok[1] <= prev:
            return
        self.waited[eng][tok[0]] = tok[1]
        sem = self.sem[tok[0]]
        val = tok[1]
        self.ops[eng].append(lambda e, s=sem, v=val: e.wait_ge(s, v))

    def op(self, eng, fn, reads=(), writes=(), slot=None, n_dma=1, self_inc=False):
        for b in reads:
            self._wait(eng, b.wtok)
        for b in writes:
            self._wait(eng, b.wtok)
            for t in b.rtoks:
                self._wait(eng, t)
        if slot is None:
            key, inc = eng, 1
            self.cnt[key] += 1
        else:
            key, inc = slot, 16
            self.cnt[key] += 16 * n_dma
        val = self.cnt[key]
        sem = self.sem[key]
        if self_inc:
            self.ops[eng].append(lambda e, f=fn, s=sem: f(e, s))
        else:
            self.ops[eng].append(lambda e, f=fn, s=sem, i=inc: f(e).then_inc(s, i))
        tok = (key, val)
        for b in reads:
            b.rtoks.append(tok)
        for b in writes:
            b.wtok = tok
            b.rtoks = []
        return tok

    def barrier_tokens(self, bufs):
        toks = []
        for b in bufs:
            if b.wtok is not None:
                toks.append(b.wtok)
            toks.extend(b.rtoks)
        return toks

    def finish(self, toks):
        for t in toks:
            self._wait("sync", t)



def build_program(phases="ABCDE"):
    nc = bass.Bass("TRN2", target_bir_lowering=False)

    def dram_in(name, shape, dtype=F32):
        return nc.dram_tensor(name, list(shape), dtype, kind="ExternalInput").ap()

    x_own = dram_in("x_own", (OWN, D))
    x_part = dram_in("x_part", (OWN, D))
    wq = dram_in("wq", (D // W_OC, 128, (D // 128) * W_OC))
    wk = dram_in("wk", (KVD // W_OC, 128, (D // 128) * W_OC))
    wv = dram_in("wv", (KVD // W_OC, 128, (D // 128) * W_OC))
    wo = dram_in("wo", (D // WO_OC, 128, (D // 128) * WO_OC))
    acat = dram_in("acat", (128, (D // 128) * 128))
    bcat = dram_in("bcat", (128, D))
    wrt = dram_in("wrt", (128, (D // 128) * 8))
    br_in = dram_in("br", (1, NE))
    cs_in = {(0, "c"): dram_in("cos_o", (128, OWN)),
             (0, "s"): dram_in("sin_o", (128, OWN)),
             (1, "c"): dram_in("cos_p", (128, OWN)),
             (1, "s"): dram_in("sin_p", (128, OWN))}
    masks_in = dram_in("masks", (NQC, 128, 4096), BF16)

    out_hbm = nc.dram_tensor("out", [OWN, D], F32, kind="ExternalOutput").ap()

    y1_d = nc.dram_tensor("y1_d", [OWN, D], F32, kind="Internal").ap()
    y2_d = nc.dram_tensor("y2_d", [OWN, D], F32, kind="Internal").ap()
    q_d = nc.dram_tensor("q_d", [NH * HD, OWN], BF16, kind="Internal").ap()
    k_d = nc.dram_tensor("k_d", [KVD, S], BF16, kind="Internal").ap()
    v_d = nc.dram_tensor("v_d", [S, KVD], BF16, kind="Internal").ap()
    ao_d = nc.dram_tensor("ao_d", [NH * HD, OWN], BF16, kind="Internal").ap()

    identf = nc.inline_tensor(np.eye(128, dtype=np.float32), "identf")
    identb = nc.inline_tensor(np.eye(128, dtype=ml_dtypes.bfloat16), "identb")
    ropePT = nc.inline_tensor(_rope_perm_T(), "ropePT")
    e8 = np.zeros((8, 128), np.float32)
    for h in range(8):
        e8[h, h * 16:(h + 1) * 16] = 1.0
    e8c = nc.inline_tensor(e8, "e8c")
    ones_col_b = nc.inline_tensor(np.ones((128, 1), dtype=ml_dtypes.bfloat16), "ones_col_b")
    ones_col_f = nc.inline_tensor(np.ones((128, 1), dtype=np.float32), "ones_col_f")
    ones_row_f = nc.inline_tensor(np.ones((1, 128), dtype=np.float32), "ones_row_f")

    sb_idn = nc.alloc_sbuf_tensor("sb_idn", [128, 128], F32)
    sb_idnb = nc.alloc_sbuf_tensor("sb_idnb", [128, 128], BF16)
    sb_pT = nc.alloc_sbuf_tensor("sb_pT", [128, 128], BF16)
    sb_e8 = nc.alloc_sbuf_tensor("sb_e8", [8, 128], F32)
    sb_o1b = nc.alloc_sbuf_tensor("sb_o1b", [128, 1], BF16)
    sb_o1f = nc.alloc_sbuf_tensor("sb_o1f", [128, 1], F32)
    sb_orf = nc.alloc_sbuf_tensor("sb_orf", [1, 128], F32)
    sb_acat = nc.alloc_sbuf_tensor("sb_acat", [128, (D // 128) * 128], BF16)
    sb_bcat = nc.alloc_sbuf_tensor("sb_bcat", [128, D], BF16)
    sb_wrt = nc.alloc_sbuf_tensor("sb_wrt", [128, (D // 128) * 8], BF16)
    sb_brr = nc.alloc_sbuf_tensor("sb_brr", [1, NE], F32)
    sb_brb = nc.alloc_sbuf_tensor("sb_brb", [128, NE], F32)
    sb_qn = nc.alloc_sbuf_tensor("sb_qn", [1, 128], F32)
    sb_bias = nc.alloc_sbuf_tensor("sb_bias", [128, NH], F32)

    ov_base = (nc.sbuf_base + 31) & ~31

    psA = nc.alloc_psum_tensor("psA", [128, 1024], F32)
    psT = nc.alloc_psum_tensor("psT", [128, 2048], BF16)
    psB = nc.alloc_psum_tensor("psB", [128, 1024], F32)
    pav = nc.alloc_psum_tensor("pav", [128, 512], F32)
    plT = nc.alloc_psum_tensor("plT", [8, 512], F32)

    sch = Sched(nc)
    SL = {k: sch.dma_slot(k) for k in (
        "const", "ffnw", "x0", "x1", "lnT0", "lnT1", "w0", "w1", "cs",
        "st_y1", "st_q0", "st_q1", "st_k0", "st_k1", "st_v0", "st_v1",
        "kvld", "attnld", "qld0", "qld1", "mld",
        "st_ao0", "st_ao1", "wo0", "wo1", "y1s0", "y1s1",
        "st_y20", "st_y21", "y2l0", "y2l1", "st_out")}

    overlay_bufs = []

    class Overlay:
        def __init__(self):
            self.off = ov_base
            self.carry = sch.barrier_tokens(overlay_bufs)
            overlay_bufs.clear()

        def t(self, name, shape, dtype):
            itm = {F32: 4, BF16: 2}[dtype]
            t = nc.alloc_sbuf_tensor_at(name, list(shape), dtype, offset=self.off)
            self.off += (shape[1] * itm + 31) & ~31
            assert self.off <= SBUF_TOP, f"overlay overflow at {name}: {self.off}"
            b = Buf(name, self.carry)
            overlay_bufs.append(b)
            return t, b

    ps_bufs = []

    def psum_phase(names):
        carry = sch.barrier_tokens(ps_bufs)
        d = {n: Buf(n, carry) for n in names}
        ps_bufs.clear()
        ps_bufs.extend(d.values())
        return d

    CB = {}
    for name, sb, src in (("idn", sb_idn, identf), ("idnb", sb_idnb, identb),
                          ("pT", sb_pT, ropePT),
                          ("e8", sb_e8, e8c), ("o1b", sb_o1b, ones_col_b),
                          ("o1f", sb_o1f, ones_col_f), ("orf", sb_orf, ones_row_f)):
        CB[name] = Buf(name)
        sch.op("sync", lambda e, sb=sb, src=src: e.dma_start(out=sb[:], in_=src.ap()),
               writes=[CB[name]], slot=SL["const"])
    CB["brr"] = Buf("brr")
    sch.op("sync", lambda e: e.dma_start(out=sb_brr[:], in_=br_in),
           writes=[CB["brr"]], slot=SL["const"])
    _const_tok = ("const", sch.cnt["const"])
    for _n in ("idn", "idnb", "pT", "e8", "o1b", "o1f", "orf", "brr"):
        CB[_n].wtok = _const_tok
    for name, sb, src in (("acat", sb_acat, acat), ("bcat", sb_bcat, bcat),
                          ("wrt", sb_wrt, wrt)):
        CB[name] = Buf(name)
        sch.op("gpsimd", lambda e, sb=sb, src=src: e.dma_start(out=sb[:], in_=src),
               writes=[CB[name]], slot=SL["ffnw"])
    _ffnw_tok = ("ffnw", sch.cnt["ffnw"])
    for _n in ("acat", "bcat", "wrt"):
        CB[_n].wtok = _ffnw_tok

    ps = psum_phase(["pA0", "pA1", "pB0", "pB1", "pT0", "pT1", "pav", "plT"])
    CB["brb"] = Buf("brb")
    sch.op("tensor", lambda e: e.matmul(out=psA[:, 0:NE], lhsT=sb_orf[:], rhs=sb_brr[:],
                                        start=True, stop=True),
           reads=[CB["orf"], CB["brr"]], writes=[ps["pA0"]])
    sch.op("scalar", lambda e: e.copy(out=sb_brb[:], in_=psA[:, 0:NE]),
           reads=[ps["pA0"]], writes=[CB["brb"]])
    B_qn = Buf("qn")
    sch.op("vector", lambda e: e.memset(sb_qn[:], 0.0), writes=[B_qn])
    B_bias = Buf("bias")

    DB = {n: Buf(n) for n in ("xo", "xp", "y1", "y2", "q", "k", "v", "ao", "out",
                              "wq", "wk", "wv", "wo", "cs", "masks")}

    ov = Overlay()
    y1fm, b_y1fm = ov.t("y1fm", [128, 32 * HALF], BF16)
    xT, b_xT = [], []
    for i in range(2):
        t, b = ov.t(f"xT{i}", [128, D], F32)
        xT.append(t); b_xT.append(b)
    y1t, b_y1t = ov.t("y1t", [128, D], F32)
    lnt, b_lnt = ov.t("lnt", [128, D], BF16)
    xfm, b_xfm = ov.t("xfm", [128, D], BF16)
    wbuf, b_wbuf = [], []
    for i in range(2):
        t, b = ov.t(f"wbuf{i}", [128, 32 * W_OC], BF16)
        wbuf.append(t); b_wbuf.append(b)
    cs_cos, b_csc = ov.t("cs_cos", [128, HALF], F32)
    cs_sin, b_css = ov.t("cs_sin", [128, HALF], F32)
    stg, b_stg = [], []
    for i in range(4):
        t, b = ov.t(f"stg{i}", [128, 512], BF16)
        stg.append(t); b_stg.append(b)
    t1f, b_t1 = [], []
    t2f, b_t2 = [], []
    for i in range(2):
        t, b = ov.t(f"t1f{i}", [128, 512], F32)
        t1f.append(t); b_t1.append(b)
        t, b = ov.t(f"t2f{i}", [128, 512], F32)
        t2f.append(t); b_t2.append(b)
    qsq, b_qsq = ov.t("qsq", [128, 512], F32)

    class FFNCtx:
        def __init__(self, ovl, tag):
            self.stats, self.b_stats = ovl.t(f"stats{tag}", [128, 48], F32)
            self.lnsc, self.b_lnsc = ovl.t(f"lnsc{tag}", [128, 8], F32)
            self.ffns, self.b_ffns = ovl.t(f"ffns{tag}", [128, 32], F32)
            self.scrT, self.b_scrT = ovl.t(f"scrT{tag}", [8, 128], F32)
            self.scb, self.b_scb = ovl.t(f"scb{tag}", [128, 128], F32)
            self.lows, self.b_lows = ovl.t(f"lows{tag}", [128, 128], BF16)
            self.lowf, self.b_lowf = ovl.t(f"lowf{tag}", [128, 128], F32)
            self.ffo, self.b_ffo = ovl.t(f"ffo{tag}", [128, 512], F32)

    fc = FFNCtx(ov, "A")

    def emit_ln(fx, src_buf, src_t, dst_buf, dst_t):
        def f_stats(e):
            for g in range(8):
                ii = e.bn_stats(out=fx.stats[:, g * 6:(g + 1) * 6],
                                in_=src_t[:, g * 512:(g + 1) * 512])
            return ii
        sch.op("vector", f_stats, reads=[src_buf], writes=[fx.b_stats])
        sch.op("vector", lambda e: e.bn_aggr(out=fx.lnsc[:, 0:2], in_=fx.stats[:]),
               reads=[fx.b_stats], writes=[fx.b_lnsc])
        sch.op("vector", lambda e: e.tensor_scalar(out=fx.lnsc[:, 2:3], in0=fx.lnsc[:, 1:2],
                                                   scalar1=LN_EPS, scalar2=None, op0=ALU.add),
               reads=[fx.b_lnsc], writes=[fx.b_lnsc])
        sch.op("scalar", lambda e: e.activation(out=fx.lnsc[:, 3:4], in_=fx.lnsc[:, 2:3],
                                                func=AF.Sqrt),
               reads=[fx.b_lnsc], writes=[fx.b_lnsc])
        sch.op("vector", lambda e: e.reciprocal(out=fx.lnsc[:, 3:4], in_=fx.lnsc[:, 3:4]),
               reads=[fx.b_lnsc], writes=[fx.b_lnsc])
        sch.op("vector", lambda e: e.tensor_scalar(out=dst_t[:], in0=src_t[:],
                                                   scalar1=fx.lnsc[:, 0:1],
                                                   scalar2=fx.lnsc[:, 3:4],
                                                   op0=ALU.subtract, op1=ALU.mult),
               reads=[src_buf, fx.b_lnsc], writes=[dst_buf])

    def emit_T(src_buf, src_t, dst_buf, dst_t, colf, slotname):
        for ft in range(32):
            par = ft % 2
            pb = ps[f"pT{par}"]
            pt = psT[:, par * 1024: par * 1024 + 128]
            sch.op("tensor", lambda e, ft=ft, pt=pt: e.transpose(
                out=pt, in_=src_t[:, ft * 128:(ft + 1) * 128], identity=sb_idnb[:]),
                reads=[src_buf, CB["idnb"]], writes=[pb])
            sch.op("scalar", lambda e, ft=ft, pt=pt: e.copy(
                out=dst_t[:, colf(ft):colf(ft) + 128], in_=pt),
                reads=[pb], writes=[dst_buf])

    def emit_ffn(fx, xbuf, x_t, outbuf, out_t, xf_t, b_xf):
        def f_rout(e):
            for ft in range(32):
                ii = e.matmul(out=psB[:, 512:512 + NE],
                              lhsT=xf_t[:, ft * 128:(ft + 1) * 128],
                              rhs=sb_wrt[:, ft * 8:(ft + 1) * 8],
                              start=(ft == 0), stop=(ft == 31))
            return ii
        sch.op("tensor", f_rout, reads=[b_xf, CB["wrt"]], writes=[ps["pB1"]])
        sch.op("scalar", lambda e: e.copy(out=fx.ffns[:, 24:32], in_=psB[:, 512:512 + NE]),
               reads=[ps["pB1"]], writes=[fx.b_ffns])
        sch.op("vector", lambda e: e.tensor_tensor(out=fx.ffns[:, 0:NE], in0=fx.ffns[:, 24:32],
                                                   in1=sb_brb[:], op=ALU.add),
               reads=[fx.b_ffns, CB["brb"]], writes=[fx.b_ffns])
        sch.op("vector", lambda e: e.reduce_max(out=fx.ffns[:, 16:17], in_=fx.ffns[:, 0:NE],
                                                axis=AX.X, negate=True),
               reads=[fx.b_ffns], writes=[fx.b_ffns])
        sch.op("scalar", lambda e: e.activation(out=fx.ffns[:, 8:16], in_=fx.ffns[:, 0:NE],
                                                func=AF.Exp, bias=fx.ffns[:, 16:17],
                                                accum_out=fx.ffns[:, 17:18]),
               reads=[fx.b_ffns], writes=[fx.b_ffns])
        sch.op("vector", lambda e: e.reciprocal(out=fx.ffns[:, 18:19], in_=fx.ffns[:, 17:18]),
               reads=[fx.b_ffns], writes=[fx.b_ffns])
        sch.op("vector", lambda e: e.tensor_scalar(out=fx.ffns[:, 0:NE], in0=fx.ffns[:, 8:16],
                                                   scalar1=fx.ffns[:, 18:19], scalar2=8.0,
                                                   op0=ALU.mult, op1=ALU.mult),
               reads=[fx.b_ffns], writes=[fx.b_ffns])
        sch.op("tensor", lambda e: e.transpose(out=plT[0:8, 0:128], in_=fx.ffns[:, 0:NE],
                                               identity=sb_idn[:]),
               reads=[fx.b_ffns, CB["idn"]], writes=[ps["plT"]])
        sch.op("scalar", lambda e: e.copy(out=fx.scrT[:], in_=plT[0:8, 0:128]),
               reads=[ps["plT"]], writes=[fx.b_scrT])
        sch.op("tensor", lambda e: e.matmul(out=psB[:, 640:768], lhsT=sb_e8[:], rhs=fx.scrT[:],
                                            start=True, stop=True),
               reads=[CB["e8"], fx.b_scrT], writes=[ps["pB1"]])
        sch.op("scalar", lambda e: e.copy(out=fx.scb[:], in_=psB[:, 640:768]),
               reads=[ps["pB1"]], writes=[fx.b_scb])
        def f_low(e):
            for ft in range(32):
                ii = e.matmul(out=psB[:, 768:896],
                              lhsT=sb_acat[:, ft * 128:(ft + 1) * 128],
                              rhs=xf_t[:, ft * 128:(ft + 1) * 128],
                              start=(ft == 0), stop=(ft == 31))
            return ii
        sch.op("tensor", f_low, reads=[b_xf, CB["acat"]], writes=[ps["pB1"]])
        sch.op("scalar", lambda e: e.copy(out=fx.lowf[:], in_=psB[:, 768:896]),
               reads=[ps["pB1"]], writes=[fx.b_lowf])
        sch.op("vector", lambda e: e.tensor_tensor(out=fx.lows[:], in0=fx.lowf[:],
                                                   in1=fx.scb[:], op=ALU.mult),
               reads=[fx.b_lowf, fx.b_scb], writes=[fx.b_lows])
        for oc in range(8):
            pbuf = ps["pB0"]
            pt = psB[:, 0:512]
            sch.op("tensor", lambda e, oc=oc, pt=pt: e.matmul(
                out=pt, lhsT=fx.lows[:], rhs=sb_bcat[:, oc * 512:(oc + 1) * 512],
                start=True, stop=True),
                reads=[fx.b_lows, CB["bcat"]], writes=[pbuf])
            sch.op("scalar", lambda e, pt=pt: e.copy(out=fx.ffo[:], in_=pt),
                   reads=[pbuf], writes=[fx.b_ffo])
            sch.op("vector", lambda e, oc=oc: e.tensor_tensor(
                out=out_t[:, oc * 512:(oc + 1) * 512], in0=fx.ffo[:],
                in1=x_t[:, oc * 512:(oc + 1) * 512], op=ALU.add),
                reads=[fx.b_ffo, xbuf], writes=[outbuf])

    def phase_A(x_dram, b_xdram, half, own):
        for ti in range(NT_HALF):
            xs = ti % 2
            row0 = half * HALF + ti * 128
            sch.op("sync", lambda e, row0=row0, xs=xs: e.dma_start(
                out=xT[xs][:], in_=x_dram[row0:row0 + 128, :]),
                reads=[b_xdram], writes=[b_xT[xs]], slot=SL[f"x{xs}"])
            emit_ln(fc, b_xT[xs], xT[xs], b_lnt, lnt)
            emit_T(b_lnt, lnt, b_xfm, xfm, lambda ft: ft * 128, "lnT0")
            emit_ffn(fc, b_xT[xs], xT[xs], b_y1t, y1t, xfm, b_xfm)
            if own:
                sch.op("scalar", lambda e, row0=row0: e.dma_start(
                    out=y1_d[row0:row0 + 128, :], in_=y1t[:]),
                    reads=[b_y1t], writes=[DB["y1"]], slot=SL["st_y1"])
            emit_ln(fc, b_y1t, y1t, b_lnt, lnt)
            emit_T(b_lnt, lnt, b_y1fm, y1fm, lambda ft, ti=ti: ft * HALF + ti * 128, "lnT1")

    def rope_store(pname, o_global, slot_col, tc, pbuf, pt, half, sslot):
        st = b_stg[sslot]
        sch.op("scalar", lambda e: e.copy(out=stg[sslot][:], in_=pt),
               reads=[pbuf], writes=[st])
        if B_ROPE:
            csl = cs_cos[:, tc * 512:(tc + 1) * 512]
            ssl = cs_sin[:, tc * 512:(tc + 1) * 512]
            rsl = 2 + sslot
            sch.op("tensor", lambda e: e.matmul(out=pav[:], lhsT=sb_pT[:], rhs=stg[sslot][:],
                                                start=True, stop=True),
                   reads=[CB["pT"], st], writes=[ps["pav"]])
            sch.op("scalar", lambda e, rsl=rsl: e.copy(out=stg[rsl][:], in_=pav[:]),
                   reads=[ps["pav"]], writes=[b_stg[rsl]])
            sch.op("vector", lambda e, c=csl: e.tensor_tensor(out=t1f[sslot][:],
                                                              in0=stg[sslot][:],
                                                              in1=c, op=ALU.mult),
                   reads=[st, b_csc], writes=[b_t1[sslot]])
            sch.op("vector", lambda e, s=ssl, rsl=rsl: e.tensor_tensor(out=t2f[sslot][:],
                                                                       in0=stg[rsl][:],
                                                                       in1=s, op=ALU.mult),
                   reads=[b_stg[rsl], b_css], writes=[b_t2[sslot]])
            sch.op("vector", lambda e: e.tensor_tensor(out=stg[sslot][:], in0=t1f[sslot][:],
                                                       in1=t2f[sslot][:], op=ALU.add),
                   reads=[b_t1[sslot], b_t2[sslot]], writes=[st])
        if pname == "q":
            col0 = half * HALF + tc * 512
            sch.op("scalar", lambda e, o=o_global, c=col0: e.dma_start(
                out=q_d[o:o + 128, c:c + 512], in_=stg[sslot][:]),
                reads=[st], writes=[DB["q"]], slot=SL[f"st_q{sslot}"])
            h = o_global // 128
            if not B_QNORM:
                return
            sch.op("scalar", lambda e: e.activation(out=qsq[:], in_=stg[sslot][:],
                                                    func=AF.Square),
                   reads=[st], writes=[b_qsq])
            sch.op("tensor", lambda e: e.matmul(out=plT[0:1, :], lhsT=sb_o1f[:], rhs=qsq[:],
                                                start=True, stop=True),
                   reads=[CB["o1f"], b_qsq], writes=[ps["plT"]])
            sch.op("scalar", lambda e: e.copy(out=qsq[0:1, :], in_=plT[0:1, :]),
                   reads=[ps["plT"]], writes=[b_qsq])
            sch.op("vector", lambda e: e.reduce_max(out=sb_qn[0:1, 40:41], in_=qsq[0:1, :],
                                                    axis=AX.X),
                   reads=[b_qsq], writes=[B_qn])
            sch.op("vector", lambda e, h=h: e.tensor_tensor(out=sb_qn[0:1, h:h + 1],
                                                            in0=sb_qn[0:1, h:h + 1],
                                                            in1=sb_qn[0:1, 40:41], op=ALU.max),
                   reads=[B_qn], writes=[B_qn])
        else:
            c0 = slot_col * 512
            sch.op("scalar", lambda e, o=o_global, c=c0: e.dma_start(
                out=k_d[o:o + 128, c:c + 512], in_=stg[sslot][:]),
                reads=[st], writes=[DB["k"]], slot=SL[f"st_k{sslot}"])

    def phase_B(half, own):
        slot_base = (0 if own else 4) + 2 * half
        lo = half * HALF
        key = 0 if own else 1
        sch.op("sync", lambda e: e.dma_start(out=cs_cos[:], in_=cs_in[(key, "c")][:, lo:lo + HALF]),
               writes=[b_csc], slot=SL["cs"])
        sch.op("sync", lambda e: e.dma_start(out=cs_sin[:], in_=cs_in[(key, "s")][:, lo:lo + HALF]),
               writes=[b_css], slot=SL["cs"])
        b_csc.wtok = b_css.wtok = ("cs", sch.cnt["cs"])

        projs = ([("q", wq, DB["wq"], D // W_OC)] if own and "q" in B_PROJS else [])
        if "k" in B_PROJS:
            projs.append(("k", wk, DB["wk"], KVD // W_OC))
        if "v" in B_PROJS:
            projs.append(("v", wv, DB["wv"], KVD // W_OC))
        it = 0
        for pname, wsrc, wdb, nchunk in projs:
            for j in range(nchunk):
                ws = it % 2
                it += 1
                sch.op("gpsimd", lambda e, j=j, ws=ws, wsrc=wsrc: e.dma_start(
                    out=wbuf[ws][:], in_=wsrc[j]),
                    reads=[wdb], writes=[b_wbuf[ws]], slot=SL[f"w{ws}"])
                if pname == "v":
                    for tt in range(NT_HALF):
                        sslot = tt % 2
                        pbuf = ps["pA0"] if sslot == 0 else ps["pA1"]
                        pt = psA[:, 0:512] if sslot == 0 else psA[:, 512:1024]
                        def f_vm(e, tt=tt, pt=pt, ws=ws):
                            for ft in range(32):
                                ii = e.matmul(
                                    out=pt[:, 0:W_OC],
                                    lhsT=y1fm[:, ft * HALF + tt * 128: ft * HALF + (tt + 1) * 128],
                                    rhs=wbuf[ws][:, ft * W_OC:(ft + 1) * W_OC],
                                    start=(ft == 0), stop=(ft == 31))
                            return ii
                        sch.op("tensor", f_vm, reads=[b_wbuf[ws], b_y1fm], writes=[pbuf])
                        st = b_stg[2 + sslot]
                        sch.op("scalar", lambda e, pt=pt, sslot=sslot: e.copy(
                            out=stg[2 + sslot][:, 0:W_OC], in_=pt[:, 0:W_OC]),
                            reads=[pbuf], writes=[st])
                        grow = (slot_base + tt // 4) * 512 + (tt % 4) * 128
                        sch.op("scalar", lambda e, g=grow, j=j, sslot=sslot: e.dma_start(
                            out=v_d[g:g + 128, j * W_OC:(j + 1) * W_OC],
                            in_=stg[2 + sslot][:, 0:W_OC]),
                            reads=[st], writes=[DB["v"]], slot=SL[f"st_v{sslot}"])
                else:
                    for o1 in range(W_OC // 128):
                        o_global = j * W_OC + o1 * 128
                        for tc in range(2):
                            sslot = (o1 * 2 + tc) % 2
                            pbuf = ps["pA0"] if sslot == 0 else ps["pA1"]
                            pt = psA[:, 0:512] if sslot == 0 else psA[:, 512:1024]
                            def f_mm(e, o1=o1, tc=tc, pt=pt, ws=ws):
                                for ft in range(32):
                                    ii = e.matmul(
                                        out=pt,
                                        lhsT=wbuf[ws][:, ft * W_OC + o1 * 128: ft * W_OC + (o1 + 1) * 128],
                                        rhs=y1fm[:, ft * HALF + tc * 512: ft * HALF + (tc + 1) * 512],
                                        start=(ft == 0), stop=(ft == 31))
                                return ii
                            sch.op("tensor", f_mm, reads=[b_wbuf[ws], b_y1fm], writes=[pbuf])
                            rope_store(pname, o_global, slot_base + tc, tc, pbuf, pt, half, sslot)

    last_toks = []
    for half in range(2):
        phase_A(x_own, DB["xo"], half, own=True)
        if "B" in phases:
            phase_B(half, own=True)
    if "B" in phases:
        for half in range(2):
            phase_A(x_part, DB["xp"], half, own=False)
            phase_B(half, own=False)
    if "C" not in phases:
        for tt in range(OWN // 128):
            tk = sch.op("sync", lambda e, tt=tt: e.dma_start(
                out=out_hbm[tt * 128:(tt + 1) * 128, :],
                in_=y1_d[tt * 128:(tt + 1) * 128, :]),
                reads=[DB["y1"]], writes=[DB["out"]], slot=SL["st_out"])
            last_toks.append(tk)
        if "B" in phases:
            last_toks.append(DB["q"].wtok)
            last_toks.append(DB["k"].wtok)
            last_toks.append(DB["v"].wtok)
        sch.finish(last_toks)
        return _finish_block(nc, sch)

    ovc = Overlay()
    K_sb, b_K = ovc.t("K_sb", [128, NKV * S], BF16)
    V_sb, b_V = ovc.t("V_sb", [128, (S // 128) * KVD], BF16)
    qtl, b_qtl = [], []
    for i in range(2):
        t, b = ovc.t(f"qtl{i}", [128, 512], BF16)
        qtl.append(t); b_qtl.append(b)
    exb, b_exb = [], []
    for i in range(2):
        t, b = ovc.t(f"exb{i}", [128, 1536], BF16)
        exb.append(t); b_exb.append(b)
    maskb, b_maskb = ovc.t("maskb", [128, 4096], BF16)
    rlb, b_rlb = ovc.t("rlb", [128, 512], F32)
    rlr, b_rlr = ovc.t("rlr", [1, 512], F32)
    aost, b_aost = [], []
    for i in range(2):
        t, b = ovc.t(f"aost{i}", [128, 512], BF16)
        aost.append(t); b_aost.append(b)
    ksq, b_ksq = ovc.t("ksq", [128, 4096], F32)
    avf, b_avf = ovc.t("avf", [128, 512], F32)

    psc = psum_phase(["pA0", "pA1", "pB0", "pB1", "pT0", "pT1", "pav", "plT"])

    def f_kld(e, sem):
        for kv in range(NKV):
            e.dma_start(out=K_sb[:, kv * S:(kv + 1) * S],
                        in_=k_d[kv * 128:(kv + 1) * 128, :].rearrange("d pos -> d pos")
                        ).then_inc(sem, 16)
    sch.op("sync", f_kld, reads=[DB["k"]], writes=[b_K], slot=SL["kvld"],
           n_dma=NKV, self_inc=True)

    def f_vld(e, sem):
        for g in range(4):
            e.dma_start(
                out=V_sb[:, g * 8 * KVD:(g + 1) * 8 * KVD].rearrange(
                    "p (pt c) -> p pt c", pt=8),
                in_=v_d[g * 1024:(g + 1) * 1024, :].rearrange(
                    "(pt p) c -> p pt c", p=128)).then_inc(sem, 16)
    sch.op("sync", f_vld, reads=[DB["v"]], writes=[b_V], slot=SL["kvld"],
           n_dma=4, self_inc=True)
    b_K.wtok = b_V.wtok = ("kvld", sch.cnt["kvld"])

    for kv in range(NKV):
        sch.op("scalar", lambda e, kv=kv: e.activation(
            out=ksq[:], in_=K_sb[:, kv * S:(kv + 1) * S], func=AF.Square),
            reads=[b_K], writes=[b_ksq])
        for c8 in range(8):
            sch.op("tensor", lambda e, c8=c8: e.matmul(
                out=plT[0:1, :], lhsT=sb_o1f[:], rhs=ksq[:, c8 * 512:(c8 + 1) * 512],
                start=True, stop=True),
                reads=[CB["o1f"], b_ksq], writes=[psc["plT"]])
            sch.op("scalar", lambda e: e.copy(out=rlr[:], in_=plT[0:1, :]),
                   reads=[psc["plT"]], writes=[b_rlr])
            sch.op("vector", lambda e: e.reduce_max(out=sb_qn[0:1, 40:41], in_=rlr[:],
                                                    axis=AX.X),
                   reads=[b_rlr], writes=[B_qn])
            sch.op("vector", lambda e, kv=kv: e.tensor_tensor(
                out=sb_qn[0:1, 32 + kv:33 + kv], in0=sb_qn[0:1, 32 + kv:33 + kv],
                in1=sb_qn[0:1, 40:41], op=ALU.max),
                reads=[B_qn], writes=[B_qn])

    for h in range(NH):
        kv = h // 4
        sch.op("vector", lambda e, h=h, kv=kv: e.tensor_tensor(
            out=sb_qn[0:1, 41:42], in0=sb_qn[0:1, h:h + 1],
            in1=sb_qn[0:1, 32 + kv:33 + kv], op=ALU.mult),
            reads=[B_qn], writes=[B_qn])
        sch.op("scalar", lambda e: e.activation(out=sb_qn[0:1, 42:43], in_=sb_qn[0:1, 41:42],
                                                func=AF.Sqrt),
               reads=[B_qn], writes=[B_qn])
        sch.op("vector", lambda e, h=h: e.tensor_scalar(
            out=sb_qn[0:1, 64 + h:65 + h], in0=sb_qn[0:1, 42:43],
            scalar1=-RSCALE, scalar2=None, op0=ALU.mult),
            reads=[B_qn], writes=[B_qn])
    sch.op("tensor", lambda e: e.matmul(out=pav[:, 0:NH], lhsT=sb_orf[:],
                                        rhs=sb_qn[0:1, 64:64 + NH], start=True, stop=True),
           reads=[CB["orf"], B_qn], writes=[psc["pav"]])
    sch.op("scalar", lambda e: e.copy(out=sb_bias[:], in_=pav[:, 0:NH]),
           reads=[psc["pav"]], writes=[B_bias])

    for p in range(NQC):
        sch.op("sync", lambda e, p=p: e.dma_start(out=maskb[:], in_=masks_in[p]),
               reads=[DB["masks"]], writes=[b_maskb], slot=SL["mld"])
        tiles = [(sl, j0) for sl in SLOTS_P[p] for j0 in range(4)]
        nk = len(tiles)
        for h in range(NH):
            kv = h // 4
            qs = h % 2
            sch.op("sync", lambda e, p=p, h=h, qs=qs: e.dma_start(
                out=qtl[qs][:], in_=q_d[h * 128:(h + 1) * 128, p * 512:(p + 1) * 512]),
                reads=[DB["q"]], writes=[b_qtl[qs]], slot=SL[f"qld{qs}"])

            groups = [tiles[i:i + 2] for i in range(0, nk, 2)]

            def emit_scores(gi):
                grp = groups[gi]
                pb = [psc["pA0"], psc["pA1"]] if gi % 2 == 0 else [psc["pB0"], psc["pB1"]]
                pt = psA if gi % 2 == 0 else psB
                def f(e, grp=grp, pt=pt, qs=qs, kv=kv):
                    for i, (sl, j0) in enumerate(grp):
                        pos = sl * 4 + j0
                        ii = e.matmul(out=pt[:, i * 512:(i + 1) * 512],
                                      lhsT=K_sb[:, kv * S + pos * 128: kv * S + (pos + 1) * 128],
                                      rhs=qtl[qs][:], start=True, stop=True,
                                      skip_group_check=True)
                    return ii
                sch.op("tensor", f, reads=[b_K, b_qtl[qs]], writes=pb)

            def emit_rest(gi, first_tile_idx):
                grp = groups[gi]
                pb = [psc["pA0"], psc["pA1"]] if gi % 2 == 0 else [psc["pB0"], psc["pB1"]]
                pt = psA if gi % 2 == 0 else psB
                eb = b_exb[gi % 2]
                et = exb[gi % 2]
                n = len(grp)
                sch.op("scalar", lambda e, pt=pt, et=et, n=n, h=h: e.activation(
                    out=et[:, 0:n * 512], in_=pt[:, 0:n * 512], func=AF.Exp,
                    bias=sb_bias[:, h:h + 1], scale=RSCALE),
                    reads=pb + [B_bias], writes=[eb])
                for i, (sl, j0) in enumerate(grp):
                    if sl == p or sl == p + 4:
                        mi = 0 if sl == p else 1
                        mc = mi * 2048 + j0 * 512
                        sch.op("vector", lambda e, i=i, mc=mc, et=et: e.tensor_tensor(
                            out=et[:, i * 512:(i + 1) * 512], in0=et[:, i * 512:(i + 1) * 512],
                            in1=maskb[:, mc:mc + 512], op=ALU.mult),
                            reads=[eb, b_maskb], writes=[eb])
                for i, (sl, j0) in enumerate(grp):
                    ti = first_tile_idx + i
                    pos = sl * 4 + j0
                    sch.op("tensor", lambda e, i=i, pos=pos, et=et, kv=kv, ti=ti: e.matmul(
                        out=pav[:], lhsT=V_sb[:, pos * KVD + kv * 128: pos * KVD + (kv + 1) * 128],
                        rhs=et[:, i * 512:(i + 1) * 512],
                        start=(ti == 0), stop=(ti == nk - 1), skip_group_check=True),
                        reads=[eb, b_V], writes=[psc["pav"]])
                    sch.op("tensor", lambda e, i=i, et=et, ti=ti: e.matmul(
                        out=plT[0:1, :], lhsT=sb_o1b[:], rhs=et[:, i * 512:(i + 1) * 512],
                        start=(ti == 0), stop=(ti == nk - 1), skip_group_check=True),
                        reads=[eb, CB["o1b"]], writes=[psc["plT"]])

            emit_scores(0)
            ti0 = 0
            for gi in range(1, len(groups)):
                emit_scores(gi)
                emit_rest(gi - 1, ti0)
                ti0 += len(groups[gi - 1])
            emit_rest(len(groups) - 1, ti0)

            sch.op("scalar", lambda e: e.copy(out=rlr[:], in_=plT[0:1, :]),
                   reads=[psc["plT"]], writes=[b_rlr])
            sch.op("vector", lambda e: e.reciprocal(out=rlr[:], in_=rlr[:]),
                   reads=[b_rlr], writes=[b_rlr])
            sch.op("tensor", lambda e: e.matmul(out=psA[:, 0:512], lhsT=sb_orf[:], rhs=rlr[:],
                                                start=True, stop=True, skip_group_check=True),
                   reads=[CB["orf"], b_rlr], writes=[psc["pA0"]])
            sch.op("scalar", lambda e: e.copy(out=rlb[:], in_=psA[:, 0:512]),
                   reads=[psc["pA0"]], writes=[b_rlb])
            ao_s = h % 2
            sch.op("scalar", lambda e: e.copy(out=avf[:], in_=pav[:]),
                   reads=[psc["pav"]], writes=[b_avf])
            sch.op("vector", lambda e, ao_s=ao_s: e.tensor_tensor(
                out=aost[ao_s][:], in0=avf[:], in1=rlb[:], op=ALU.mult),
                reads=[b_avf, b_rlb], writes=[b_aost[ao_s]])
            sch.op("scalar", lambda e, p=p, h=h, ao_s=ao_s: e.dma_start(
                out=ao_d[h * 128:(h + 1) * 128, p * 512:(p + 1) * 512], in_=aost[ao_s][:]),
                reads=[b_aost[ao_s]], writes=[DB["ao"]], slot=SL[f"st_ao{ao_s}"])

    if "D" not in phases:
        for tt in range(OWN // 128):
            tk = sch.op("sync", lambda e, tt=tt: e.dma_start(
                out=out_hbm[tt * 128:(tt + 1) * 128, :],
                in_=y1_d[tt * 128:(tt + 1) * 128, :]),
                reads=[DB["y1"]], writes=[DB["out"]], slot=SL["st_out"])
        sch.finish([tk, DB["ao"].wtok])
        return _finish_block(nc, sch)

    ovd = Overlay()
    attn_sb, b_attn = ovd.t("attn_sb", [128, 32 * OWN], BF16)
    wob, b_wob = [], []
    for i in range(2):
        t, b = ovd.t(f"wob{i}", [128, 32 * WO_OC], BF16)
        wob.append(t); b_wob.append(b)
    y1s, b_y1s = [], []
    for i in range(2):
        t, b = ovd.t(f"y1s{i}", [128, WO_OC], F32)
        y1s.append(t); b_y1s.append(b)
    y2s, b_y2s = [], []
    for i in range(2):
        t, b = ovd.t(f"y2s{i}", [128, WO_OC], F32)
        y2s.append(t); b_y2s.append(b)
    y2f, b_y2f = [], []
    for i in range(2):
        t, b = ovd.t(f"y2f{i}", [128, WO_OC], F32)
        y2f.append(t); b_y2f.append(b)

    psd = psum_phase(["pA0", "pA1", "pB0", "pB1", "pT0", "pT1", "pav", "plT"])

    def f_ald(e, sem):
        for g in range(4):
            e.dma_start(
                out=attn_sb[:, g * 8 * OWN:(g + 1) * 8 * OWN].rearrange(
                    "p (hc q) -> p hc q", hc=8),
                in_=ao_d[g * 1024:(g + 1) * 1024, :].rearrange(
                    "(hc p) q -> p hc q", p=128)).then_inc(sem, 16)
    sch.op("sync", f_ald, reads=[DB["ao"]], writes=[b_attn], slot=SL["attnld"],
           n_dma=4, self_inc=True)

    for oc in range(D // WO_OC):
        ws = oc % 2
        sch.op("gpsimd", lambda e, oc=oc, ws=ws: e.dma_start(out=wob[ws][:], in_=wo[oc]),
               reads=[DB["wo"]], writes=[b_wob[ws]], slot=SL[f"wo{ws}"])
        for tt in range(OWN // 128):
            sl2 = tt % 2
            pbuf = psd["pA0"] if sl2 == 0 else psd["pA1"]
            pt = psA[:, 0:WO_OC] if sl2 == 0 else psA[:, 512:512 + WO_OC]
            def f_wo(e, tt=tt, ws=ws, pt=pt):
                for hc in range(32):
                    ii = e.matmul(out=pt,
                                  lhsT=attn_sb[:, hc * OWN + tt * 128: hc * OWN + (tt + 1) * 128],
                                  rhs=wob[ws][:, hc * WO_OC:(hc + 1) * WO_OC],
                                  start=(hc == 0), stop=(hc == 31))
                return ii
            sch.op("tensor", f_wo, reads=[b_wob[ws], b_attn], writes=[pbuf])
            sch.op("sync", lambda e, tt=tt, oc=oc, sl2=sl2: e.dma_start(
                out=y1s[sl2][:], in_=y1_d[tt * 128:(tt + 1) * 128, oc * WO_OC:(oc + 1) * WO_OC]),
                reads=[DB["y1"]], writes=[b_y1s[sl2]], slot=SL[f"y1s{sl2}"])
            sch.op("scalar", lambda e, sl2=sl2, pt=pt: e.copy(out=y2f[sl2][:], in_=pt),
                   reads=[pbuf], writes=[b_y2f[sl2]])
            sch.op("vector", lambda e, sl2=sl2: e.tensor_tensor(
                out=y2s[sl2][:], in0=y2f[sl2][:], in1=y1s[sl2][:], op=ALU.add),
                reads=[b_y2f[sl2], b_y1s[sl2]], writes=[b_y2s[sl2]])
            sch.op("scalar", lambda e, tt=tt, oc=oc, sl2=sl2: e.dma_start(
                out=y2_d[tt * 128:(tt + 1) * 128, oc * WO_OC:(oc + 1) * WO_OC],
                in_=y2s[sl2][:]),
                reads=[b_y2s[sl2]], writes=[DB["y2"]], slot=SL[f"st_y2{sl2}"])

    if "E" not in phases:
        for tt in range(OWN // 128):
            tk = sch.op("sync", lambda e, tt=tt: e.dma_start(
                out=out_hbm[tt * 128:(tt + 1) * 128, :],
                in_=y2_d[tt * 128:(tt + 1) * 128, :]),
                reads=[DB["y2"]], writes=[DB["out"]], slot=SL["st_out"])
        sch.finish([tk])
        return _finish_block(nc, sch)

    ove = Overlay()
    y2T, b_y2T = [], []
    for i in range(2):
        t, b = ove.t(f"y2T{i}", [128, D], F32)
        y2T.append(t); b_y2T.append(b)
    y3t, b_y3t = ove.t("y3t", [128, D], F32)
    lntE, b_lntE = ove.t("lntE", [128, D], BF16)
    xfmE, b_xfmE = ove.t("xfmE", [128, D], BF16)
    fcE = FFNCtx(ove, "E")

    pse = psum_phase(["pA0", "pA1", "pB0", "pB1", "pT0", "pT1", "pav", "plT"])
    ps.update(pse)

    out_toks = []

    def phase_E():
        for tt in range(OWN // 128):
            ysl = tt % 2
            row0 = tt * 128
            sch.op("sync", lambda e, row0=row0, ysl=ysl: e.dma_start(
                out=y2T[ysl][:], in_=y2_d[row0:row0 + 128, :]),
                reads=[DB["y2"]], writes=[b_y2T[ysl]], slot=SL[f"y2l{ysl}"])
            emit_ln(fcE, b_y2T[ysl], y2T[ysl], b_lntE, lntE)
            emit_T(b_lntE, lntE, b_xfmE, xfmE, lambda ft: ft * 128, "lnT0")
            emit_ffn(fcE, b_y2T[ysl], y2T[ysl], b_y3t, y3t, xfmE, b_xfmE)
            tk = sch.op("sync", lambda e, row0=row0: e.dma_start(
                out=out_hbm[row0:row0 + 128, :], in_=y3t[:]),
                reads=[b_y3t], writes=[DB["out"]], slot=SL["st_out"])
            out_toks.append(tk)

    phase_E()
    sch.finish(out_toks[-1:])
    return _finish_block(nc, sch)


def _finish_block(nc, sch):
    with nc.Block() as block:
        @block.sync
        def _(e):
            for f in sch.ops["sync"]:
                f(e)

        @block.tensor
        def _(e):
            for f in sch.ops["tensor"]:
                f(e)

        @block.vector
        def _(e):
            for f in sch.ops["vector"]:
                f(e)

        @block.scalar
        def _(e):
            for f in sch.ops["scalar"]:
                f(e)

        @block.gpsimd
        def _(e):
            for f in sch.ops["gpsimd"]:
                f(e)

    return nc



def _own_rows(hg):
    qcs = OWN_QCS[hg]
    return np.concatenate([np.arange(g * 512, (g + 1) * 512) for g in qcs])


def make_in_maps(x, Wq, Wk, Wv, Wo, A, Bm, Wr, br):
    wq_t = _tile_lhsT(np.ascontiguousarray(Wq.T), W_OC)
    wk_t = _tile_lhsT(np.ascontiguousarray(Wk.T), W_OC)
    wv_t = _tile_lhsT(np.ascontiguousarray(Wv.T), W_OC)
    wo_t = _tile_lhsT(np.ascontiguousarray(Wo.T), WO_OC)
    acat = np.ascontiguousarray(
        _tile_lhsT(np.ascontiguousarray(A.transpose(2, 0, 1).reshape(D, 128)), 128)[0])
    bcat = np.ascontiguousarray(Bm.transpose(0, 2, 1).reshape(128, D))
    wrt = np.ascontiguousarray(
        _tile_lhsT(np.ascontiguousarray(Wr.T), 8)[0])
    brr = np.ascontiguousarray(br.reshape(1, NE))

    in_maps = []
    for c in range(8):
        b, hg = c // 2, c % 2
        rows_o = _own_rows(hg)
        rows_p = _own_rows(1 - hg)
        cos_o, sin_o = _rope_tables(rows_o)
        cos_p, sin_p = _rope_tables(rows_p)
        in_maps.append({
            "x_own": np.ascontiguousarray(x[b][rows_o]),
            "x_part": np.ascontiguousarray(x[b][rows_p]),
            "wq": wq_t, "wk": wk_t, "wv": wv_t, "wo": wo_t,
            "acat": acat, "bcat": bcat, "wrt": wrt, "br": brr,
            "cos_o": cos_o, "sin_o": sin_o, "cos_p": cos_p, "sin_p": sin_p,
            "masks": _masks_for(hg),
        })
    return in_maps


def kernel(L0_attn_hidden_states, L0_attn_Wq, L0_attn_Wk, L0_attn_Wv, L0_attn_Wo,
           L0_ffn_A, L0_ffn_B, L0_ffn_Wr, L0_ffn_br):
    x = np.asarray(L0_attn_hidden_states, dtype=np.float32)
    in_maps = make_in_maps(x,
                           np.asarray(L0_attn_Wq, np.float32),
                           np.asarray(L0_attn_Wk, np.float32),
                           np.asarray(L0_attn_Wv, np.float32),
                           np.asarray(L0_attn_Wo, np.float32),
                           np.asarray(L0_ffn_A, np.float32),
                           np.asarray(L0_ffn_B, np.float32),
                           np.asarray(L0_ffn_Wr, np.float32),
                           np.asarray(L0_ffn_br, np.float32))
    nc = build_program()
    res = run_bass_kernel_spmd(nc, in_maps, core_ids=list(range(8)))
    out = np.empty((NB, S, D), dtype=np.float32)
    for c in range(8):
        b, hg = c // 2, c % 2
        out[b][_own_rows(hg)] = res.results[c]["out"]
    return out


# version 03
# speedup vs baseline: 1.1140x; 1.1140x over previous
import math

import numpy as np
import ml_dtypes

import concourse.bass as bass
import concourse.mybir as mybir
from concourse.bass_utils import run_bass_kernel_spmd

F32 = mybir.dt.float32
BF16 = mybir.dt.bfloat16
AF = mybir.ActivationFunctionType
ALU = mybir.AluOpType
AX = mybir.AxisListType

D = 4096
S = 4096
NB = 4
NH = 32
NKV = 8
HD = 128
KVD = NKV * HD
NE = 8
OWN = 2048
HALF = 1024
NQC = 4
NT_HALF = 8
RSCALE = float(1.0 / math.sqrt(HD))
LN_EPS = 1e-5

OWN_QCS = ([0, 3, 4, 7], [1, 2, 5, 6])
NK_UNIF = [8, 16, 24, 32]
SLOTS_P = [[0, 4], [0, 1, 4, 5], [0, 1, 2, 4, 5, 6], [0, 1, 2, 3, 4, 5, 6, 7]]

W_OC = 256
WO_OC = 256

B_PROJS = ("q", "k", "v")
B_ROPE = True
B_QNORM = True
A_TRANSPOSE = True
SBUF_TOP = 229344



def _tile_lhsT(wt, oc):
    f, o = wt.shape
    return np.ascontiguousarray(
        wt.reshape(f // 128, 128, o // oc, oc).transpose(2, 1, 0, 3).reshape(o // oc, 128, -1)
    )


def _rope_tables(positions):
    pos = np.asarray(positions, dtype=np.float64)[None, :]
    half = np.arange(64, dtype=np.float64)
    freqs = np.exp(-2.0 * half * (math.log(10000.0) / HD))[:, None]
    ang = pos * freqs
    cos = np.concatenate([np.cos(ang), np.cos(ang)], axis=0).astype(np.float32)
    sin = np.concatenate([np.sin(ang), np.sin(ang)], axis=0).astype(np.float32)
    return np.ascontiguousarray(cos), np.ascontiguousarray(sin)


def _masks_for(hg):
    own = OWN_QCS[hg]
    part = OWN_QCS[1 - hg]
    g_order = own + part
    out = np.zeros((NQC, 128, 4096), dtype=np.float32)
    kp = np.arange(128)[:, None]
    qf = np.arange(512)[None, :]
    for p in range(NQC):
        gq = own[p]
        for mi, slot in enumerate((p, p + 4)):
            g = g_order[slot]
            for j0 in range(4):
                if g < gq:
                    m = np.ones((128, 512), np.float32)
                elif g > gq:
                    m = np.zeros((128, 512), np.float32)
                else:
                    m = ((kp + j0 * 128) <= qf).astype(np.float32)
                out[p, :, mi * 2048 + j0 * 512: mi * 2048 + (j0 + 1) * 512] = m
    return np.ascontiguousarray(out.astype(ml_dtypes.bfloat16))


def _rope_perm_T():
    p = np.zeros((128, 128), np.float32)
    for dd in range(64):
        p[dd, dd + 64] = -1.0
        p[dd + 64, dd] = 1.0
    return np.ascontiguousarray(p.T.astype(ml_dtypes.bfloat16))



class Buf:
    __slots__ = ("name", "wtok", "rtoks")

    def __init__(self, name, carry=()):
        self.name = name
        self.wtok = None
        self.rtoks = list(carry)


class Sched:
    ENGINES = ("tensor", "vector", "scalar", "gpsimd", "sync")

    def __init__(self, nc):
        self.nc = nc
        self.ops = {e: [] for e in self.ENGINES}
        self.sem = {}
        self.cnt = {}
        self.waited = {e: {} for e in self.ENGINES}
        for e in ("tensor", "vector", "scalar", "gpsimd"):
            self.sem[e] = nc.alloc_semaphore(f"sem_{e}")
            self.cnt[e] = 0
        self._sn = 0

    def dma_slot(self, name):
        self.sem[name] = self.nc.alloc_semaphore(f"ds_{name}")
        self.cnt[name] = 0
        return name

    def _wait(self, eng, tok):
        if tok is None:
            return
        if eng == "tensor" and tok[0] == "tensor":
            return
        prev = self.waited[eng].get(tok[0], 0)
        if tok[1] <= prev:
            return
        self.waited[eng][tok[0]] = tok[1]
        sem = self.sem[tok[0]]
        val = tok[1]
        self.ops[eng].append(lambda e, s=sem, v=val: e.wait_ge(s, v))

    def op(self, eng, fn, reads=(), writes=(), slot=None, n_dma=1, self_inc=False):
        for b in reads:
            self._wait(eng, b.wtok)
        for b in writes:
            self._wait(eng, b.wtok)
            for t in b.rtoks:
                self._wait(eng, t)
        if slot is None:
            key, inc = eng, 1
            self.cnt[key] += 1
        else:
            key, inc = slot, 16
            self.cnt[key] += 16 * n_dma
        val = self.cnt[key]
        sem = self.sem[key]
        if self_inc:
            self.ops[eng].append(lambda e, f=fn, s=sem: f(e, s))
        else:
            self.ops[eng].append(lambda e, f=fn, s=sem, i=inc: f(e).then_inc(s, i))
        tok = (key, val)
        for b in reads:
            b.rtoks.append(tok)
        for b in writes:
            b.wtok = tok
            b.rtoks = []
        return tok

    def barrier_tokens(self, bufs):
        toks = []
        for b in bufs:
            if b.wtok is not None:
                toks.append(b.wtok)
            toks.extend(b.rtoks)
        return toks

    def finish(self, toks):
        for t in toks:
            self._wait("sync", t)



def build_program(phases="ABCDE"):
    nc = bass.Bass("TRN2", target_bir_lowering=False)

    def dram_in(name, shape, dtype=F32):
        return nc.dram_tensor(name, list(shape), dtype, kind="ExternalInput").ap()

    x_own = dram_in("x_own", (OWN, D))
    x_part = dram_in("x_part", (OWN, D))
    wq = dram_in("wq", (D // W_OC, 128, (D // 128) * W_OC))
    wk = dram_in("wk", (KVD // W_OC, 128, (D // 128) * W_OC))
    wv = dram_in("wv", (KVD // W_OC, 128, (D // 128) * W_OC))
    wo = dram_in("wo", (D // WO_OC, 128, (D // 128) * WO_OC))
    acat = dram_in("acat", (128, (D // 128) * 128))
    bcat = dram_in("bcat", (128, D))
    wrt = dram_in("wrt", (128, (D // 128) * 8))
    br_in = dram_in("br", (1, NE))
    cs_in = {(0, "c"): dram_in("cos_o", (128, OWN)),
             (0, "s"): dram_in("sin_o", (128, OWN)),
             (1, "c"): dram_in("cos_p", (128, OWN)),
             (1, "s"): dram_in("sin_p", (128, OWN))}
    masks_in = dram_in("masks", (NQC, 128, 4096), BF16)

    out_hbm = nc.dram_tensor("out", [OWN, D], F32, kind="ExternalOutput").ap()

    y1_d = nc.dram_tensor("y1_d", [OWN, D], F32, kind="Internal").ap()
    y2_d = nc.dram_tensor("y2_d", [OWN, D], F32, kind="Internal").ap()
    q_d = nc.dram_tensor("q_d", [NH * HD, OWN], BF16, kind="Internal").ap()
    k_d = nc.dram_tensor("k_d", [KVD, S], BF16, kind="Internal").ap()
    v_d = nc.dram_tensor("v_d", [S, KVD], BF16, kind="Internal").ap()
    ao_d = nc.dram_tensor("ao_d", [NH * HD, OWN], BF16, kind="Internal").ap()

    identf = nc.inline_tensor(np.eye(128, dtype=np.float32), "identf")
    identb = nc.inline_tensor(np.eye(128, dtype=ml_dtypes.bfloat16), "identb")
    ropePT = nc.inline_tensor(_rope_perm_T(), "ropePT")
    e8 = np.zeros((8, 128), np.float32)
    for h in range(8):
        e8[h, h * 16:(h + 1) * 16] = 1.0
    e8c = nc.inline_tensor(e8, "e8c")
    ones_col_b = nc.inline_tensor(np.ones((128, 1), dtype=ml_dtypes.bfloat16), "ones_col_b")
    ones_sq_b = nc.inline_tensor(np.ones((128, 128), dtype=ml_dtypes.bfloat16), "ones_sq_b")
    ones_col_f = nc.inline_tensor(np.ones((128, 1), dtype=np.float32), "ones_col_f")
    ones_row_f = nc.inline_tensor(np.ones((1, 128), dtype=np.float32), "ones_row_f")

    sb_idn = nc.alloc_sbuf_tensor("sb_idn", [128, 128], F32)
    sb_idnb = nc.alloc_sbuf_tensor("sb_idnb", [128, 128], BF16)
    sb_pT = nc.alloc_sbuf_tensor("sb_pT", [128, 128], BF16)
    sb_e8 = nc.alloc_sbuf_tensor("sb_e8", [8, 128], F32)
    sb_o1b = nc.alloc_sbuf_tensor("sb_o1b", [128, 1], BF16)
    sb_osq = nc.alloc_sbuf_tensor("sb_osq", [128, 128], BF16)
    sb_o1f = nc.alloc_sbuf_tensor("sb_o1f", [128, 1], F32)
    sb_orf = nc.alloc_sbuf_tensor("sb_orf", [1, 128], F32)
    sb_acat = nc.alloc_sbuf_tensor("sb_acat", [128, (D // 128) * 128], BF16)
    sb_bcat = nc.alloc_sbuf_tensor("sb_bcat", [128, D], BF16)
    sb_wrt = nc.alloc_sbuf_tensor("sb_wrt", [128, (D // 128) * 8], BF16)
    sb_brr = nc.alloc_sbuf_tensor("sb_brr", [1, NE], F32)
    sb_brb = nc.alloc_sbuf_tensor("sb_brb", [128, NE], F32)
    sb_qn = nc.alloc_sbuf_tensor("sb_qn", [1, 128], F32)
    sb_bias = nc.alloc_sbuf_tensor("sb_bias", [128, NH], F32)

    ov_base = (nc.sbuf_base + 31) & ~31

    psA = nc.alloc_psum_tensor("psA", [128, 1024], F32)
    psT = nc.alloc_psum_tensor("psT", [128, 2048], BF16)
    psB = nc.alloc_psum_tensor("psB", [128, 1024], F32)
    pav = nc.alloc_psum_tensor("pav", [128, 512], F32)
    plB = nc.alloc_psum_tensor("plB", [128, 512], F32)

    sch = Sched(nc)
    SL = {k: sch.dma_slot(k) for k in (
        "const", "ffnw", "x0", "x1", "lnT0", "lnT1", "w0", "w1", "cs",
        "st_y1", "st_q0", "st_q1", "st_k0", "st_k1", "st_v0", "st_v1",
        "kvld", "attnld", "qld0", "qld1", "mld",
        "st_ao0", "st_ao1", "wo0", "wo1", "y1s0", "y1s1",
        "st_y20", "st_y21", "y2l0", "y2l1", "st_out")}

    overlay_bufs = []

    class Overlay:
        def __init__(self):
            self.off = ov_base
            self.carry = sch.barrier_tokens(overlay_bufs)
            overlay_bufs.clear()

        def t(self, name, shape, dtype):
            itm = {F32: 4, BF16: 2}[dtype]
            t = nc.alloc_sbuf_tensor_at(name, list(shape), dtype, offset=self.off)
            self.off += (shape[1] * itm + 31) & ~31
            assert self.off <= SBUF_TOP, f"overlay overflow at {name}: {self.off}"
            b = Buf(name, self.carry)
            overlay_bufs.append(b)
            return t, b

    ps_bufs = []

    def psum_phase(names):
        carry = sch.barrier_tokens(ps_bufs)
        d = {n: Buf(n, carry) for n in names}
        ps_bufs.clear()
        ps_bufs.extend(d.values())
        return d

    CB = {}
    for name, sb, src in (("idn", sb_idn, identf), ("idnb", sb_idnb, identb),
                          ("pT", sb_pT, ropePT),
                          ("e8", sb_e8, e8c), ("o1b", sb_o1b, ones_col_b), ("osq", sb_osq, ones_sq_b),
                          ("o1f", sb_o1f, ones_col_f), ("orf", sb_orf, ones_row_f)):
        CB[name] = Buf(name)
        sch.op("sync", lambda e, sb=sb, src=src: e.dma_start(out=sb[:], in_=src.ap()),
               writes=[CB[name]], slot=SL["const"])
    CB["brr"] = Buf("brr")
    sch.op("sync", lambda e: e.dma_start(out=sb_brr[:], in_=br_in),
           writes=[CB["brr"]], slot=SL["const"])
    _const_tok = ("const", sch.cnt["const"])
    for _n in ("idn", "idnb", "pT", "e8", "o1b", "osq", "o1f", "orf", "brr"):
        CB[_n].wtok = _const_tok
    for name, sb, src in (("acat", sb_acat, acat), ("bcat", sb_bcat, bcat),
                          ("wrt", sb_wrt, wrt)):
        CB[name] = Buf(name)
        sch.op("gpsimd", lambda e, sb=sb, src=src: e.dma_start(out=sb[:], in_=src),
               writes=[CB[name]], slot=SL["ffnw"])
    _ffnw_tok = ("ffnw", sch.cnt["ffnw"])
    for _n in ("acat", "bcat", "wrt"):
        CB[_n].wtok = _ffnw_tok

    ps = psum_phase(["pA0", "pA1", "pB0", "pB1", "pT0", "pT1", "pav", "plB"])
    CB["brb"] = Buf("brb")
    sch.op("tensor", lambda e: e.matmul(out=psA[:, 0:NE], lhsT=sb_orf[:], rhs=sb_brr[:],
                                        start=True, stop=True),
           reads=[CB["orf"], CB["brr"]], writes=[ps["pA0"]])
    sch.op("scalar", lambda e: e.copy(out=sb_brb[:], in_=psA[:, 0:NE]),
           reads=[ps["pA0"]], writes=[CB["brb"]])
    B_qn = Buf("qn")
    sch.op("vector", lambda e: e.memset(sb_qn[:], 0.0), writes=[B_qn])
    B_bias = Buf("bias")

    DB = {n: Buf(n) for n in ("xo", "xp", "y1", "y2", "q", "k", "v", "ao", "out",
                              "wq", "wk", "wv", "wo", "cs", "masks")}

    ov = Overlay()
    y1fm, b_y1fm = ov.t("y1fm", [128, 32 * HALF], BF16)
    xT, b_xT = [], []
    for i in range(2):
        t, b = ov.t(f"xT{i}", [128, D], F32)
        xT.append(t); b_xT.append(b)
    y1t, b_y1t = ov.t("y1t", [128, D], F32)
    lnt, b_lnt = ov.t("lnt", [128, D], BF16)
    xfm, b_xfm = ov.t("xfm", [128, D], BF16)
    wbuf, b_wbuf = [], []
    for i in range(2):
        t, b = ov.t(f"wbuf{i}", [128, 32 * W_OC], BF16)
        wbuf.append(t); b_wbuf.append(b)
    cs_cos, b_csc = ov.t("cs_cos", [128, HALF], F32)
    cs_sin, b_css = ov.t("cs_sin", [128, HALF], F32)
    stg, b_stg = [], []
    for i in range(4):
        t, b = ov.t(f"stg{i}", [128, 512], BF16)
        stg.append(t); b_stg.append(b)
    t1f, b_t1 = [], []
    t2f, b_t2 = [], []
    for i in range(2):
        t, b = ov.t(f"t1f{i}", [128, 512], F32)
        t1f.append(t); b_t1.append(b)
        t, b = ov.t(f"t2f{i}", [128, 512], F32)
        t2f.append(t); b_t2.append(b)
    qsq, b_qsq = ov.t("qsq", [128, 512], F32)

    class FFNCtx:
        def __init__(self, ovl, tag):
            self.stats, self.b_stats = ovl.t(f"stats{tag}", [128, 48], F32)
            self.lnsc, self.b_lnsc = ovl.t(f"lnsc{tag}", [128, 8], F32)
            self.ffns, self.b_ffns = ovl.t(f"ffns{tag}", [128, 32], F32)
            self.scrT, self.b_scrT = ovl.t(f"scrT{tag}", [8, 128], F32)
            self.scb, self.b_scb = ovl.t(f"scb{tag}", [128, 128], F32)
            self.lows, self.b_lows = ovl.t(f"lows{tag}", [128, 128], BF16)
            self.lowf, self.b_lowf = ovl.t(f"lowf{tag}", [128, 128], F32)
            self.ffo, self.b_ffo = ovl.t(f"ffo{tag}", [128, 512], F32)

    fc = FFNCtx(ov, "A")

    def emit_ln(fx, src_buf, src_t, dst_buf, dst_t):
        def f_stats(e):
            for g in range(8):
                ii = e.bn_stats(out=fx.stats[:, g * 6:(g + 1) * 6],
                                in_=src_t[:, g * 512:(g + 1) * 512])
            return ii
        sch.op("vector", f_stats, reads=[src_buf], writes=[fx.b_stats])
        sch.op("vector", lambda e: e.bn_aggr(out=fx.lnsc[:, 0:2], in_=fx.stats[:]),
               reads=[fx.b_stats], writes=[fx.b_lnsc])
        sch.op("vector", lambda e: e.tensor_scalar(out=fx.lnsc[:, 2:3], in0=fx.lnsc[:, 1:2],
                                                   scalar1=LN_EPS, scalar2=None, op0=ALU.add),
               reads=[fx.b_lnsc], writes=[fx.b_lnsc])
        sch.op("scalar", lambda e: e.activation(out=fx.lnsc[:, 4:5], in_=fx.lnsc[:, 2:3],
                                                func=AF.Ln),
               reads=[fx.b_lnsc], writes=[fx.b_lnsc])
        sch.op("scalar", lambda e: e.activation(out=fx.lnsc[:, 3:4], in_=fx.lnsc[:, 4:5],
                                                func=AF.Exp, scale=-0.5),
               reads=[fx.b_lnsc], writes=[fx.b_lnsc])
        sch.op("vector", lambda e: e.tensor_scalar(out=dst_t[:], in0=src_t[:],
                                                   scalar1=fx.lnsc[:, 0:1],
                                                   scalar2=fx.lnsc[:, 3:4],
                                                   op0=ALU.subtract, op1=ALU.mult),
               reads=[src_buf, fx.b_lnsc], writes=[dst_buf])

    def emit_T(src_buf, src_t, dst_buf, dst_t, colf, slotname):
        for ft in range(32):
            par = ft % 2
            pb = ps[f"pT{par}"]
            pt = psT[:, par * 1024: par * 1024 + 128]
            sch.op("tensor", lambda e, ft=ft, pt=pt: e.transpose(
                out=pt, in_=src_t[:, ft * 128:(ft + 1) * 128], identity=sb_idnb[:]),
                reads=[src_buf, CB["idnb"]], writes=[pb])
            sch.op("scalar", lambda e, ft=ft, pt=pt: e.copy(
                out=dst_t[:, colf(ft):colf(ft) + 128], in_=pt),
                reads=[pb], writes=[dst_buf])

    def emit_ffn(fx, xbuf, x_t, outbuf, out_t, xf_t, b_xf):
        def f_rout(e):
            for ft in range(32):
                ii = e.matmul(out=psB[:, 512:512 + NE],
                              lhsT=xf_t[:, ft * 128:(ft + 1) * 128],
                              rhs=sb_wrt[:, ft * 8:(ft + 1) * 8],
                              start=(ft == 0), stop=(ft == 31))
            return ii
        sch.op("tensor", f_rout, reads=[b_xf, CB["wrt"]], writes=[ps["pB1"]])
        sch.op("scalar", lambda e: e.copy(out=fx.ffns[:, 24:32], in_=psB[:, 512:512 + NE]),
               reads=[ps["pB1"]], writes=[fx.b_ffns])
        sch.op("vector", lambda e: e.tensor_tensor(out=fx.ffns[:, 0:NE], in0=fx.ffns[:, 24:32],
                                                   in1=sb_brb[:], op=ALU.add),
               reads=[fx.b_ffns, CB["brb"]], writes=[fx.b_ffns])
        sch.op("vector", lambda e: e.reduce_max(out=fx.ffns[:, 16:17], in_=fx.ffns[:, 0:NE],
                                                axis=AX.X, negate=True),
               reads=[fx.b_ffns], writes=[fx.b_ffns])
        sch.op("scalar", lambda e: e.activation(out=fx.ffns[:, 8:16], in_=fx.ffns[:, 0:NE],
                                                func=AF.Exp, bias=fx.ffns[:, 16:17],
                                                accum_out=fx.ffns[:, 17:18]),
               reads=[fx.b_ffns], writes=[fx.b_ffns])
        sch.op("vector", lambda e: e.reciprocal(out=fx.ffns[:, 18:19], in_=fx.ffns[:, 17:18]),
               reads=[fx.b_ffns], writes=[fx.b_ffns])
        sch.op("vector", lambda e: e.tensor_scalar(out=fx.ffns[:, 0:NE], in0=fx.ffns[:, 8:16],
                                                   scalar1=fx.ffns[:, 18:19], scalar2=8.0,
                                                   op0=ALU.mult, op1=ALU.mult),
               reads=[fx.b_ffns], writes=[fx.b_ffns])
        sch.op("tensor", lambda e: e.transpose(out=psB[0:8, 512:640], in_=fx.ffns[:, 0:NE],
                                               identity=sb_idn[:]),
               reads=[fx.b_ffns, CB["idn"]], writes=[ps["pB1"]])
        sch.op("scalar", lambda e: e.copy(out=fx.scrT[:], in_=psB[0:8, 512:640]),
               reads=[ps["pB1"]], writes=[fx.b_scrT])
        sch.op("tensor", lambda e: e.matmul(out=psB[:, 640:768], lhsT=sb_e8[:], rhs=fx.scrT[:],
                                            start=True, stop=True),
               reads=[CB["e8"], fx.b_scrT], writes=[ps["pB1"]])
        sch.op("scalar", lambda e: e.copy(out=fx.scb[:], in_=psB[:, 640:768]),
               reads=[ps["pB1"]], writes=[fx.b_scb])
        def f_low(e):
            for ft in range(32):
                ii = e.matmul(out=psB[:, 768:896],
                              lhsT=sb_acat[:, ft * 128:(ft + 1) * 128],
                              rhs=xf_t[:, ft * 128:(ft + 1) * 128],
                              start=(ft == 0), stop=(ft == 31))
            return ii
        sch.op("tensor", f_low, reads=[b_xf, CB["acat"]], writes=[ps["pB1"]])
        sch.op("scalar", lambda e: e.copy(out=fx.lowf[:], in_=psB[:, 768:896]),
               reads=[ps["pB1"]], writes=[fx.b_lowf])
        sch.op("vector", lambda e: e.tensor_tensor(out=fx.lows[:], in0=fx.lowf[:],
                                                   in1=fx.scb[:], op=ALU.mult),
               reads=[fx.b_lowf, fx.b_scb], writes=[fx.b_lows])
        for oc in range(8):
            pbuf = ps["pB0"]
            pt = psB[:, 0:512]
            sch.op("tensor", lambda e, oc=oc, pt=pt: e.matmul(
                out=pt, lhsT=fx.lows[:], rhs=sb_bcat[:, oc * 512:(oc + 1) * 512],
                start=True, stop=True),
                reads=[fx.b_lows, CB["bcat"]], writes=[pbuf])
            sch.op("scalar", lambda e, pt=pt: e.copy(out=fx.ffo[:], in_=pt),
                   reads=[pbuf], writes=[fx.b_ffo])
            sch.op("vector", lambda e, oc=oc: e.tensor_tensor(
                out=out_t[:, oc * 512:(oc + 1) * 512], in0=fx.ffo[:],
                in1=x_t[:, oc * 512:(oc + 1) * 512], op=ALU.add),
                reads=[fx.b_ffo, xbuf], writes=[outbuf])

    def phase_A(x_dram, b_xdram, half, own):
        for ti in range(NT_HALF):
            xs = ti % 2
            row0 = half * HALF + ti * 128
            sch.op("sync", lambda e, row0=row0, xs=xs: e.dma_start(
                out=xT[xs][:], in_=x_dram[row0:row0 + 128, :]),
                reads=[b_xdram], writes=[b_xT[xs]], slot=SL[f"x{xs}"])
            emit_ln(fc, b_xT[xs], xT[xs], b_lnt, lnt)
            emit_T(b_lnt, lnt, b_xfm, xfm, lambda ft: ft * 128, "lnT0")
            emit_ffn(fc, b_xT[xs], xT[xs], b_y1t, y1t, xfm, b_xfm)
            if own:
                sch.op("scalar", lambda e, row0=row0: e.dma_start(
                    out=y1_d[row0:row0 + 128, :], in_=y1t[:]),
                    reads=[b_y1t], writes=[DB["y1"]], slot=SL["st_y1"])
            emit_ln(fc, b_y1t, y1t, b_lnt, lnt)
            emit_T(b_lnt, lnt, b_y1fm, y1fm, lambda ft, ti=ti: ft * HALF + ti * 128, "lnT1")

    def rope_store(pname, o_global, slot_col, tc, pbuf, pt, half, sslot):
        st = b_stg[sslot]
        sch.op("scalar", lambda e: e.copy(out=stg[sslot][:], in_=pt),
               reads=[pbuf], writes=[st])
        if B_ROPE:
            csl = cs_cos[:, tc * 512:(tc + 1) * 512]
            ssl = cs_sin[:, tc * 512:(tc + 1) * 512]
            rsl = 2 + sslot
            sch.op("tensor", lambda e: e.matmul(out=pav[:], lhsT=sb_pT[:], rhs=stg[sslot][:],
                                                start=True, stop=True),
                   reads=[CB["pT"], st], writes=[ps["pav"]])
            sch.op("scalar", lambda e, rsl=rsl: e.copy(out=stg[rsl][:], in_=pav[:]),
                   reads=[ps["pav"]], writes=[b_stg[rsl]])
            sch.op("vector", lambda e, c=csl: e.tensor_tensor(out=t1f[sslot][:],
                                                              in0=stg[sslot][:],
                                                              in1=c, op=ALU.mult),
                   reads=[st, b_csc], writes=[b_t1[sslot]])
            sch.op("vector", lambda e, s=ssl, rsl=rsl: e.tensor_tensor(out=t2f[sslot][:],
                                                                       in0=stg[rsl][:],
                                                                       in1=s, op=ALU.mult),
                   reads=[b_stg[rsl], b_css], writes=[b_t2[sslot]])
            sch.op("vector", lambda e: e.tensor_tensor(out=stg[sslot][:], in0=t1f[sslot][:],
                                                       in1=t2f[sslot][:], op=ALU.add),
                   reads=[b_t1[sslot], b_t2[sslot]], writes=[st])
        if pname == "q":
            col0 = half * HALF + tc * 512
            sch.op("scalar", lambda e, o=o_global, c=col0: e.dma_start(
                out=q_d[o:o + 128, c:c + 512], in_=stg[sslot][:]),
                reads=[st], writes=[DB["q"]], slot=SL[f"st_q{sslot}"])
            h = o_global // 128
            if not B_QNORM:
                return
            sch.op("vector", lambda e: e.tensor_mul(out=qsq[:], in0=stg[sslot][:],
                                                    in1=stg[sslot][:]),
                   reads=[st], writes=[b_qsq])
            sch.op("tensor", lambda e: e.matmul(out=plB[0:1, :], lhsT=sb_o1f[:], rhs=qsq[:],
                                                start=True, stop=True),
                   reads=[CB["o1f"], b_qsq], writes=[ps["plB"]])
            sch.op("scalar", lambda e: e.copy(out=qsq[0:1, :], in_=plB[0:1, :]),
                   reads=[ps["plB"]], writes=[b_qsq])
            sch.op("vector", lambda e: e.reduce_max(out=sb_qn[0:1, 40:41], in_=qsq[0:1, :],
                                                    axis=AX.X),
                   reads=[b_qsq], writes=[B_qn])
            sch.op("vector", lambda e, h=h: e.tensor_tensor(out=sb_qn[0:1, h:h + 1],
                                                            in0=sb_qn[0:1, h:h + 1],
                                                            in1=sb_qn[0:1, 40:41], op=ALU.max),
                   reads=[B_qn], writes=[B_qn])
        else:
            c0 = slot_col * 512
            sch.op("scalar", lambda e, o=o_global, c=c0: e.dma_start(
                out=k_d[o:o + 128, c:c + 512], in_=stg[sslot][:]),
                reads=[st], writes=[DB["k"]], slot=SL[f"st_k{sslot}"])

    def phase_B(half, own):
        slot_base = (0 if own else 4) + 2 * half
        lo = half * HALF
        key = 0 if own else 1
        sch.op("sync", lambda e: e.dma_start(out=cs_cos[:], in_=cs_in[(key, "c")][:, lo:lo + HALF]),
               writes=[b_csc], slot=SL["cs"])
        sch.op("sync", lambda e: e.dma_start(out=cs_sin[:], in_=cs_in[(key, "s")][:, lo:lo + HALF]),
               writes=[b_css], slot=SL["cs"])
        b_csc.wtok = b_css.wtok = ("cs", sch.cnt["cs"])

        projs = ([("q", wq, DB["wq"], D // W_OC)] if own and "q" in B_PROJS else [])
        if "k" in B_PROJS:
            projs.append(("k", wk, DB["wk"], KVD // W_OC))
        if "v" in B_PROJS:
            projs.append(("v", wv, DB["wv"], KVD // W_OC))
        it = 0
        for pname, wsrc, wdb, nchunk in projs:
            for j in range(nchunk):
                ws = it % 2
                it += 1
                sch.op("gpsimd", lambda e, j=j, ws=ws, wsrc=wsrc: e.dma_start(
                    out=wbuf[ws][:], in_=wsrc[j]),
                    reads=[wdb], writes=[b_wbuf[ws]], slot=SL[f"w{ws}"])
                if pname == "v":
                    for tt in range(NT_HALF):
                        sslot = tt % 2
                        pbuf = ps["pA0"] if sslot == 0 else ps["pA1"]
                        pt = psA[:, 0:512] if sslot == 0 else psA[:, 512:1024]
                        def f_vm(e, tt=tt, pt=pt, ws=ws):
                            for ft in range(32):
                                ii = e.matmul(
                                    out=pt[:, 0:W_OC],
                                    lhsT=y1fm[:, ft * HALF + tt * 128: ft * HALF + (tt + 1) * 128],
                                    rhs=wbuf[ws][:, ft * W_OC:(ft + 1) * W_OC],
                                    start=(ft == 0), stop=(ft == 31))
                            return ii
                        sch.op("tensor", f_vm, reads=[b_wbuf[ws], b_y1fm], writes=[pbuf])
                        st = b_stg[2 + sslot]
                        sch.op("scalar", lambda e, pt=pt, sslot=sslot: e.copy(
                            out=stg[2 + sslot][:, 0:W_OC], in_=pt[:, 0:W_OC]),
                            reads=[pbuf], writes=[st])
                        grow = (slot_base + tt // 4) * 512 + (tt % 4) * 128
                        sch.op("scalar", lambda e, g=grow, j=j, sslot=sslot: e.dma_start(
                            out=v_d[g:g + 128, j * W_OC:(j + 1) * W_OC],
                            in_=stg[2 + sslot][:, 0:W_OC]),
                            reads=[st], writes=[DB["v"]], slot=SL[f"st_v{sslot}"])
                else:
                    for o1 in range(W_OC // 128):
                        o_global = j * W_OC + o1 * 128
                        for tc in range(2):
                            sslot = (o1 * 2 + tc) % 2
                            pbuf = ps["pA0"] if sslot == 0 else ps["pA1"]
                            pt = psA[:, 0:512] if sslot == 0 else psA[:, 512:1024]
                            def f_mm(e, o1=o1, tc=tc, pt=pt, ws=ws):
                                for ft in range(32):
                                    ii = e.matmul(
                                        out=pt,
                                        lhsT=wbuf[ws][:, ft * W_OC + o1 * 128: ft * W_OC + (o1 + 1) * 128],
                                        rhs=y1fm[:, ft * HALF + tc * 512: ft * HALF + (tc + 1) * 512],
                                        start=(ft == 0), stop=(ft == 31))
                                return ii
                            sch.op("tensor", f_mm, reads=[b_wbuf[ws], b_y1fm], writes=[pbuf])
                            rope_store(pname, o_global, slot_base + tc, tc, pbuf, pt, half, sslot)

    last_toks = []
    for half in range(2):
        phase_A(x_own, DB["xo"], half, own=True)
        if "B" in phases:
            phase_B(half, own=True)
    if "B" in phases:
        for half in range(2):
            phase_A(x_part, DB["xp"], half, own=False)
            phase_B(half, own=False)
    if "C" not in phases:
        for tt in range(OWN // 128):
            tk = sch.op("sync", lambda e, tt=tt: e.dma_start(
                out=out_hbm[tt * 128:(tt + 1) * 128, :],
                in_=y1_d[tt * 128:(tt + 1) * 128, :]),
                reads=[DB["y1"]], writes=[DB["out"]], slot=SL["st_out"])
            last_toks.append(tk)
        if "B" in phases:
            last_toks.append(DB["q"].wtok)
            last_toks.append(DB["k"].wtok)
            last_toks.append(DB["v"].wtok)
        sch.finish(last_toks)
        return _finish_block(nc, sch)

    ovc = Overlay()
    K_sb, b_K = ovc.t("K_sb", [128, NKV * S], BF16)
    V_sb, b_V = ovc.t("V_sb", [128, (S // 128) * KVD], BF16)
    qtl, b_qtl = [], []
    for i in range(2):
        t, b = ovc.t(f"qtl{i}", [128, 512], BF16)
        qtl.append(t); b_qtl.append(b)
    exb, b_exb = [], []
    for i in range(2):
        t, b = ovc.t(f"exb{i}", [128, 1536], BF16)
        exb.append(t); b_exb.append(b)
    maskb, b_maskb = ovc.t("maskb", [128, 4096], BF16)
    rlb, b_rlb = ovc.t("rlb", [128, 512], F32)
    rlr, b_rlr = ovc.t("rlr", [1, 512], F32)
    aost, b_aost = [], []
    for i in range(2):
        t, b = ovc.t(f"aost{i}", [128, 512], BF16)
        aost.append(t); b_aost.append(b)
    ksq, b_ksq = ovc.t("ksq", [128, 4096], F32)
    avf, b_avf = ovc.t("avf", [128, 512], F32)

    psc = psum_phase(["pA0", "pA1", "pB0", "pB1", "pT0", "pT1", "pav", "plB"])

    def f_kld(e, sem):
        for kv in range(NKV):
            e.dma_start(out=K_sb[:, kv * S:(kv + 1) * S],
                        in_=k_d[kv * 128:(kv + 1) * 128, :].rearrange("d pos -> d pos")
                        ).then_inc(sem, 16)
    sch.op("sync", f_kld, reads=[DB["k"]], writes=[b_K], slot=SL["kvld"],
           n_dma=NKV, self_inc=True)

    def f_vld(e, sem):
        for g in range(4):
            e.dma_start(
                out=V_sb[:, g * 8 * KVD:(g + 1) * 8 * KVD].rearrange(
                    "p (pt c) -> p pt c", pt=8),
                in_=v_d[g * 1024:(g + 1) * 1024, :].rearrange(
                    "(pt p) c -> p pt c", p=128)).then_inc(sem, 16)
    sch.op("sync", f_vld, reads=[DB["v"]], writes=[b_V], slot=SL["kvld"],
           n_dma=4, self_inc=True)
    b_K.wtok = b_V.wtok = ("kvld", sch.cnt["kvld"])

    for kv in range(NKV):
        sch.op("vector", lambda e, kv=kv: e.tensor_mul(
            out=ksq[:], in0=K_sb[:, kv * S:(kv + 1) * S], in1=K_sb[:, kv * S:(kv + 1) * S]),
            reads=[b_K], writes=[b_ksq])
        for c8 in range(8):
            sch.op("tensor", lambda e, c8=c8: e.matmul(
                out=plB[0:1, :], lhsT=sb_o1f[:], rhs=ksq[:, c8 * 512:(c8 + 1) * 512],
                start=True, stop=True),
                reads=[CB["o1f"], b_ksq], writes=[psc["plB"]])
            sch.op("scalar", lambda e: e.copy(out=rlr[:], in_=plB[0:1, :]),
                   reads=[psc["plB"]], writes=[b_rlr])
            sch.op("vector", lambda e: e.reduce_max(out=sb_qn[0:1, 40:41], in_=rlr[:],
                                                    axis=AX.X),
                   reads=[b_rlr], writes=[B_qn])
            sch.op("vector", lambda e, kv=kv: e.tensor_tensor(
                out=sb_qn[0:1, 32 + kv:33 + kv], in0=sb_qn[0:1, 32 + kv:33 + kv],
                in1=sb_qn[0:1, 40:41], op=ALU.max),
                reads=[B_qn], writes=[B_qn])

    for h in range(NH):
        kv = h // 4
        sch.op("vector", lambda e, h=h, kv=kv: e.tensor_tensor(
            out=sb_qn[0:1, 41:42], in0=sb_qn[0:1, h:h + 1],
            in1=sb_qn[0:1, 32 + kv:33 + kv], op=ALU.mult),
            reads=[B_qn], writes=[B_qn])
        sch.op("scalar", lambda e: e.activation(out=sb_qn[0:1, 43:44], in_=sb_qn[0:1, 41:42],
                                                func=AF.Ln),
               reads=[B_qn], writes=[B_qn])
        sch.op("scalar", lambda e: e.activation(out=sb_qn[0:1, 42:43], in_=sb_qn[0:1, 43:44],
                                                func=AF.Exp, scale=0.5),
               reads=[B_qn], writes=[B_qn])
        sch.op("vector", lambda e, h=h: e.tensor_scalar(
            out=sb_qn[0:1, 64 + h:65 + h], in0=sb_qn[0:1, 42:43],
            scalar1=-RSCALE, scalar2=None, op0=ALU.mult),
            reads=[B_qn], writes=[B_qn])
    sch.op("tensor", lambda e: e.matmul(out=pav[:, 0:NH], lhsT=sb_orf[:],
                                        rhs=sb_qn[0:1, 64:64 + NH], start=True, stop=True),
           reads=[CB["orf"], B_qn], writes=[psc["pav"]])
    sch.op("scalar", lambda e: e.copy(out=sb_bias[:], in_=pav[:, 0:NH]),
           reads=[psc["pav"]], writes=[B_bias])

    for p in range(NQC):
        sch.op("sync", lambda e, p=p: e.dma_start(out=maskb[:], in_=masks_in[p]),
               reads=[DB["masks"]], writes=[b_maskb], slot=SL["mld"])
        tiles = [(sl, j0) for sl in SLOTS_P[p] for j0 in range(4)]
        nk = len(tiles)
        for h in range(NH):
            kv = h // 4
            qs = h % 2
            sch.op("sync", lambda e, p=p, h=h, qs=qs: e.dma_start(
                out=qtl[qs][:], in_=q_d[h * 128:(h + 1) * 128, p * 512:(p + 1) * 512]),
                reads=[DB["q"]], writes=[b_qtl[qs]], slot=SL[f"qld{qs}"])

            groups = [tiles[i:i + 2] for i in range(0, nk, 2)]

            def emit_scores(gi):
                grp = groups[gi]
                pb = [psc["pA0"], psc["pA1"]] if gi % 2 == 0 else [psc["pB0"], psc["pB1"]]
                pt = psA if gi % 2 == 0 else psB
                def f(e, grp=grp, pt=pt, qs=qs, kv=kv):
                    for i, (sl, j0) in enumerate(grp):
                        pos = sl * 4 + j0
                        ii = e.matmul(out=pt[:, i * 512:(i + 1) * 512],
                                      lhsT=K_sb[:, kv * S + pos * 128: kv * S + (pos + 1) * 128],
                                      rhs=qtl[qs][:], start=True, stop=True,
                                      skip_group_check=True)
                    return ii
                sch.op("tensor", f, reads=[b_K, b_qtl[qs]], writes=pb)

            def emit_rest(gi, first_tile_idx):
                grp = groups[gi]
                pb = [psc["pA0"], psc["pA1"]] if gi % 2 == 0 else [psc["pB0"], psc["pB1"]]
                pt = psA if gi % 2 == 0 else psB
                eb = b_exb[gi % 2]
                et = exb[gi % 2]
                n = len(grp)
                sch.op("scalar", lambda e, pt=pt, et=et, n=n, h=h: e.activation(
                    out=et[:, 0:n * 512], in_=pt[:, 0:n * 512], func=AF.Exp,
                    bias=sb_bias[:, h:h + 1], scale=RSCALE),
                    reads=pb + [B_bias], writes=[eb])
                for i, (sl, j0) in enumerate(grp):
                    if sl == p or sl == p + 4:
                        mi = 0 if sl == p else 1
                        mc = mi * 2048 + j0 * 512
                        sch.op("vector", lambda e, i=i, mc=mc, et=et: e.tensor_tensor(
                            out=et[:, i * 512:(i + 1) * 512], in0=et[:, i * 512:(i + 1) * 512],
                            in1=maskb[:, mc:mc + 512], op=ALU.mult),
                            reads=[eb, b_maskb], writes=[eb])
                for i, (sl, j0) in enumerate(grp):
                    ti = first_tile_idx + i
                    pos = sl * 4 + j0
                    sch.op("tensor", lambda e, i=i, pos=pos, et=et, kv=kv, ti=ti: e.matmul(
                        out=pav[:], lhsT=V_sb[:, pos * KVD + kv * 128: pos * KVD + (kv + 1) * 128],
                        rhs=et[:, i * 512:(i + 1) * 512],
                        start=(ti == 0), stop=(ti == nk - 1), skip_group_check=True),
                        reads=[eb, b_V], writes=[psc["pav"]])
                    sch.op("tensor", lambda e, i=i, et=et, ti=ti: e.matmul(
                        out=plB[:], lhsT=sb_osq[:], rhs=et[:, i * 512:(i + 1) * 512],
                        start=(ti == 0), stop=(ti == nk - 1), skip_group_check=True),
                        reads=[eb, CB["osq"]], writes=[psc["plB"]])

            emit_scores(0)
            ti0 = 0
            for gi in range(1, len(groups)):
                emit_scores(gi)
                emit_rest(gi - 1, ti0)
                ti0 += len(groups[gi - 1])
            emit_rest(len(groups) - 1, ti0)

            sch.op("scalar", lambda e: e.activation(out=rlb[:], in_=plB[:], func=AF.Ln),
                   reads=[psc["plB"]], writes=[b_rlb])
            sch.op("scalar", lambda e: e.activation(out=rlb[:], in_=rlb[:], func=AF.Exp,
                                                    scale=-1.0),
                   reads=[b_rlb], writes=[b_rlb])
            ao_s = h % 2
            sch.op("scalar", lambda e: e.copy(out=avf[:], in_=pav[:]),
                   reads=[psc["pav"]], writes=[b_avf])
            sch.op("vector", lambda e, ao_s=ao_s: e.tensor_tensor(
                out=aost[ao_s][:], in0=avf[:], in1=rlb[:], op=ALU.mult),
                reads=[b_avf, b_rlb], writes=[b_aost[ao_s]])
            sch.op("scalar", lambda e, p=p, h=h, ao_s=ao_s: e.dma_start(
                out=ao_d[h * 128:(h + 1) * 128, p * 512:(p + 1) * 512], in_=aost[ao_s][:]),
                reads=[b_aost[ao_s]], writes=[DB["ao"]], slot=SL[f"st_ao{ao_s}"])

    if "D" not in phases:
        for tt in range(OWN // 128):
            tk = sch.op("sync", lambda e, tt=tt: e.dma_start(
                out=out_hbm[tt * 128:(tt + 1) * 128, :],
                in_=y1_d[tt * 128:(tt + 1) * 128, :]),
                reads=[DB["y1"]], writes=[DB["out"]], slot=SL["st_out"])
        sch.finish([tk, DB["ao"].wtok])
        return _finish_block(nc, sch)

    ovd = Overlay()
    attn_sb, b_attn = ovd.t("attn_sb", [128, 32 * OWN], BF16)
    wob, b_wob = [], []
    for i in range(2):
        t, b = ovd.t(f"wob{i}", [128, 32 * WO_OC], BF16)
        wob.append(t); b_wob.append(b)
    y1s, b_y1s = [], []
    for i in range(2):
        t, b = ovd.t(f"y1s{i}", [128, WO_OC], F32)
        y1s.append(t); b_y1s.append(b)
    y2s, b_y2s = [], []
    for i in range(2):
        t, b = ovd.t(f"y2s{i}", [128, WO_OC], F32)
        y2s.append(t); b_y2s.append(b)
    y2f, b_y2f = [], []
    for i in range(2):
        t, b = ovd.t(f"y2f{i}", [128, WO_OC], F32)
        y2f.append(t); b_y2f.append(b)

    psd = psum_phase(["pA0", "pA1", "pB0", "pB1", "pT0", "pT1", "pav", "plB"])

    def f_ald(e, sem):
        for g in range(4):
            e.dma_start(
                out=attn_sb[:, g * 8 * OWN:(g + 1) * 8 * OWN].rearrange(
                    "p (hc q) -> p hc q", hc=8),
                in_=ao_d[g * 1024:(g + 1) * 1024, :].rearrange(
                    "(hc p) q -> p hc q", p=128)).then_inc(sem, 16)
    sch.op("sync", f_ald, reads=[DB["ao"]], writes=[b_attn], slot=SL["attnld"],
           n_dma=4, self_inc=True)

    for oc in range(D // WO_OC):
        ws = oc % 2
        sch.op("gpsimd", lambda e, oc=oc, ws=ws: e.dma_start(out=wob[ws][:], in_=wo[oc]),
               reads=[DB["wo"]], writes=[b_wob[ws]], slot=SL[f"wo{ws}"])
        for tt in range(OWN // 128):
            sl2 = tt % 2
            pbuf = psd["pA0"] if sl2 == 0 else psd["pA1"]
            pt = psA[:, 0:WO_OC] if sl2 == 0 else psA[:, 512:512 + WO_OC]
            def f_wo(e, tt=tt, ws=ws, pt=pt):
                for hc in range(32):
                    ii = e.matmul(out=pt,
                                  lhsT=attn_sb[:, hc * OWN + tt * 128: hc * OWN + (tt + 1) * 128],
                                  rhs=wob[ws][:, hc * WO_OC:(hc + 1) * WO_OC],
                                  start=(hc == 0), stop=(hc == 31))
                return ii
            sch.op("tensor", f_wo, reads=[b_wob[ws], b_attn], writes=[pbuf])
            sch.op("sync", lambda e, tt=tt, oc=oc, sl2=sl2: e.dma_start(
                out=y1s[sl2][:], in_=y1_d[tt * 128:(tt + 1) * 128, oc * WO_OC:(oc + 1) * WO_OC]),
                reads=[DB["y1"]], writes=[b_y1s[sl2]], slot=SL[f"y1s{sl2}"])
            sch.op("scalar", lambda e, sl2=sl2, pt=pt: e.copy(out=y2f[sl2][:], in_=pt),
                   reads=[pbuf], writes=[b_y2f[sl2]])
            sch.op("vector", lambda e, sl2=sl2: e.tensor_tensor(
                out=y2s[sl2][:], in0=y2f[sl2][:], in1=y1s[sl2][:], op=ALU.add),
                reads=[b_y2f[sl2], b_y1s[sl2]], writes=[b_y2s[sl2]])
            sch.op("scalar", lambda e, tt=tt, oc=oc, sl2=sl2: e.dma_start(
                out=y2_d[tt * 128:(tt + 1) * 128, oc * WO_OC:(oc + 1) * WO_OC],
                in_=y2s[sl2][:]),
                reads=[b_y2s[sl2]], writes=[DB["y2"]], slot=SL[f"st_y2{sl2}"])

    if "E" not in phases:
        for tt in range(OWN // 128):
            tk = sch.op("sync", lambda e, tt=tt: e.dma_start(
                out=out_hbm[tt * 128:(tt + 1) * 128, :],
                in_=y2_d[tt * 128:(tt + 1) * 128, :]),
                reads=[DB["y2"]], writes=[DB["out"]], slot=SL["st_out"])
        sch.finish([tk])
        return _finish_block(nc, sch)

    ove = Overlay()
    y2T, b_y2T = [], []
    for i in range(2):
        t, b = ove.t(f"y2T{i}", [128, D], F32)
        y2T.append(t); b_y2T.append(b)
    y3t, b_y3t = ove.t("y3t", [128, D], F32)
    lntE, b_lntE = ove.t("lntE", [128, D], BF16)
    xfmE, b_xfmE = ove.t("xfmE", [128, D], BF16)
    fcE = FFNCtx(ove, "E")

    pse = psum_phase(["pA0", "pA1", "pB0", "pB1", "pT0", "pT1", "pav", "plB"])
    ps.update(pse)

    out_toks = []

    def phase_E():
        for tt in range(OWN // 128):
            ysl = tt % 2
            row0 = tt * 128
            sch.op("sync", lambda e, row0=row0, ysl=ysl: e.dma_start(
                out=y2T[ysl][:], in_=y2_d[row0:row0 + 128, :]),
                reads=[DB["y2"]], writes=[b_y2T[ysl]], slot=SL[f"y2l{ysl}"])
            emit_ln(fcE, b_y2T[ysl], y2T[ysl], b_lntE, lntE)
            emit_T(b_lntE, lntE, b_xfmE, xfmE, lambda ft: ft * 128, "lnT0")
            emit_ffn(fcE, b_y2T[ysl], y2T[ysl], b_y3t, y3t, xfmE, b_xfmE)
            tk = sch.op("sync", lambda e, row0=row0: e.dma_start(
                out=out_hbm[row0:row0 + 128, :], in_=y3t[:]),
                reads=[b_y3t], writes=[DB["out"]], slot=SL["st_out"])
            out_toks.append(tk)

    phase_E()
    sch.finish(out_toks[-1:])
    return _finish_block(nc, sch)


def _finish_block(nc, sch):
    with nc.Block() as block:
        @block.sync
        def _(e):
            for f in sch.ops["sync"]:
                f(e)

        @block.tensor
        def _(e):
            for f in sch.ops["tensor"]:
                f(e)

        @block.vector
        def _(e):
            for f in sch.ops["vector"]:
                f(e)

        @block.scalar
        def _(e):
            for f in sch.ops["scalar"]:
                f(e)

        @block.gpsimd
        def _(e):
            for f in sch.ops["gpsimd"]:
                f(e)

    return nc



def _own_rows(hg):
    qcs = OWN_QCS[hg]
    return np.concatenate([np.arange(g * 512, (g + 1) * 512) for g in qcs])


def make_in_maps(x, Wq, Wk, Wv, Wo, A, Bm, Wr, br):
    wq_t = _tile_lhsT(np.ascontiguousarray(Wq.T), W_OC)
    wk_t = _tile_lhsT(np.ascontiguousarray(Wk.T), W_OC)
    wv_t = _tile_lhsT(np.ascontiguousarray(Wv.T), W_OC)
    wo_t = _tile_lhsT(np.ascontiguousarray(Wo.T), WO_OC)
    acat = np.ascontiguousarray(
        _tile_lhsT(np.ascontiguousarray(A.transpose(2, 0, 1).reshape(D, 128)), 128)[0])
    bcat = np.ascontiguousarray(Bm.transpose(0, 2, 1).reshape(128, D))
    wrt = np.ascontiguousarray(
        _tile_lhsT(np.ascontiguousarray(Wr.T), 8)[0])
    brr = np.ascontiguousarray(br.reshape(1, NE))

    in_maps = []
    for c in range(8):
        b, hg = c // 2, c % 2
        rows_o = _own_rows(hg)
        rows_p = _own_rows(1 - hg)
        cos_o, sin_o = _rope_tables(rows_o)
        cos_p, sin_p = _rope_tables(rows_p)
        in_maps.append({
            "x_own": np.ascontiguousarray(x[b][rows_o]),
            "x_part": np.ascontiguousarray(x[b][rows_p]),
            "wq": wq_t, "wk": wk_t, "wv": wv_t, "wo": wo_t,
            "acat": acat, "bcat": bcat, "wrt": wrt, "br": brr,
            "cos_o": cos_o, "sin_o": sin_o, "cos_p": cos_p, "sin_p": sin_p,
            "masks": _masks_for(hg),
        })
    return in_maps


def kernel(L0_attn_hidden_states, L0_attn_Wq, L0_attn_Wk, L0_attn_Wv, L0_attn_Wo,
           L0_ffn_A, L0_ffn_B, L0_ffn_Wr, L0_ffn_br):
    x = np.asarray(L0_attn_hidden_states, dtype=np.float32)
    in_maps = make_in_maps(x,
                           np.asarray(L0_attn_Wq, np.float32),
                           np.asarray(L0_attn_Wk, np.float32),
                           np.asarray(L0_attn_Wv, np.float32),
                           np.asarray(L0_attn_Wo, np.float32),
                           np.asarray(L0_ffn_A, np.float32),
                           np.asarray(L0_ffn_B, np.float32),
                           np.asarray(L0_ffn_Wr, np.float32),
                           np.asarray(L0_ffn_br, np.float32))
    nc = build_program()
    res = run_bass_kernel_spmd(nc, in_maps, core_ids=list(range(8)))
    out = np.empty((NB, S, D), dtype=np.float32)
    for c in range(8):
        b, hg = c // 2, c % 2
        out[b][_own_rows(hg)] = res.results[c]["out"]
    return out


# version 04
# speedup vs baseline: 1.1523x; 1.0344x over previous
import math

import numpy as np
import ml_dtypes

import concourse.bass as bass
import concourse.mybir as mybir
from concourse.bass_utils import run_bass_kernel_spmd

F32 = mybir.dt.float32
BF16 = mybir.dt.bfloat16
AF = mybir.ActivationFunctionType
ALU = mybir.AluOpType
AX = mybir.AxisListType

D = 4096
S = 4096
NB = 4
NH = 32
NKV = 8
HD = 128
KVD = NKV * HD
NE = 8
OWN = 2048
HALF = 1024
NQC = 4
NT_HALF = 8
RSCALE = float(1.0 / math.sqrt(HD))
LN_EPS = 1e-5

OWN_QCS = ([0, 3, 4, 7], [1, 2, 5, 6])
NK_UNIF = [8, 16, 24, 32]
SLOTS_P = [[0, 4], [0, 1, 4, 5], [0, 1, 2, 4, 5, 6], [0, 1, 2, 3, 4, 5, 6, 7]]

W_OC = 256
WO_OC = 256

B_PROJS = ("q", "k", "v")
B_ROPE = True
B_QNORM = True
A_TRANSPOSE = True
SBUF_TOP = 229344



def _tile_lhsT(wt, oc):
    f, o = wt.shape
    return np.ascontiguousarray(
        wt.reshape(f // 128, 128, o // oc, oc).transpose(2, 1, 0, 3).reshape(o // oc, 128, -1)
    )


def _rope_tables(positions):
    pos = np.asarray(positions, dtype=np.float64)[None, :]
    half = np.arange(64, dtype=np.float64)
    freqs = np.exp(-2.0 * half * (math.log(10000.0) / HD))[:, None]
    ang = pos * freqs
    cos = np.concatenate([np.cos(ang), np.cos(ang)], axis=0).astype(np.float32)
    sin = np.concatenate([np.sin(ang), np.sin(ang)], axis=0).astype(np.float32)
    return np.ascontiguousarray(cos), np.ascontiguousarray(sin)


def _masks_for(hg):
    own = OWN_QCS[hg]
    part = OWN_QCS[1 - hg]
    g_order = own + part
    out = np.zeros((NQC, 128, 4096), dtype=np.float32)
    kp = np.arange(128)[:, None]
    qf = np.arange(512)[None, :]
    for p in range(NQC):
        gq = own[p]
        for mi, slot in enumerate((p, p + 4)):
            g = g_order[slot]
            for j0 in range(4):
                if g < gq:
                    m = np.ones((128, 512), np.float32)
                elif g > gq:
                    m = np.zeros((128, 512), np.float32)
                else:
                    m = ((kp + j0 * 128) <= qf).astype(np.float32)
                out[p, :, mi * 2048 + j0 * 512: mi * 2048 + (j0 + 1) * 512] = m
    return np.ascontiguousarray(out.astype(ml_dtypes.bfloat16))


def _rope_perm_T():
    p = np.zeros((128, 128), np.float32)
    for dd in range(64):
        p[dd, dd + 64] = -1.0
        p[dd + 64, dd] = 1.0
    return np.ascontiguousarray(p.T.astype(ml_dtypes.bfloat16))



class Buf:
    __slots__ = ("name", "wtok", "rtoks")

    def __init__(self, name, carry=()):
        self.name = name
        self.wtok = None
        self.rtoks = list(carry)


class Sched:
    ENGINES = ("tensor", "vector", "scalar", "gpsimd", "sync")

    def __init__(self, nc):
        self.nc = nc
        self.ops = {e: [] for e in self.ENGINES}
        self.sem = {}
        self.cnt = {}
        self.waited = {e: {} for e in self.ENGINES}
        for e in ("tensor", "vector", "scalar", "gpsimd"):
            self.sem[e] = nc.alloc_semaphore(f"sem_{e}")
            self.cnt[e] = 0
        self._sn = 0

    def dma_slot(self, name):
        self.sem[name] = self.nc.alloc_semaphore(f"ds_{name}")
        self.cnt[name] = 0
        return name

    def _wait(self, eng, tok):
        if tok is None:
            return
        if eng == "tensor" and tok[0] == "tensor":
            return
        prev = self.waited[eng].get(tok[0], 0)
        if tok[1] <= prev:
            return
        self.waited[eng][tok[0]] = tok[1]
        sem = self.sem[tok[0]]
        val = tok[1]
        self.ops[eng].append(lambda e, s=sem, v=val: e.wait_ge(s, v))

    def op(self, eng, fn, reads=(), writes=(), slot=None, n_dma=1, self_inc=False):
        for b in reads:
            self._wait(eng, b.wtok)
        for b in writes:
            self._wait(eng, b.wtok)
            for t in b.rtoks:
                self._wait(eng, t)
        if slot is None:
            key, inc = eng, 1
            self.cnt[key] += 1
        else:
            key, inc = slot, 16
            self.cnt[key] += 16 * n_dma
        val = self.cnt[key]
        sem = self.sem[key]
        if self_inc:
            self.ops[eng].append(lambda e, f=fn, s=sem: f(e, s))
        else:
            self.ops[eng].append(lambda e, f=fn, s=sem, i=inc: f(e).then_inc(s, i))
        tok = (key, val)
        for b in reads:
            b.rtoks.append(tok)
        for b in writes:
            b.wtok = tok
            b.rtoks = []
        return tok

    def barrier_tokens(self, bufs):
        toks = []
        for b in bufs:
            if b.wtok is not None:
                toks.append(b.wtok)
            toks.extend(b.rtoks)
        return toks

    def finish(self, toks):
        for t in toks:
            self._wait("sync", t)



def build_program(phases="ABCDE"):
    nc = bass.Bass("TRN2", target_bir_lowering=False)

    def dram_in(name, shape, dtype=F32):
        return nc.dram_tensor(name, list(shape), dtype, kind="ExternalInput").ap()

    x_own = dram_in("x_own", (OWN, D))
    x_part = dram_in("x_part", (OWN, D))
    wq = dram_in("wq", (D // W_OC, 128, (D // 128) * W_OC))
    wk = dram_in("wk", (KVD // W_OC, 128, (D // 128) * W_OC))
    wv = dram_in("wv", (KVD // W_OC, 128, (D // 128) * W_OC))
    wo = dram_in("wo", (D // WO_OC, 128, (D // 128) * WO_OC))
    acat = dram_in("acat", (128, (D // 128) * 128))
    bcat = dram_in("bcat", (128, D))
    wrt = dram_in("wrt", (128, (D // 128) * 8))
    br_in = dram_in("br", (1, NE))
    cs_in = {(0, "c"): dram_in("cos_o", (128, OWN)),
             (0, "s"): dram_in("sin_o", (128, OWN)),
             (1, "c"): dram_in("cos_p", (128, OWN)),
             (1, "s"): dram_in("sin_p", (128, OWN))}
    masks_in = dram_in("masks", (NQC, 128, 4096), BF16)

    out_hbm = nc.dram_tensor("out", [OWN, D], F32, kind="ExternalOutput").ap()

    y1_d = nc.dram_tensor("y1_d", [OWN, D], F32, kind="Internal").ap()
    wqb = nc.dram_tensor("wqb", [D // W_OC, 128, (D // 128) * W_OC], BF16, kind="Internal").ap()
    wkb = nc.dram_tensor("wkb", [KVD // W_OC, 128, (D // 128) * W_OC], BF16, kind="Internal").ap()
    wvb = nc.dram_tensor("wvb", [KVD // W_OC, 128, (D // 128) * W_OC], BF16, kind="Internal").ap()
    wob = nc.dram_tensor("wob", [D // WO_OC, 128, (D // 128) * WO_OC], BF16, kind="Internal").ap()
    y2_d = nc.dram_tensor("y2_d", [OWN, D], F32, kind="Internal").ap()
    q_d = nc.dram_tensor("q_d", [NH * HD, OWN], BF16, kind="Internal").ap()
    k_d = nc.dram_tensor("k_d", [KVD, S], BF16, kind="Internal").ap()
    v_d = nc.dram_tensor("v_d", [S, KVD], BF16, kind="Internal").ap()
    ao_d = nc.dram_tensor("ao_d", [NH * HD, OWN], BF16, kind="Internal").ap()

    identf = nc.inline_tensor(np.eye(128, dtype=np.float32), "identf")
    identb = nc.inline_tensor(np.eye(128, dtype=ml_dtypes.bfloat16), "identb")
    ropePT = nc.inline_tensor(_rope_perm_T(), "ropePT")
    e8 = np.zeros((8, 128), np.float32)
    for h in range(8):
        e8[h, h * 16:(h + 1) * 16] = 1.0
    e8c = nc.inline_tensor(e8, "e8c")
    ones_col_b = nc.inline_tensor(np.ones((128, 1), dtype=ml_dtypes.bfloat16), "ones_col_b")
    ones_sq_b = nc.inline_tensor(np.ones((128, 128), dtype=ml_dtypes.bfloat16), "ones_sq_b")
    ones_col_f = nc.inline_tensor(np.ones((128, 1), dtype=np.float32), "ones_col_f")
    ones_row_f = nc.inline_tensor(np.ones((1, 128), dtype=np.float32), "ones_row_f")

    sb_idn = nc.alloc_sbuf_tensor("sb_idn", [128, 128], F32)
    sb_idnb = nc.alloc_sbuf_tensor("sb_idnb", [128, 128], BF16)
    sb_pT = nc.alloc_sbuf_tensor("sb_pT", [128, 128], BF16)
    sb_e8 = nc.alloc_sbuf_tensor("sb_e8", [8, 128], F32)
    sb_o1b = nc.alloc_sbuf_tensor("sb_o1b", [128, 1], BF16)
    sb_osq = nc.alloc_sbuf_tensor("sb_osq", [128, 128], BF16)
    sb_o1f = nc.alloc_sbuf_tensor("sb_o1f", [128, 1], F32)
    sb_orf = nc.alloc_sbuf_tensor("sb_orf", [1, 128], F32)
    sb_acat = nc.alloc_sbuf_tensor("sb_acat", [128, (D // 128) * 128], BF16)
    sb_bcat = nc.alloc_sbuf_tensor("sb_bcat", [128, D], BF16)
    sb_wrt = nc.alloc_sbuf_tensor("sb_wrt", [128, (D // 128) * 8], BF16)
    sb_brr = nc.alloc_sbuf_tensor("sb_brr", [1, NE], F32)
    sb_brb = nc.alloc_sbuf_tensor("sb_brb", [128, NE], F32)
    sb_qn = nc.alloc_sbuf_tensor("sb_qn", [1, 128], F32)
    sb_bias = nc.alloc_sbuf_tensor("sb_bias", [128, NH], F32)

    ov_base = (nc.sbuf_base + 31) & ~31

    psA = nc.alloc_psum_tensor("psA", [128, 1024], F32)
    psT = nc.alloc_psum_tensor("psT", [128, 2048], BF16)
    psB = nc.alloc_psum_tensor("psB", [128, 1024], F32)
    pav = nc.alloc_psum_tensor("pav", [128, 512], F32)
    plB = nc.alloc_psum_tensor("plB", [128, 512], F32)

    sch = Sched(nc)
    SL = {k: sch.dma_slot(k) for k in (
        "const", "ffnw", "x0", "x1", "lnT0", "lnT1", "w0", "w1", "cs",
        "st_y1", "st_q0", "st_q1", "st_k0", "st_k1", "st_v0", "st_v1",
        "kvld", "attnld", "qld0", "qld1", "mld",
        "wcq", "wck", "wcv", "wco",
        "st_ao0", "st_ao1", "wo0", "wo1", "y1s0", "y1s1",
        "st_y20", "st_y21", "y2l0", "y2l1", "st_out")}

    overlay_bufs = []

    class Overlay:
        def __init__(self):
            self.off = ov_base
            self.carry = sch.barrier_tokens(overlay_bufs)
            overlay_bufs.clear()

        def t(self, name, shape, dtype):
            itm = {F32: 4, BF16: 2}[dtype]
            t = nc.alloc_sbuf_tensor_at(name, list(shape), dtype, offset=self.off)
            self.off += (shape[1] * itm + 31) & ~31
            assert self.off <= SBUF_TOP, f"overlay overflow at {name}: {self.off}"
            b = Buf(name, self.carry)
            overlay_bufs.append(b)
            return t, b

    ps_bufs = []

    def psum_phase(names):
        carry = sch.barrier_tokens(ps_bufs)
        d = {n: Buf(n, carry) for n in names}
        ps_bufs.clear()
        ps_bufs.extend(d.values())
        return d

    CB = {}
    for name, sb, src in (("idn", sb_idn, identf), ("idnb", sb_idnb, identb),
                          ("pT", sb_pT, ropePT),
                          ("e8", sb_e8, e8c), ("o1b", sb_o1b, ones_col_b), ("osq", sb_osq, ones_sq_b),
                          ("o1f", sb_o1f, ones_col_f), ("orf", sb_orf, ones_row_f)):
        CB[name] = Buf(name)
        sch.op("sync", lambda e, sb=sb, src=src: e.dma_start(out=sb[:], in_=src.ap()),
               writes=[CB[name]], slot=SL["const"])
    CB["brr"] = Buf("brr")
    sch.op("sync", lambda e: e.dma_start(out=sb_brr[:], in_=br_in),
           writes=[CB["brr"]], slot=SL["const"])
    _const_tok = ("const", sch.cnt["const"])
    for _n in ("idn", "idnb", "pT", "e8", "o1b", "osq", "o1f", "orf", "brr"):
        CB[_n].wtok = _const_tok
    for name, sb, src in (("acat", sb_acat, acat), ("bcat", sb_bcat, bcat),
                          ("wrt", sb_wrt, wrt)):
        CB[name] = Buf(name)
        sch.op("gpsimd", lambda e, sb=sb, src=src: e.dma_start(out=sb[:], in_=src),
               writes=[CB[name]], slot=SL["ffnw"])
    _ffnw_tok = ("ffnw", sch.cnt["ffnw"])
    for _n in ("acat", "bcat", "wrt"):
        CB[_n].wtok = _ffnw_tok


    ps = psum_phase(["pA0", "pA1", "pB0", "pB1", "pT0", "pT1", "pav", "plB"])
    CB["brb"] = Buf("brb")
    sch.op("tensor", lambda e: e.matmul(out=psA[:, 0:NE], lhsT=sb_orf[:], rhs=sb_brr[:],
                                        start=True, stop=True),
           reads=[CB["orf"], CB["brr"]], writes=[ps["pA0"]])
    sch.op("scalar", lambda e: e.copy(out=sb_brb[:], in_=psA[:, 0:NE]),
           reads=[ps["pA0"]], writes=[CB["brb"]])
    B_qn = Buf("qn")
    sch.op("vector", lambda e: e.memset(sb_qn[:], 0.0), writes=[B_qn])
    B_bias = Buf("bias")

    DB = {n: Buf(n) for n in ("xo", "xp", "y1", "y2", "q", "k", "v", "ao", "out",
                              "wq", "wk", "wv", "wo", "cs", "masks")}
    for slot, srcw, dstw, nch in (("wcq", wq, wqb, D // W_OC),
                                  ("wck", wk, wkb, KVD // W_OC),
                                  ("wcv", wv, wvb, KVD // W_OC),
                                  ("wco", wo, wob, D // WO_OC)):
        dbn = "wqb wk wv wo".split()
        def f_cvt(e, sem, srcw=srcw, dstw=dstw, nch=nch):
            for j in range(nch):
                e.dma_start(out=dstw[j], in_=srcw[j]).then_inc(sem, 16)
        bname = {"wcq": "wqb", "wck": "wkb", "wcv": "wvb", "wco": "wob"}[slot]
        DB[bname] = Buf(bname)
        sch.op("gpsimd", f_cvt, reads=[DB[{"wcq": "wq", "wck": "wk", "wcv": "wv",
                                           "wco": "wo"}[slot]]],
               writes=[DB[bname]], slot=SL[slot], n_dma=nch, self_inc=True)

    ov = Overlay()
    y1fm, b_y1fm = ov.t("y1fm", [128, 32 * HALF], BF16)
    xT, b_xT = [], []
    for i in range(2):
        t, b = ov.t(f"xT{i}", [128, D], F32)
        xT.append(t); b_xT.append(b)
    y1t, b_y1t = ov.t("y1t", [128, D], F32)
    lnt, b_lnt = ov.t("lnt", [128, D], BF16)
    xfm, b_xfm = ov.t("xfm", [128, D], BF16)
    wbuf, b_wbuf = [], []
    for i in range(2):
        t, b = ov.t(f"wbuf{i}", [128, 32 * W_OC], BF16)
        wbuf.append(t); b_wbuf.append(b)
    cs_cos, b_csc = ov.t("cs_cos", [128, HALF], F32)
    cs_sin, b_css = ov.t("cs_sin", [128, HALF], F32)
    stg, b_stg = [], []
    for i in range(4):
        t, b = ov.t(f"stg{i}", [128, 512], BF16)
        stg.append(t); b_stg.append(b)
    t1f, b_t1 = [], []
    t2f, b_t2 = [], []
    for i in range(2):
        t, b = ov.t(f"t1f{i}", [128, 512], F32)
        t1f.append(t); b_t1.append(b)
        t, b = ov.t(f"t2f{i}", [128, 512], F32)
        t2f.append(t); b_t2.append(b)
    qsq, b_qsq = ov.t("qsq", [128, 512], F32)

    class FFNCtx:
        def __init__(self, ovl, tag):
            self.stats, self.b_stats = ovl.t(f"stats{tag}", [128, 48], F32)
            self.lnsc, self.b_lnsc = ovl.t(f"lnsc{tag}", [128, 8], F32)
            self.ffns, self.b_ffns = ovl.t(f"ffns{tag}", [128, 32], F32)
            self.scrT, self.b_scrT = ovl.t(f"scrT{tag}", [8, 128], F32)
            self.scb, self.b_scb = ovl.t(f"scb{tag}", [128, 128], F32)
            self.lows, self.b_lows = ovl.t(f"lows{tag}", [128, 128], BF16)
            self.lowf, self.b_lowf = ovl.t(f"lowf{tag}", [128, 128], F32)
            self.ffo, self.b_ffo = ovl.t(f"ffo{tag}", [128, 512], F32)

    fc = FFNCtx(ov, "A")

    def emit_ln(fx, src_buf, src_t, dst_buf, dst_t):
        def f_stats(e):
            for g in range(8):
                ii = e.bn_stats(out=fx.stats[:, g * 6:(g + 1) * 6],
                                in_=src_t[:, g * 512:(g + 1) * 512])
            return ii
        sch.op("vector", f_stats, reads=[src_buf], writes=[fx.b_stats])
        sch.op("vector", lambda e: e.bn_aggr(out=fx.lnsc[:, 0:2], in_=fx.stats[:]),
               reads=[fx.b_stats], writes=[fx.b_lnsc])
        sch.op("vector", lambda e: e.tensor_scalar(out=fx.lnsc[:, 2:3], in0=fx.lnsc[:, 1:2],
                                                   scalar1=LN_EPS, scalar2=None, op0=ALU.add),
               reads=[fx.b_lnsc], writes=[fx.b_lnsc])
        sch.op("scalar", lambda e: e.activation(out=fx.lnsc[:, 4:5], in_=fx.lnsc[:, 2:3],
                                                func=AF.Ln),
               reads=[fx.b_lnsc], writes=[fx.b_lnsc])
        sch.op("scalar", lambda e: e.activation(out=fx.lnsc[:, 3:4], in_=fx.lnsc[:, 4:5],
                                                func=AF.Exp, scale=-0.5),
               reads=[fx.b_lnsc], writes=[fx.b_lnsc])
        sch.op("vector", lambda e: e.tensor_scalar(out=dst_t[:], in0=src_t[:],
                                                   scalar1=fx.lnsc[:, 0:1],
                                                   scalar2=fx.lnsc[:, 3:4],
                                                   op0=ALU.subtract, op1=ALU.mult),
               reads=[src_buf, fx.b_lnsc], writes=[dst_buf])

    def emit_T(src_buf, src_t, dst_buf, dst_t, colf, slotname):
        for g in range(8):
            par = g % 2
            pb = ps[f"pT{par}"]
            for k4 in range(4):
                ft = g * 4 + k4
                pt = psT[:, par * 1024 + k4 * 128: par * 1024 + (k4 + 1) * 128]
                sch.op("tensor", lambda e, ft=ft, pt=pt: e.transpose(
                    out=pt, in_=src_t[:, ft * 128:(ft + 1) * 128], identity=sb_idnb[:]),
                    reads=[src_buf, CB["idnb"]], writes=[pb])
            ft0 = g * 4
            stride = colf(ft0 + 1) - colf(ft0)
            if stride == 128:
                dview = dst_t[:, colf(ft0):colf(ft0) + 512]
                sview = psT[:, par * 1024: par * 1024 + 512]
            else:
                dview = dst_t[:].rearrange("p (f c) -> p f c", c=stride)[
                    :, colf(ft0) // stride: colf(ft0) // stride + 4,
                    (colf(ft0) % stride):(colf(ft0) % stride) + 128]
                sview = psT[:, par * 1024: par * 1024 + 512].rearrange(
                    "p (f c) -> p f c", c=128)
            sch.op("scalar", lambda e, dview=dview, sview=sview: e.copy(
                out=dview, in_=sview),
                reads=[ps[f"pT{par}"]], writes=[dst_buf])

    def emit_ffn(fx, xbuf, x_t, outbuf, out_t, xf_t, b_xf):
        def f_rout(e):
            for ft in range(32):
                ii = e.matmul(out=psB[:, 512:512 + NE],
                              lhsT=xf_t[:, ft * 128:(ft + 1) * 128],
                              rhs=sb_wrt[:, ft * 8:(ft + 1) * 8],
                              start=(ft == 0), stop=(ft == 31))
            return ii
        sch.op("tensor", f_rout, reads=[b_xf, CB["wrt"]], writes=[ps["pB1"]])
        sch.op("scalar", lambda e: e.copy(out=fx.ffns[:, 24:32], in_=psB[:, 512:512 + NE]),
               reads=[ps["pB1"]], writes=[fx.b_ffns])
        sch.op("vector", lambda e: e.tensor_tensor(out=fx.ffns[:, 0:NE], in0=fx.ffns[:, 24:32],
                                                   in1=sb_brb[:], op=ALU.add),
               reads=[fx.b_ffns, CB["brb"]], writes=[fx.b_ffns])
        sch.op("vector", lambda e: e.reduce_max(out=fx.ffns[:, 16:17], in_=fx.ffns[:, 0:NE],
                                                axis=AX.X, negate=True),
               reads=[fx.b_ffns], writes=[fx.b_ffns])
        sch.op("scalar", lambda e: e.activation(out=fx.ffns[:, 8:16], in_=fx.ffns[:, 0:NE],
                                                func=AF.Exp, bias=fx.ffns[:, 16:17],
                                                accum_out=fx.ffns[:, 17:18]),
               reads=[fx.b_ffns], writes=[fx.b_ffns])
        sch.op("vector", lambda e: e.reciprocal(out=fx.ffns[:, 18:19], in_=fx.ffns[:, 17:18]),
               reads=[fx.b_ffns], writes=[fx.b_ffns])
        sch.op("vector", lambda e: e.tensor_scalar(out=fx.ffns[:, 0:NE], in0=fx.ffns[:, 8:16],
                                                   scalar1=fx.ffns[:, 18:19], scalar2=8.0,
                                                   op0=ALU.mult, op1=ALU.mult),
               reads=[fx.b_ffns], writes=[fx.b_ffns])
        sch.op("tensor", lambda e: e.transpose(out=psB[0:8, 512:640], in_=fx.ffns[:, 0:NE],
                                               identity=sb_idn[:]),
               reads=[fx.b_ffns, CB["idn"]], writes=[ps["pB1"]])
        sch.op("scalar", lambda e: e.copy(out=fx.scrT[:], in_=psB[0:8, 512:640]),
               reads=[ps["pB1"]], writes=[fx.b_scrT])
        sch.op("tensor", lambda e: e.matmul(out=psB[:, 640:768], lhsT=sb_e8[:], rhs=fx.scrT[:],
                                            start=True, stop=True),
               reads=[CB["e8"], fx.b_scrT], writes=[ps["pB1"]])
        sch.op("scalar", lambda e: e.copy(out=fx.scb[:], in_=psB[:, 640:768]),
               reads=[ps["pB1"]], writes=[fx.b_scb])
        def f_low(e):
            for ft in range(32):
                ii = e.matmul(out=psB[:, 768:896],
                              lhsT=sb_acat[:, ft * 128:(ft + 1) * 128],
                              rhs=xf_t[:, ft * 128:(ft + 1) * 128],
                              start=(ft == 0), stop=(ft == 31))
            return ii
        sch.op("tensor", f_low, reads=[b_xf, CB["acat"]], writes=[ps["pB1"]])
        sch.op("scalar", lambda e: e.copy(out=fx.lowf[:], in_=psB[:, 768:896]),
               reads=[ps["pB1"]], writes=[fx.b_lowf])
        sch.op("vector", lambda e: e.tensor_tensor(out=fx.lows[:], in0=fx.lowf[:],
                                                   in1=fx.scb[:], op=ALU.mult),
               reads=[fx.b_lowf, fx.b_scb], writes=[fx.b_lows])
        for oc in range(8):
            pbuf = ps["pB0"]
            pt = psB[:, 0:512]
            sch.op("tensor", lambda e, oc=oc, pt=pt: e.matmul(
                out=pt, lhsT=fx.lows[:], rhs=sb_bcat[:, oc * 512:(oc + 1) * 512],
                start=True, stop=True),
                reads=[fx.b_lows, CB["bcat"]], writes=[pbuf])
            sch.op("scalar", lambda e, pt=pt: e.copy(out=fx.ffo[:], in_=pt),
                   reads=[pbuf], writes=[fx.b_ffo])
            sch.op("vector", lambda e, oc=oc: e.tensor_tensor(
                out=out_t[:, oc * 512:(oc + 1) * 512], in0=fx.ffo[:],
                in1=x_t[:, oc * 512:(oc + 1) * 512], op=ALU.add),
                reads=[fx.b_ffo, xbuf], writes=[outbuf])

    def phase_A(x_dram, b_xdram, half, own):
        for ti in range(NT_HALF):
            xs = ti % 2
            row0 = half * HALF + ti * 128
            sch.op("sync", lambda e, row0=row0, xs=xs: e.dma_start(
                out=xT[xs][:], in_=x_dram[row0:row0 + 128, :]),
                reads=[b_xdram], writes=[b_xT[xs]], slot=SL[f"x{xs}"])
            emit_ln(fc, b_xT[xs], xT[xs], b_lnt, lnt)
            emit_T(b_lnt, lnt, b_xfm, xfm, lambda ft: ft * 128, "lnT0")
            emit_ffn(fc, b_xT[xs], xT[xs], b_y1t, y1t, xfm, b_xfm)
            if own:
                sch.op("scalar", lambda e, row0=row0: e.dma_start(
                    out=y1_d[row0:row0 + 128, :], in_=y1t[:]),
                    reads=[b_y1t], writes=[DB["y1"]], slot=SL["st_y1"])
            emit_ln(fc, b_y1t, y1t, b_lnt, lnt)
            emit_T(b_lnt, lnt, b_y1fm, y1fm, lambda ft, ti=ti: ft * HALF + ti * 128, "lnT1")

    def rope_store(pname, o_global, slot_col, tc, pbuf, pt, half, sslot):
        st = b_stg[sslot]
        sch.op("scalar", lambda e: e.copy(out=stg[sslot][:], in_=pt),
               reads=[pbuf], writes=[st])
        if B_ROPE:
            csl = cs_cos[:, tc * 512:(tc + 1) * 512]
            ssl = cs_sin[:, tc * 512:(tc + 1) * 512]
            rsl = 2 + sslot
            sch.op("tensor", lambda e: e.matmul(out=pav[:], lhsT=sb_pT[:], rhs=stg[sslot][:],
                                                start=True, stop=True),
                   reads=[CB["pT"], st], writes=[ps["pav"]])
            sch.op("scalar", lambda e, rsl=rsl: e.copy(out=stg[rsl][:], in_=pav[:]),
                   reads=[ps["pav"]], writes=[b_stg[rsl]])
            sch.op("vector", lambda e, c=csl: e.tensor_tensor(out=t1f[sslot][:],
                                                              in0=stg[sslot][:],
                                                              in1=c, op=ALU.mult),
                   reads=[st, b_csc], writes=[b_t1[sslot]])
            sch.op("vector", lambda e, s=ssl, rsl=rsl: e.tensor_tensor(out=t2f[sslot][:],
                                                                       in0=stg[rsl][:],
                                                                       in1=s, op=ALU.mult),
                   reads=[b_stg[rsl], b_css], writes=[b_t2[sslot]])
            sch.op("vector", lambda e: e.tensor_tensor(out=stg[sslot][:], in0=t1f[sslot][:],
                                                       in1=t2f[sslot][:], op=ALU.add),
                   reads=[b_t1[sslot], b_t2[sslot]], writes=[st])
        if pname == "q":
            col0 = half * HALF + tc * 512
            sch.op("scalar", lambda e, o=o_global, c=col0: e.dma_start(
                out=q_d[o:o + 128, c:c + 512], in_=stg[sslot][:]),
                reads=[st], writes=[DB["q"]], slot=SL[f"st_q{sslot}"])
            h = o_global // 128
            if not B_QNORM:
                return
            sch.op("vector", lambda e: e.tensor_mul(out=qsq[:], in0=stg[sslot][:],
                                                    in1=stg[sslot][:]),
                   reads=[st], writes=[b_qsq])
            sch.op("tensor", lambda e: e.matmul(out=plB[0:1, :], lhsT=sb_o1f[:], rhs=qsq[:],
                                                start=True, stop=True),
                   reads=[CB["o1f"], b_qsq], writes=[ps["plB"]])
            sch.op("scalar", lambda e: e.copy(out=qsq[0:1, :], in_=plB[0:1, :]),
                   reads=[ps["plB"]], writes=[b_qsq])
            sch.op("vector", lambda e: e.reduce_max(out=sb_qn[0:1, 40:41], in_=qsq[0:1, :],
                                                    axis=AX.X),
                   reads=[b_qsq], writes=[B_qn])
            sch.op("vector", lambda e, h=h: e.tensor_tensor(out=sb_qn[0:1, h:h + 1],
                                                            in0=sb_qn[0:1, h:h + 1],
                                                            in1=sb_qn[0:1, 40:41], op=ALU.max),
                   reads=[B_qn], writes=[B_qn])
        else:
            c0 = slot_col * 512
            sch.op("scalar", lambda e, o=o_global, c=c0: e.dma_start(
                out=k_d[o:o + 128, c:c + 512], in_=stg[sslot][:]),
                reads=[st], writes=[DB["k"]], slot=SL[f"st_k{sslot}"])

    def phase_B(half, own):
        slot_base = (0 if own else 4) + 2 * half
        lo = half * HALF
        key = 0 if own else 1
        sch.op("sync", lambda e: e.dma_start(out=cs_cos[:], in_=cs_in[(key, "c")][:, lo:lo + HALF]),
               writes=[b_csc], slot=SL["cs"])
        sch.op("sync", lambda e: e.dma_start(out=cs_sin[:], in_=cs_in[(key, "s")][:, lo:lo + HALF]),
               writes=[b_css], slot=SL["cs"])
        b_csc.wtok = b_css.wtok = ("cs", sch.cnt["cs"])

        projs = ([("q", wqb, DB["wqb"], D // W_OC)] if own and "q" in B_PROJS else [])
        if "k" in B_PROJS:
            projs.append(("k", wkb, DB["wkb"], KVD // W_OC))
        if "v" in B_PROJS:
            projs.append(("v", wvb, DB["wvb"], KVD // W_OC))
        it = 0
        for pname, wsrc, wdb, nchunk in projs:
            for j in range(nchunk):
                ws = it % 2
                it += 1
                sch.op("sync", lambda e, j=j, ws=ws, wsrc=wsrc: e.dma_start(
                    out=wbuf[ws][:], in_=wsrc[j]),
                    reads=[wdb], writes=[b_wbuf[ws]], slot=SL[f"w{ws}"])
                if pname == "v":
                    for tt in range(NT_HALF):
                        sslot = tt % 2
                        pbuf = ps["pA0"] if sslot == 0 else ps["pA1"]
                        pt = psA[:, 0:512] if sslot == 0 else psA[:, 512:1024]
                        def f_vm(e, tt=tt, pt=pt, ws=ws):
                            for ft in range(32):
                                ii = e.matmul(
                                    out=pt[:, 0:W_OC],
                                    lhsT=y1fm[:, ft * HALF + tt * 128: ft * HALF + (tt + 1) * 128],
                                    rhs=wbuf[ws][:, ft * W_OC:(ft + 1) * W_OC],
                                    start=(ft == 0), stop=(ft == 31))
                            return ii
                        sch.op("tensor", f_vm, reads=[b_wbuf[ws], b_y1fm], writes=[pbuf])
                        st = b_stg[2 + sslot]
                        sch.op("scalar", lambda e, pt=pt, sslot=sslot: e.copy(
                            out=stg[2 + sslot][:, 0:W_OC], in_=pt[:, 0:W_OC]),
                            reads=[pbuf], writes=[st])
                        grow = (slot_base + tt // 4) * 512 + (tt % 4) * 128
                        sch.op("scalar", lambda e, g=grow, j=j, sslot=sslot: e.dma_start(
                            out=v_d[g:g + 128, j * W_OC:(j + 1) * W_OC],
                            in_=stg[2 + sslot][:, 0:W_OC]),
                            reads=[st], writes=[DB["v"]], slot=SL[f"st_v{sslot}"])
                else:
                    for o1 in range(W_OC // 128):
                        o_global = j * W_OC + o1 * 128
                        for tc in range(2):
                            sslot = (o1 * 2 + tc) % 2
                            pbuf = ps["pA0"] if sslot == 0 else ps["pA1"]
                            pt = psA[:, 0:512] if sslot == 0 else psA[:, 512:1024]
                            def f_mm(e, o1=o1, tc=tc, pt=pt, ws=ws):
                                for ft in range(32):
                                    ii = e.matmul(
                                        out=pt,
                                        lhsT=wbuf[ws][:, ft * W_OC + o1 * 128: ft * W_OC + (o1 + 1) * 128],
                                        rhs=y1fm[:, ft * HALF + tc * 512: ft * HALF + (tc + 1) * 512],
                                        start=(ft == 0), stop=(ft == 31))
                                return ii
                            sch.op("tensor", f_mm, reads=[b_wbuf[ws], b_y1fm], writes=[pbuf])
                            rope_store(pname, o_global, slot_base + tc, tc, pbuf, pt, half, sslot)

    last_toks = []
    for half in range(2):
        phase_A(x_own, DB["xo"], half, own=True)
        if "B" in phases:
            phase_B(half, own=True)
    if "B" in phases:
        for half in range(2):
            phase_A(x_part, DB["xp"], half, own=False)
            phase_B(half, own=False)
    if "C" not in phases:
        for tt in range(OWN // 128):
            tk = sch.op("sync", lambda e, tt=tt: e.dma_start(
                out=out_hbm[tt * 128:(tt + 1) * 128, :],
                in_=y1_d[tt * 128:(tt + 1) * 128, :]),
                reads=[DB["y1"]], writes=[DB["out"]], slot=SL["st_out"])
            last_toks.append(tk)
        if "B" in phases:
            last_toks.append(DB["q"].wtok)
            last_toks.append(DB["k"].wtok)
            last_toks.append(DB["v"].wtok)
        sch.finish(last_toks)
        return _finish_block(nc, sch)

    ovc = Overlay()
    K_sb, b_K = ovc.t("K_sb", [128, NKV * S], BF16)
    V_sb, b_V = ovc.t("V_sb", [128, (S // 128) * KVD], BF16)
    qtl, b_qtl = [], []
    for i in range(2):
        t, b = ovc.t(f"qtl{i}", [128, 512], BF16)
        qtl.append(t); b_qtl.append(b)
    exb, b_exb = [], []
    for i in range(2):
        t, b = ovc.t(f"exb{i}", [128, 1536], BF16)
        exb.append(t); b_exb.append(b)
    maskb, b_maskb = ovc.t("maskb", [128, 4096], BF16)
    rlb, b_rlb = ovc.t("rlb", [128, 512], F32)
    rlr, b_rlr = ovc.t("rlr", [1, 512], F32)
    aost, b_aost = [], []
    for i in range(2):
        t, b = ovc.t(f"aost{i}", [128, 512], BF16)
        aost.append(t); b_aost.append(b)
    ksq, b_ksq = ovc.t("ksq", [128, 4096], F32)
    avf, b_avf = ovc.t("avf", [128, 512], F32)

    psc = psum_phase(["pA0", "pA1", "pB0", "pB1", "pT0", "pT1", "pav", "plB"])

    def f_kld(e, sem):
        for kv in range(NKV):
            e.dma_start(out=K_sb[:, kv * S:(kv + 1) * S],
                        in_=k_d[kv * 128:(kv + 1) * 128, :].rearrange("d pos -> d pos")
                        ).then_inc(sem, 16)
    sch.op("sync", f_kld, reads=[DB["k"]], writes=[b_K], slot=SL["kvld"],
           n_dma=NKV, self_inc=True)

    def f_vld(e, sem):
        for g in range(4):
            e.dma_start(
                out=V_sb[:, g * 8 * KVD:(g + 1) * 8 * KVD].rearrange(
                    "p (pt c) -> p pt c", pt=8),
                in_=v_d[g * 1024:(g + 1) * 1024, :].rearrange(
                    "(pt p) c -> p pt c", p=128)).then_inc(sem, 16)
    sch.op("sync", f_vld, reads=[DB["v"]], writes=[b_V], slot=SL["kvld"],
           n_dma=4, self_inc=True)
    b_K.wtok = b_V.wtok = ("kvld", sch.cnt["kvld"])

    for kv in range(NKV):
        sch.op("vector", lambda e, kv=kv: e.tensor_mul(
            out=ksq[:], in0=K_sb[:, kv * S:(kv + 1) * S], in1=K_sb[:, kv * S:(kv + 1) * S]),
            reads=[b_K], writes=[b_ksq])
        for c8 in range(8):
            sch.op("tensor", lambda e, c8=c8: e.matmul(
                out=plB[0:1, :], lhsT=sb_o1f[:], rhs=ksq[:, c8 * 512:(c8 + 1) * 512],
                start=True, stop=True),
                reads=[CB["o1f"], b_ksq], writes=[psc["plB"]])
            sch.op("scalar", lambda e: e.copy(out=rlr[:], in_=plB[0:1, :]),
                   reads=[psc["plB"]], writes=[b_rlr])
            sch.op("vector", lambda e: e.reduce_max(out=sb_qn[0:1, 40:41], in_=rlr[:],
                                                    axis=AX.X),
                   reads=[b_rlr], writes=[B_qn])
            sch.op("vector", lambda e, kv=kv: e.tensor_tensor(
                out=sb_qn[0:1, 32 + kv:33 + kv], in0=sb_qn[0:1, 32 + kv:33 + kv],
                in1=sb_qn[0:1, 40:41], op=ALU.max),
                reads=[B_qn], writes=[B_qn])

    for h in range(NH):
        kv = h // 4
        sch.op("vector", lambda e, h=h, kv=kv: e.tensor_tensor(
            out=sb_qn[0:1, 41:42], in0=sb_qn[0:1, h:h + 1],
            in1=sb_qn[0:1, 32 + kv:33 + kv], op=ALU.mult),
            reads=[B_qn], writes=[B_qn])
        sch.op("scalar", lambda e: e.activation(out=sb_qn[0:1, 43:44], in_=sb_qn[0:1, 41:42],
                                                func=AF.Ln),
               reads=[B_qn], writes=[B_qn])
        sch.op("scalar", lambda e: e.activation(out=sb_qn[0:1, 42:43], in_=sb_qn[0:1, 43:44],
                                                func=AF.Exp, scale=0.5),
               reads=[B_qn], writes=[B_qn])
        sch.op("vector", lambda e, h=h: e.tensor_scalar(
            out=sb_qn[0:1, 64 + h:65 + h], in0=sb_qn[0:1, 42:43],
            scalar1=-RSCALE, scalar2=None, op0=ALU.mult),
            reads=[B_qn], writes=[B_qn])
    sch.op("tensor", lambda e: e.matmul(out=pav[:, 0:NH], lhsT=sb_orf[:],
                                        rhs=sb_qn[0:1, 64:64 + NH], start=True, stop=True),
           reads=[CB["orf"], B_qn], writes=[psc["pav"]])
    sch.op("scalar", lambda e: e.copy(out=sb_bias[:], in_=pav[:, 0:NH]),
           reads=[psc["pav"]], writes=[B_bias])

    for p in range(NQC):
        sch.op("sync", lambda e, p=p: e.dma_start(out=maskb[:], in_=masks_in[p]),
               reads=[DB["masks"]], writes=[b_maskb], slot=SL["mld"])
        tiles = [(sl, j0) for sl in SLOTS_P[p] for j0 in range(4)]
        nk = len(tiles)
        for h in range(NH):
            kv = h // 4
            qs = h % 2
            sch.op("sync", lambda e, p=p, h=h, qs=qs: e.dma_start(
                out=qtl[qs][:], in_=q_d[h * 128:(h + 1) * 128, p * 512:(p + 1) * 512]),
                reads=[DB["q"]], writes=[b_qtl[qs]], slot=SL[f"qld{qs}"])

            groups = [tiles[i:i + 2] for i in range(0, nk, 2)]

            def emit_scores(gi):
                grp = groups[gi]
                pb = [psc["pA0"], psc["pA1"]] if gi % 2 == 0 else [psc["pB0"], psc["pB1"]]
                pt = psA if gi % 2 == 0 else psB
                def f(e, grp=grp, pt=pt, qs=qs, kv=kv):
                    for i, (sl, j0) in enumerate(grp):
                        pos = sl * 4 + j0
                        ii = e.matmul(out=pt[:, i * 512:(i + 1) * 512],
                                      lhsT=K_sb[:, kv * S + pos * 128: kv * S + (pos + 1) * 128],
                                      rhs=qtl[qs][:], start=True, stop=True,
                                      skip_group_check=True)
                    return ii
                sch.op("tensor", f, reads=[b_K, b_qtl[qs]], writes=pb)

            def emit_rest(gi, first_tile_idx):
                grp = groups[gi]
                pb = [psc["pA0"], psc["pA1"]] if gi % 2 == 0 else [psc["pB0"], psc["pB1"]]
                pt = psA if gi % 2 == 0 else psB
                eb = b_exb[gi % 2]
                et = exb[gi % 2]
                n = len(grp)
                sch.op("scalar", lambda e, pt=pt, et=et, n=n, h=h: e.activation(
                    out=et[:, 0:n * 512], in_=pt[:, 0:n * 512], func=AF.Exp,
                    bias=sb_bias[:, h:h + 1], scale=RSCALE),
                    reads=pb + [B_bias], writes=[eb])
                for i, (sl, j0) in enumerate(grp):
                    if sl == p or sl == p + 4:
                        mi = 0 if sl == p else 1
                        mc = mi * 2048 + j0 * 512
                        sch.op("vector", lambda e, i=i, mc=mc, et=et: e.tensor_tensor(
                            out=et[:, i * 512:(i + 1) * 512], in0=et[:, i * 512:(i + 1) * 512],
                            in1=maskb[:, mc:mc + 512], op=ALU.mult),
                            reads=[eb, b_maskb], writes=[eb])
                for i, (sl, j0) in enumerate(grp):
                    ti = first_tile_idx + i
                    pos = sl * 4 + j0
                    sch.op("tensor", lambda e, i=i, pos=pos, et=et, kv=kv, ti=ti: e.matmul(
                        out=pav[:], lhsT=V_sb[:, pos * KVD + kv * 128: pos * KVD + (kv + 1) * 128],
                        rhs=et[:, i * 512:(i + 1) * 512],
                        start=(ti == 0), stop=(ti == nk - 1), skip_group_check=True),
                        reads=[eb, b_V], writes=[psc["pav"]])
                    sch.op("tensor", lambda e, i=i, et=et, ti=ti: e.matmul(
                        out=plB[:], lhsT=sb_osq[:], rhs=et[:, i * 512:(i + 1) * 512],
                        start=(ti == 0), stop=(ti == nk - 1), skip_group_check=True),
                        reads=[eb, CB["osq"]], writes=[psc["plB"]])

            emit_scores(0)
            ti0 = 0
            for gi in range(1, len(groups)):
                emit_scores(gi)
                emit_rest(gi - 1, ti0)
                ti0 += len(groups[gi - 1])
            emit_rest(len(groups) - 1, ti0)

            sch.op("scalar", lambda e: e.activation(out=rlb[:], in_=plB[:], func=AF.Ln),
                   reads=[psc["plB"]], writes=[b_rlb])
            sch.op("scalar", lambda e: e.activation(out=rlb[:], in_=rlb[:], func=AF.Exp,
                                                    scale=-1.0),
                   reads=[b_rlb], writes=[b_rlb])
            ao_s = h % 2
            sch.op("scalar", lambda e: e.copy(out=avf[:], in_=pav[:]),
                   reads=[psc["pav"]], writes=[b_avf])
            sch.op("vector", lambda e, ao_s=ao_s: e.tensor_tensor(
                out=aost[ao_s][:], in0=avf[:], in1=rlb[:], op=ALU.mult),
                reads=[b_avf, b_rlb], writes=[b_aost[ao_s]])
            sch.op("scalar", lambda e, p=p, h=h, ao_s=ao_s: e.dma_start(
                out=ao_d[h * 128:(h + 1) * 128, p * 512:(p + 1) * 512], in_=aost[ao_s][:]),
                reads=[b_aost[ao_s]], writes=[DB["ao"]], slot=SL[f"st_ao{ao_s}"])

    if "D" not in phases:
        for tt in range(OWN // 128):
            tk = sch.op("sync", lambda e, tt=tt: e.dma_start(
                out=out_hbm[tt * 128:(tt + 1) * 128, :],
                in_=y1_d[tt * 128:(tt + 1) * 128, :]),
                reads=[DB["y1"]], writes=[DB["out"]], slot=SL["st_out"])
        sch.finish([tk, DB["ao"].wtok])
        return _finish_block(nc, sch)

    ovd = Overlay()
    attn_sb, b_attn = ovd.t("attn_sb", [128, 32 * OWN], BF16)
    wobuf, b_wob = [], []
    for i in range(2):
        t, b = ovd.t(f"wob{i}", [128, 32 * WO_OC], BF16)
        wobuf.append(t); b_wob.append(b)
    y1s, b_y1s = [], []
    for i in range(2):
        t, b = ovd.t(f"y1s{i}", [128, WO_OC], F32)
        y1s.append(t); b_y1s.append(b)
    y2s, b_y2s = [], []
    for i in range(2):
        t, b = ovd.t(f"y2s{i}", [128, WO_OC], F32)
        y2s.append(t); b_y2s.append(b)
    y2f, b_y2f = [], []
    for i in range(2):
        t, b = ovd.t(f"y2f{i}", [128, WO_OC], F32)
        y2f.append(t); b_y2f.append(b)

    psd = psum_phase(["pA0", "pA1", "pB0", "pB1", "pT0", "pT1", "pav", "plB"])

    def f_ald(e, sem):
        for g in range(4):
            e.dma_start(
                out=attn_sb[:, g * 8 * OWN:(g + 1) * 8 * OWN].rearrange(
                    "p (hc q) -> p hc q", hc=8),
                in_=ao_d[g * 1024:(g + 1) * 1024, :].rearrange(
                    "(hc p) q -> p hc q", p=128)).then_inc(sem, 16)
    sch.op("sync", f_ald, reads=[DB["ao"]], writes=[b_attn], slot=SL["attnld"],
           n_dma=4, self_inc=True)

    for oc in range(D // WO_OC):
        ws = oc % 2
        sch.op("sync", lambda e, oc=oc, ws=ws: e.dma_start(out=wobuf[ws][:], in_=wob[oc]),
               reads=[DB["wob"]], writes=[b_wob[ws]], slot=SL[f"wo{ws}"])
        for tt in range(OWN // 128):
            sl2 = tt % 2
            pbuf = psd["pA0"] if sl2 == 0 else psd["pA1"]
            pt = psA[:, 0:WO_OC] if sl2 == 0 else psA[:, 512:512 + WO_OC]
            def f_wo(e, tt=tt, ws=ws, pt=pt):
                for hc in range(32):
                    ii = e.matmul(out=pt,
                                  lhsT=attn_sb[:, hc * OWN + tt * 128: hc * OWN + (tt + 1) * 128],
                                  rhs=wobuf[ws][:, hc * WO_OC:(hc + 1) * WO_OC],
                                  start=(hc == 0), stop=(hc == 31))
                return ii
            sch.op("tensor", f_wo, reads=[b_wob[ws], b_attn], writes=[pbuf])
            sch.op("sync", lambda e, tt=tt, oc=oc, sl2=sl2: e.dma_start(
                out=y1s[sl2][:], in_=y1_d[tt * 128:(tt + 1) * 128, oc * WO_OC:(oc + 1) * WO_OC]),
                reads=[DB["y1"]], writes=[b_y1s[sl2]], slot=SL[f"y1s{sl2}"])
            sch.op("scalar", lambda e, sl2=sl2, pt=pt: e.copy(out=y2f[sl2][:], in_=pt),
                   reads=[pbuf], writes=[b_y2f[sl2]])
            sch.op("vector", lambda e, sl2=sl2: e.tensor_tensor(
                out=y2s[sl2][:], in0=y2f[sl2][:], in1=y1s[sl2][:], op=ALU.add),
                reads=[b_y2f[sl2], b_y1s[sl2]], writes=[b_y2s[sl2]])
            sch.op("scalar", lambda e, tt=tt, oc=oc, sl2=sl2: e.dma_start(
                out=y2_d[tt * 128:(tt + 1) * 128, oc * WO_OC:(oc + 1) * WO_OC],
                in_=y2s[sl2][:]),
                reads=[b_y2s[sl2]], writes=[DB["y2"]], slot=SL[f"st_y2{sl2}"])

    if "E" not in phases:
        for tt in range(OWN // 128):
            tk = sch.op("sync", lambda e, tt=tt: e.dma_start(
                out=out_hbm[tt * 128:(tt + 1) * 128, :],
                in_=y2_d[tt * 128:(tt + 1) * 128, :]),
                reads=[DB["y2"]], writes=[DB["out"]], slot=SL["st_out"])
        sch.finish([tk])
        return _finish_block(nc, sch)

    ove = Overlay()
    y2T, b_y2T = [], []
    for i in range(2):
        t, b = ove.t(f"y2T{i}", [128, D], F32)
        y2T.append(t); b_y2T.append(b)
    y3t, b_y3t = ove.t("y3t", [128, D], F32)
    lntE, b_lntE = ove.t("lntE", [128, D], BF16)
    xfmE, b_xfmE = ove.t("xfmE", [128, D], BF16)
    fcE = FFNCtx(ove, "E")

    pse = psum_phase(["pA0", "pA1", "pB0", "pB1", "pT0", "pT1", "pav", "plB"])
    ps.update(pse)

    out_toks = []

    def phase_E():
        for tt in range(OWN // 128):
            ysl = tt % 2
            row0 = tt * 128
            sch.op("sync", lambda e, row0=row0, ysl=ysl: e.dma_start(
                out=y2T[ysl][:], in_=y2_d[row0:row0 + 128, :]),
                reads=[DB["y2"]], writes=[b_y2T[ysl]], slot=SL[f"y2l{ysl}"])
            emit_ln(fcE, b_y2T[ysl], y2T[ysl], b_lntE, lntE)
            emit_T(b_lntE, lntE, b_xfmE, xfmE, lambda ft: ft * 128, "lnT0")
            emit_ffn(fcE, b_y2T[ysl], y2T[ysl], b_y3t, y3t, xfmE, b_xfmE)
            tk = sch.op("sync", lambda e, row0=row0: e.dma_start(
                out=out_hbm[row0:row0 + 128, :], in_=y3t[:]),
                reads=[b_y3t], writes=[DB["out"]], slot=SL["st_out"])
            out_toks.append(tk)

    phase_E()
    sch.finish(out_toks[-1:])
    return _finish_block(nc, sch)


def _finish_block(nc, sch):
    with nc.Block() as block:
        @block.sync
        def _(e):
            for f in sch.ops["sync"]:
                f(e)

        @block.tensor
        def _(e):
            for f in sch.ops["tensor"]:
                f(e)

        @block.vector
        def _(e):
            for f in sch.ops["vector"]:
                f(e)

        @block.scalar
        def _(e):
            for f in sch.ops["scalar"]:
                f(e)

        @block.gpsimd
        def _(e):
            for f in sch.ops["gpsimd"]:
                f(e)

    return nc



def _own_rows(hg):
    qcs = OWN_QCS[hg]
    return np.concatenate([np.arange(g * 512, (g + 1) * 512) for g in qcs])


def make_in_maps(x, Wq, Wk, Wv, Wo, A, Bm, Wr, br):
    wq_t = _tile_lhsT(np.ascontiguousarray(Wq.T), W_OC)
    wk_t = _tile_lhsT(np.ascontiguousarray(Wk.T), W_OC)
    wv_t = _tile_lhsT(np.ascontiguousarray(Wv.T), W_OC)
    wo_t = _tile_lhsT(np.ascontiguousarray(Wo.T), WO_OC)
    acat = np.ascontiguousarray(
        _tile_lhsT(np.ascontiguousarray(A.transpose(2, 0, 1).reshape(D, 128)), 128)[0])
    bcat = np.ascontiguousarray(Bm.transpose(0, 2, 1).reshape(128, D))
    wrt = np.ascontiguousarray(
        _tile_lhsT(np.ascontiguousarray(Wr.T), 8)[0])
    brr = np.ascontiguousarray(br.reshape(1, NE))

    in_maps = []
    for c in range(8):
        b, hg = c // 2, c % 2
        rows_o = _own_rows(hg)
        rows_p = _own_rows(1 - hg)
        cos_o, sin_o = _rope_tables(rows_o)
        cos_p, sin_p = _rope_tables(rows_p)
        in_maps.append({
            "x_own": np.ascontiguousarray(x[b][rows_o]),
            "x_part": np.ascontiguousarray(x[b][rows_p]),
            "wq": wq_t, "wk": wk_t, "wv": wv_t, "wo": wo_t,
            "acat": acat, "bcat": bcat, "wrt": wrt, "br": brr,
            "cos_o": cos_o, "sin_o": sin_o, "cos_p": cos_p, "sin_p": sin_p,
            "masks": _masks_for(hg),
        })
    return in_maps


def kernel(L0_attn_hidden_states, L0_attn_Wq, L0_attn_Wk, L0_attn_Wv, L0_attn_Wo,
           L0_ffn_A, L0_ffn_B, L0_ffn_Wr, L0_ffn_br):
    x = np.asarray(L0_attn_hidden_states, dtype=np.float32)
    in_maps = make_in_maps(x,
                           np.asarray(L0_attn_Wq, np.float32),
                           np.asarray(L0_attn_Wk, np.float32),
                           np.asarray(L0_attn_Wv, np.float32),
                           np.asarray(L0_attn_Wo, np.float32),
                           np.asarray(L0_ffn_A, np.float32),
                           np.asarray(L0_ffn_B, np.float32),
                           np.asarray(L0_ffn_Wr, np.float32),
                           np.asarray(L0_ffn_br, np.float32))
    nc = build_program()
    res = run_bass_kernel_spmd(nc, in_maps, core_ids=list(range(8)))
    out = np.empty((NB, S, D), dtype=np.float32)
    for c in range(8):
        b, hg = c // 2, c % 2
        out[b][_own_rows(hg)] = res.results[c]["out"]
    return out


# version 05
# speedup vs baseline: 1.1532x; 1.0008x over previous
import math

import numpy as np
import ml_dtypes

import concourse.bass as bass
import concourse.mybir as mybir
from concourse.bass_utils import run_bass_kernel_spmd

F32 = mybir.dt.float32
BF16 = mybir.dt.bfloat16
AF = mybir.ActivationFunctionType
ALU = mybir.AluOpType
AX = mybir.AxisListType

D = 4096
S = 4096
NB = 4
NH = 32
NKV = 8
HD = 128
KVD = NKV * HD
NE = 8
OWN = 2048
HALF = 1024
NQC = 4
NT_HALF = 8
RSCALE = float(1.0 / math.sqrt(HD))
LN_EPS = 1e-5

OWN_QCS = ([0, 3, 4, 7], [1, 2, 5, 6])
NK_UNIF = [8, 16, 24, 32]
SLOTS_P = [[0, 4], [0, 1, 4, 5], [0, 1, 2, 4, 5, 6], [0, 1, 2, 3, 4, 5, 6, 7]]

W_OC = 256
WO_OC = 256

B_PROJS = ("q", "k", "v")
B_ROPE = True
B_QNORM = True
A_TRANSPOSE = True
SBUF_TOP = 229344



def _tile_lhsT(wt, oc):
    f, o = wt.shape
    return np.ascontiguousarray(
        wt.reshape(f // 128, 128, o // oc, oc).transpose(2, 1, 0, 3).reshape(o // oc, 128, -1)
    )


def _rope_tables(positions):
    pos = np.asarray(positions, dtype=np.float64)[None, :]
    half = np.arange(64, dtype=np.float64)
    freqs = np.exp(-2.0 * half * (math.log(10000.0) / HD))[:, None]
    ang = pos * freqs
    cos = np.concatenate([np.cos(ang), np.cos(ang)], axis=0).astype(np.float32)
    sin = np.concatenate([np.sin(ang), np.sin(ang)], axis=0).astype(np.float32)
    return np.ascontiguousarray(cos), np.ascontiguousarray(sin)


def _masks_for(hg):
    own = OWN_QCS[hg]
    part = OWN_QCS[1 - hg]
    g_order = own + part
    out = np.zeros((NQC, 128, 4096), dtype=np.float32)
    kp = np.arange(128)[:, None]
    qf = np.arange(512)[None, :]
    for p in range(NQC):
        gq = own[p]
        for mi, slot in enumerate((p, p + 4)):
            g = g_order[slot]
            for j0 in range(4):
                if g < gq:
                    m = np.ones((128, 512), np.float32)
                elif g > gq:
                    m = np.zeros((128, 512), np.float32)
                else:
                    m = ((kp + j0 * 128) <= qf).astype(np.float32)
                out[p, :, mi * 2048 + j0 * 512: mi * 2048 + (j0 + 1) * 512] = m
    return np.ascontiguousarray(out.astype(ml_dtypes.bfloat16))


def _rope_perm_T():
    p = np.zeros((128, 128), np.float32)
    for dd in range(64):
        p[dd, dd + 64] = -1.0
        p[dd + 64, dd] = 1.0
    return np.ascontiguousarray(p.T.astype(ml_dtypes.bfloat16))



class Buf:
    __slots__ = ("name", "wtok", "rtoks")

    def __init__(self, name, carry=()):
        self.name = name
        self.wtok = None
        self.rtoks = list(carry)


class Sched:
    ENGINES = ("tensor", "vector", "scalar", "gpsimd", "sync")

    def __init__(self, nc):
        self.nc = nc
        self.ops = {e: [] for e in self.ENGINES}
        self.sem = {}
        self.cnt = {}
        self.waited = {e: {} for e in self.ENGINES}
        for e in ("tensor", "vector", "scalar", "gpsimd"):
            self.sem[e] = nc.alloc_semaphore(f"sem_{e}")
            self.cnt[e] = 0
        self._sn = 0

    def dma_slot(self, name):
        self.sem[name] = self.nc.alloc_semaphore(f"ds_{name}")
        self.cnt[name] = 0
        return name

    def _wait(self, eng, tok):
        if tok is None:
            return
        if eng == "tensor" and tok[0] == "tensor":
            return
        prev = self.waited[eng].get(tok[0], 0)
        if tok[1] <= prev:
            return
        self.waited[eng][tok[0]] = tok[1]
        sem = self.sem[tok[0]]
        val = tok[1]
        self.ops[eng].append(lambda e, s=sem, v=val: e.wait_ge(s, v))

    def op(self, eng, fn, reads=(), writes=(), slot=None, n_dma=1, self_inc=False):
        for b in reads:
            self._wait(eng, b.wtok)
        for b in writes:
            self._wait(eng, b.wtok)
            for t in b.rtoks:
                self._wait(eng, t)
        if slot is None:
            key, inc = eng, 1
            self.cnt[key] += 1
        else:
            key, inc = slot, 16
            self.cnt[key] += 16 * n_dma
        val = self.cnt[key]
        sem = self.sem[key]
        if self_inc:
            self.ops[eng].append(lambda e, f=fn, s=sem: f(e, s))
        else:
            self.ops[eng].append(lambda e, f=fn, s=sem, i=inc: f(e).then_inc(s, i))
        tok = (key, val)
        for b in reads:
            b.rtoks.append(tok)
        for b in writes:
            b.wtok = tok
            b.rtoks = []
        return tok

    def barrier_tokens(self, bufs):
        toks = []
        for b in bufs:
            if b.wtok is not None:
                toks.append(b.wtok)
            toks.extend(b.rtoks)
        return toks

    def finish(self, toks):
        for t in toks:
            self._wait("sync", t)



def build_program(phases="ABCDE"):
    nc = bass.Bass("TRN2", target_bir_lowering=False)

    def dram_in(name, shape, dtype=F32):
        return nc.dram_tensor(name, list(shape), dtype, kind="ExternalInput").ap()

    x_own = dram_in("x_own", (OWN, D))
    x_part = dram_in("x_part", (OWN, D))
    wq = dram_in("wq", (D // W_OC, 128, (D // 128) * W_OC))
    wk = dram_in("wk", (KVD // W_OC, 128, (D // 128) * W_OC))
    wv = dram_in("wv", (KVD // W_OC, 128, (D // 128) * W_OC))
    wo = dram_in("wo", (D // WO_OC, 128, (D // 128) * WO_OC))
    acat = dram_in("acat", (128, (D // 128) * 128))
    bcat = dram_in("bcat", (128, D))
    wrt = dram_in("wrt", (128, (D // 128) * 8))
    br_in = dram_in("br", (1, NE))
    cs_in = {(0, "c"): dram_in("cos_o", (128, OWN)),
             (0, "s"): dram_in("sin_o", (128, OWN)),
             (1, "c"): dram_in("cos_p", (128, OWN)),
             (1, "s"): dram_in("sin_p", (128, OWN))}
    masks_in = dram_in("masks", (NQC, 128, 4096), BF16)

    out_hbm = nc.dram_tensor("out", [OWN, D], F32, kind="ExternalOutput").ap()

    y1_d = nc.dram_tensor("y1_d", [OWN, D], F32, kind="Internal").ap()
    wqb = nc.dram_tensor("wqb", [D // W_OC, 128, (D // 128) * W_OC], BF16, kind="Internal").ap()
    wkb = nc.dram_tensor("wkb", [KVD // W_OC, 128, (D // 128) * W_OC], BF16, kind="Internal").ap()
    wvb = nc.dram_tensor("wvb", [KVD // W_OC, 128, (D // 128) * W_OC], BF16, kind="Internal").ap()
    wob = nc.dram_tensor("wob", [D // WO_OC, 128, (D // 128) * WO_OC], BF16, kind="Internal").ap()
    y2_d = nc.dram_tensor("y2_d", [OWN, D], F32, kind="Internal").ap()
    q_d = nc.dram_tensor("q_d", [NH * HD, OWN], BF16, kind="Internal").ap()
    k_d = nc.dram_tensor("k_d", [KVD, S], BF16, kind="Internal").ap()
    v_d = nc.dram_tensor("v_d", [S, KVD], BF16, kind="Internal").ap()
    ao_d = nc.dram_tensor("ao_d", [NH * HD, OWN], BF16, kind="Internal").ap()

    identf = nc.inline_tensor(np.eye(128, dtype=np.float32), "identf")
    identb = nc.inline_tensor(np.eye(128, dtype=ml_dtypes.bfloat16), "identb")
    ropePT = nc.inline_tensor(_rope_perm_T(), "ropePT")
    e8 = np.zeros((8, 128), np.float32)
    for h in range(8):
        e8[h, h * 16:(h + 1) * 16] = 1.0
    e8c = nc.inline_tensor(e8, "e8c")
    ones_col_b = nc.inline_tensor(np.ones((128, 1), dtype=ml_dtypes.bfloat16), "ones_col_b")
    ones_sq_b = nc.inline_tensor(np.ones((128, 128), dtype=ml_dtypes.bfloat16), "ones_sq_b")
    ones_col_f = nc.inline_tensor(np.ones((128, 1), dtype=np.float32), "ones_col_f")
    ones_row_f = nc.inline_tensor(np.ones((1, 128), dtype=np.float32), "ones_row_f")

    sb_idn = nc.alloc_sbuf_tensor("sb_idn", [128, 128], F32)
    sb_idnb = nc.alloc_sbuf_tensor("sb_idnb", [128, 128], BF16)
    sb_pT = nc.alloc_sbuf_tensor("sb_pT", [128, 128], BF16)
    sb_e8 = nc.alloc_sbuf_tensor("sb_e8", [8, 128], F32)
    sb_o1b = nc.alloc_sbuf_tensor("sb_o1b", [128, 1], BF16)
    sb_osq = nc.alloc_sbuf_tensor("sb_osq", [128, 128], BF16)
    sb_o1f = nc.alloc_sbuf_tensor("sb_o1f", [128, 1], F32)
    sb_orf = nc.alloc_sbuf_tensor("sb_orf", [1, 128], F32)
    sb_acat = nc.alloc_sbuf_tensor("sb_acat", [128, (D // 128) * 128], BF16)
    sb_bcat = nc.alloc_sbuf_tensor("sb_bcat", [128, D], BF16)
    sb_wrt = nc.alloc_sbuf_tensor("sb_wrt", [128, (D // 128) * 8], BF16)
    sb_brr = nc.alloc_sbuf_tensor("sb_brr", [1, NE], F32)
    sb_brb = nc.alloc_sbuf_tensor("sb_brb", [128, NE], F32)
    sb_qn = nc.alloc_sbuf_tensor("sb_qn", [1, 128], F32)
    sb_bias = nc.alloc_sbuf_tensor("sb_bias", [128, NH], F32)

    ov_base = (nc.sbuf_base + 31) & ~31

    psA = nc.alloc_psum_tensor("psA", [128, 1024], F32)
    psT = nc.alloc_psum_tensor("psT", [128, 2048], BF16)
    psB = nc.alloc_psum_tensor("psB", [128, 1024], F32)
    pav = nc.alloc_psum_tensor("pav", [128, 512], F32)
    plB = nc.alloc_psum_tensor("plB", [128, 512], F32)

    sch = Sched(nc)
    SL = {k: sch.dma_slot(k) for k in (
        "const", "ffnw", "x0", "x1", "lnT0", "lnT1", "w0", "w1", "cs",
        "st_y1", "st_q0", "st_q1", "st_k0", "st_k1", "st_v0", "st_v1",
        "kvld", "attnld", "qld0", "qld1", "mld",
        "wcq", "wck", "wcv", "wco",
        "st_ao0", "st_ao1", "wo0", "wo1", "y1s0", "y1s1",
        "st_y20", "st_y21", "y2l0", "y2l1", "st_out")}

    overlay_bufs = []

    class Overlay:
        def __init__(self):
            self.off = ov_base
            self.carry = sch.barrier_tokens(overlay_bufs)
            overlay_bufs.clear()

        def t(self, name, shape, dtype):
            itm = {F32: 4, BF16: 2}[dtype]
            t = nc.alloc_sbuf_tensor_at(name, list(shape), dtype, offset=self.off)
            self.off += (shape[1] * itm + 31) & ~31
            assert self.off <= SBUF_TOP, f"overlay overflow at {name}: {self.off}"
            b = Buf(name, self.carry)
            overlay_bufs.append(b)
            return t, b

    ps_bufs = []

    def psum_phase(names):
        carry = sch.barrier_tokens(ps_bufs)
        d = {n: Buf(n, carry) for n in names}
        ps_bufs.clear()
        ps_bufs.extend(d.values())
        return d

    CB = {}
    for name, sb, src in (("idn", sb_idn, identf), ("idnb", sb_idnb, identb),
                          ("pT", sb_pT, ropePT),
                          ("e8", sb_e8, e8c), ("o1b", sb_o1b, ones_col_b), ("osq", sb_osq, ones_sq_b),
                          ("o1f", sb_o1f, ones_col_f), ("orf", sb_orf, ones_row_f)):
        CB[name] = Buf(name)
        sch.op("sync", lambda e, sb=sb, src=src: e.dma_start(out=sb[:], in_=src.ap()),
               writes=[CB[name]], slot=SL["const"])
    CB["brr"] = Buf("brr")
    sch.op("sync", lambda e: e.dma_start(out=sb_brr[:], in_=br_in),
           writes=[CB["brr"]], slot=SL["const"])
    _const_tok = ("const", sch.cnt["const"])
    for _n in ("idn", "idnb", "pT", "e8", "o1b", "osq", "o1f", "orf", "brr"):
        CB[_n].wtok = _const_tok
    for name, sb, src in (("acat", sb_acat, acat), ("bcat", sb_bcat, bcat),
                          ("wrt", sb_wrt, wrt)):
        CB[name] = Buf(name)
        sch.op("gpsimd", lambda e, sb=sb, src=src: e.dma_start(out=sb[:], in_=src),
               writes=[CB[name]], slot=SL["ffnw"])
    _ffnw_tok = ("ffnw", sch.cnt["ffnw"])
    for _n in ("acat", "bcat", "wrt"):
        CB[_n].wtok = _ffnw_tok


    ps = psum_phase(["pA0", "pA1", "pB0", "pB1", "pT0", "pT1", "pav", "plB"])
    CB["brb"] = Buf("brb")
    sch.op("tensor", lambda e: e.matmul(out=psA[:, 0:NE], lhsT=sb_orf[:], rhs=sb_brr[:],
                                        start=True, stop=True),
           reads=[CB["orf"], CB["brr"]], writes=[ps["pA0"]])
    sch.op("scalar", lambda e: e.copy(out=sb_brb[:], in_=psA[:, 0:NE]),
           reads=[ps["pA0"]], writes=[CB["brb"]])
    B_qn = Buf("qn")
    sch.op("vector", lambda e: e.memset(sb_qn[:], 0.0), writes=[B_qn])
    B_bias = Buf("bias")

    DB = {n: Buf(n) for n in ("xo", "xp", "y1", "y2", "q", "k", "v", "ao", "out",
                              "wq", "wk", "wv", "wo", "cs", "masks")}
    for bname, sname, srcw, dstw, nch in (("wqb", "wq", wq, wqb, D // W_OC),
                                          ("wkb", "wk", wk, wkb, KVD // W_OC),
                                          ("wvb", "wv", wv, wvb, KVD // W_OC),
                                          ("wob", "wo", wo, wob, D // WO_OC)):
        DB[bname] = Buf(bname)
    _wcslot = [SL["wcq"], SL["wck"]]
    _wcst = [SL["wcv"], SL["wco"]]
    _wit = 0
    _wconv_plan = []
    for bname, sname, srcw, dstw, nch in (("wqb", "wq", wq, wqb, D // W_OC),
                                          ("wkb", "wk", wk, wkb, KVD // W_OC),
                                          ("wvb", "wv", wv, wvb, KVD // W_OC),
                                          ("wob", "wo", wo, wob, D // WO_OC)):
        _wconv_plan.append((bname, sname, srcw, dstw, nch))

    ov = Overlay()
    y1fm, b_y1fm = ov.t("y1fm", [128, 32 * HALF], BF16)
    xT, b_xT = [], []
    for i in range(2):
        t, b = ov.t(f"xT{i}", [128, D], F32)
        xT.append(t); b_xT.append(b)
    y1t, b_y1t = ov.t("y1t", [128, D], F32)
    lnt, b_lnt = ov.t("lnt", [128, D], BF16)
    xfm, b_xfm = ov.t("xfm", [128, D], BF16)
    wbuf, b_wbuf = [], []
    for i in range(2):
        t, b = ov.t(f"wbuf{i}", [128, 32 * W_OC], BF16)
        wbuf.append(t); b_wbuf.append(b)
    for bname, sname, srcw, dstw, nch in _wconv_plan:
        for j in range(nch):
            ws = _wit % 2
            _wit += 1
            sch.op("gpsimd", lambda e, j=j, srcw=srcw, ws=ws: e.dma_start(
                out=wbuf[ws][:], in_=srcw[j]),
                reads=[DB[sname]], writes=[b_wbuf[ws]], slot=_wcslot[ws])
            sch.op("gpsimd", lambda e, j=j, dstw=dstw, ws=ws: e.dma_start(
                out=dstw[j], in_=wbuf[ws][:]),
                reads=[b_wbuf[ws]], writes=[DB[bname]], slot=_wcst[ws])

    cs_cos, b_csc = ov.t("cs_cos", [128, HALF], F32)
    cs_sin, b_css = ov.t("cs_sin", [128, HALF], F32)
    stg, b_stg = [], []
    for i in range(4):
        t, b = ov.t(f"stg{i}", [128, 512], BF16)
        stg.append(t); b_stg.append(b)
    t1f, b_t1 = [], []
    t2f, b_t2 = [], []
    for i in range(2):
        t, b = ov.t(f"t1f{i}", [128, 512], F32)
        t1f.append(t); b_t1.append(b)
        t, b = ov.t(f"t2f{i}", [128, 512], F32)
        t2f.append(t); b_t2.append(b)
    qsq, b_qsq = ov.t("qsq", [128, 512], F32)

    class FFNCtx:
        def __init__(self, ovl, tag):
            self.stats, self.b_stats = ovl.t(f"stats{tag}", [128, 48], F32)
            self.lnsc, self.b_lnsc = ovl.t(f"lnsc{tag}", [128, 8], F32)
            self.ffns, self.b_ffns = ovl.t(f"ffns{tag}", [128, 32], F32)
            self.scrT, self.b_scrT = ovl.t(f"scrT{tag}", [8, 128], F32)
            self.scb, self.b_scb = ovl.t(f"scb{tag}", [128, 128], F32)
            self.lows, self.b_lows = ovl.t(f"lows{tag}", [128, 128], BF16)
            self.lowf, self.b_lowf = ovl.t(f"lowf{tag}", [128, 128], F32)
            self.ffo, self.b_ffo = ovl.t(f"ffo{tag}", [128, 512], F32)

    fc = FFNCtx(ov, "A")

    def emit_ln(fx, src_buf, src_t, dst_buf, dst_t):
        def f_stats(e):
            for g in range(8):
                ii = e.bn_stats(out=fx.stats[:, g * 6:(g + 1) * 6],
                                in_=src_t[:, g * 512:(g + 1) * 512])
            return ii
        sch.op("vector", f_stats, reads=[src_buf], writes=[fx.b_stats])
        sch.op("vector", lambda e: e.bn_aggr(out=fx.lnsc[:, 0:2], in_=fx.stats[:]),
               reads=[fx.b_stats], writes=[fx.b_lnsc])
        sch.op("vector", lambda e: e.tensor_scalar(out=fx.lnsc[:, 2:3], in0=fx.lnsc[:, 1:2],
                                                   scalar1=LN_EPS, scalar2=None, op0=ALU.add),
               reads=[fx.b_lnsc], writes=[fx.b_lnsc])
        sch.op("scalar", lambda e: e.activation(out=fx.lnsc[:, 4:5], in_=fx.lnsc[:, 2:3],
                                                func=AF.Ln),
               reads=[fx.b_lnsc], writes=[fx.b_lnsc])
        sch.op("scalar", lambda e: e.activation(out=fx.lnsc[:, 3:4], in_=fx.lnsc[:, 4:5],
                                                func=AF.Exp, scale=-0.5),
               reads=[fx.b_lnsc], writes=[fx.b_lnsc])
        sch.op("vector", lambda e: e.tensor_scalar(out=dst_t[:], in0=src_t[:],
                                                   scalar1=fx.lnsc[:, 0:1],
                                                   scalar2=fx.lnsc[:, 3:4],
                                                   op0=ALU.subtract, op1=ALU.mult),
               reads=[src_buf, fx.b_lnsc], writes=[dst_buf])

    def emit_T(src_buf, src_t, dst_buf, dst_t, colf, slotname):
        for g in range(8):
            par = g % 2
            pb = ps[f"pT{par}"]
            def f_tq(e, g=g, par=par):
                for k4 in range(4):
                    ft = g * 4 + k4
                    pt = psT[:, par * 1024 + k4 * 128: par * 1024 + (k4 + 1) * 128]
                    ii = e.transpose(out=pt, in_=src_t[:, ft * 128:(ft + 1) * 128],
                                     identity=sb_idnb[:])
                return ii
            sch.op("tensor", f_tq, reads=[src_buf, CB["idnb"]], writes=[pb])
            ft0 = g * 4
            stride = colf(ft0 + 1) - colf(ft0)
            if stride == 128:
                dview = dst_t[:, colf(ft0):colf(ft0) + 512]
                sview = psT[:, par * 1024: par * 1024 + 512]
            else:
                dview = dst_t[:].rearrange("p (f c) -> p f c", c=stride)[
                    :, colf(ft0) // stride: colf(ft0) // stride + 4,
                    (colf(ft0) % stride):(colf(ft0) % stride) + 128]
                sview = psT[:, par * 1024: par * 1024 + 512].rearrange(
                    "p (f c) -> p f c", c=128)
            sch.op("scalar", lambda e, dview=dview, sview=sview: e.copy(
                out=dview, in_=sview),
                reads=[ps[f"pT{par}"]], writes=[dst_buf])

    def emit_ffn(fx, xbuf, x_t, outbuf, out_t, xf_t, b_xf):
        def f_rout(e):
            for ft in range(32):
                ii = e.matmul(out=psB[:, 512:512 + NE],
                              lhsT=xf_t[:, ft * 128:(ft + 1) * 128],
                              rhs=sb_wrt[:, ft * 8:(ft + 1) * 8],
                              start=(ft == 0), stop=(ft == 31))
            return ii
        sch.op("tensor", f_rout, reads=[b_xf, CB["wrt"]], writes=[ps["pB1"]])
        sch.op("scalar", lambda e: e.copy(out=fx.ffns[:, 24:32], in_=psB[:, 512:512 + NE]),
               reads=[ps["pB1"]], writes=[fx.b_ffns])
        sch.op("vector", lambda e: e.tensor_tensor(out=fx.ffns[:, 0:NE], in0=fx.ffns[:, 24:32],
                                                   in1=sb_brb[:], op=ALU.add),
               reads=[fx.b_ffns, CB["brb"]], writes=[fx.b_ffns])
        sch.op("vector", lambda e: e.reduce_max(out=fx.ffns[:, 16:17], in_=fx.ffns[:, 0:NE],
                                                axis=AX.X, negate=True),
               reads=[fx.b_ffns], writes=[fx.b_ffns])
        sch.op("scalar", lambda e: e.activation(out=fx.ffns[:, 8:16], in_=fx.ffns[:, 0:NE],
                                                func=AF.Exp, bias=fx.ffns[:, 16:17],
                                                accum_out=fx.ffns[:, 17:18]),
               reads=[fx.b_ffns], writes=[fx.b_ffns])
        sch.op("vector", lambda e: e.reciprocal(out=fx.ffns[:, 18:19], in_=fx.ffns[:, 17:18]),
               reads=[fx.b_ffns], writes=[fx.b_ffns])
        sch.op("vector", lambda e: e.tensor_scalar(out=fx.ffns[:, 0:NE], in0=fx.ffns[:, 8:16],
                                                   scalar1=fx.ffns[:, 18:19], scalar2=8.0,
                                                   op0=ALU.mult, op1=ALU.mult),
               reads=[fx.b_ffns], writes=[fx.b_ffns])
        sch.op("tensor", lambda e: e.transpose(out=psB[0:8, 512:640], in_=fx.ffns[:, 0:NE],
                                               identity=sb_idn[:]),
               reads=[fx.b_ffns, CB["idn"]], writes=[ps["pB1"]])
        sch.op("scalar", lambda e: e.copy(out=fx.scrT[:], in_=psB[0:8, 512:640]),
               reads=[ps["pB1"]], writes=[fx.b_scrT])
        sch.op("tensor", lambda e: e.matmul(out=psB[:, 640:768], lhsT=sb_e8[:], rhs=fx.scrT[:],
                                            start=True, stop=True),
               reads=[CB["e8"], fx.b_scrT], writes=[ps["pB1"]])
        sch.op("scalar", lambda e: e.copy(out=fx.scb[:], in_=psB[:, 640:768]),
               reads=[ps["pB1"]], writes=[fx.b_scb])
        def f_low(e):
            for ft in range(32):
                ii = e.matmul(out=psB[:, 768:896],
                              lhsT=sb_acat[:, ft * 128:(ft + 1) * 128],
                              rhs=xf_t[:, ft * 128:(ft + 1) * 128],
                              start=(ft == 0), stop=(ft == 31))
            return ii
        sch.op("tensor", f_low, reads=[b_xf, CB["acat"]], writes=[ps["pB1"]])
        sch.op("scalar", lambda e: e.copy(out=fx.lowf[:], in_=psB[:, 768:896]),
               reads=[ps["pB1"]], writes=[fx.b_lowf])
        sch.op("vector", lambda e: e.tensor_tensor(out=fx.lows[:], in0=fx.lowf[:],
                                                   in1=fx.scb[:], op=ALU.mult),
               reads=[fx.b_lowf, fx.b_scb], writes=[fx.b_lows])
        for oc in range(8):
            pbuf = ps["pB0"]
            pt = psB[:, 0:512]
            sch.op("tensor", lambda e, oc=oc, pt=pt: e.matmul(
                out=pt, lhsT=fx.lows[:], rhs=sb_bcat[:, oc * 512:(oc + 1) * 512],
                start=True, stop=True),
                reads=[fx.b_lows, CB["bcat"]], writes=[pbuf])
            sch.op("scalar", lambda e, pt=pt: e.copy(out=fx.ffo[:], in_=pt),
                   reads=[pbuf], writes=[fx.b_ffo])
            sch.op("vector", lambda e, oc=oc: e.tensor_tensor(
                out=out_t[:, oc * 512:(oc + 1) * 512], in0=fx.ffo[:],
                in1=x_t[:, oc * 512:(oc + 1) * 512], op=ALU.add),
                reads=[fx.b_ffo, xbuf], writes=[outbuf])

    def phase_A(x_dram, b_xdram, half, own):
        for ti in range(NT_HALF):
            xs = ti % 2
            row0 = half * HALF + ti * 128
            sch.op("sync", lambda e, row0=row0, xs=xs: e.dma_start(
                out=xT[xs][:], in_=x_dram[row0:row0 + 128, :]),
                reads=[b_xdram], writes=[b_xT[xs]], slot=SL[f"x{xs}"])
            emit_ln(fc, b_xT[xs], xT[xs], b_lnt, lnt)
            emit_T(b_lnt, lnt, b_xfm, xfm, lambda ft: ft * 128, "lnT0")
            emit_ffn(fc, b_xT[xs], xT[xs], b_y1t, y1t, xfm, b_xfm)
            if own:
                sch.op("scalar", lambda e, row0=row0: e.dma_start(
                    out=y1_d[row0:row0 + 128, :], in_=y1t[:]),
                    reads=[b_y1t], writes=[DB["y1"]], slot=SL["st_y1"])
            emit_ln(fc, b_y1t, y1t, b_lnt, lnt)
            emit_T(b_lnt, lnt, b_y1fm, y1fm, lambda ft, ti=ti: ft * HALF + ti * 128, "lnT1")

    def rope_store(pname, o_global, slot_col, tc, pbuf, pt, half, sslot):
        st = b_stg[sslot]
        sch.op("scalar", lambda e: e.copy(out=stg[sslot][:], in_=pt),
               reads=[pbuf], writes=[st])
        if B_ROPE:
            csl = cs_cos[:, tc * 512:(tc + 1) * 512]
            ssl = cs_sin[:, tc * 512:(tc + 1) * 512]
            rsl = 2 + sslot
            sch.op("tensor", lambda e: e.matmul(out=pav[:], lhsT=sb_pT[:], rhs=stg[sslot][:],
                                                start=True, stop=True),
                   reads=[CB["pT"], st], writes=[ps["pav"]])
            sch.op("scalar", lambda e, rsl=rsl: e.copy(out=stg[rsl][:], in_=pav[:]),
                   reads=[ps["pav"]], writes=[b_stg[rsl]])
            sch.op("vector", lambda e, c=csl: e.tensor_tensor(out=t1f[sslot][:],
                                                              in0=stg[sslot][:],
                                                              in1=c, op=ALU.mult),
                   reads=[st, b_csc], writes=[b_t1[sslot]])
            sch.op("vector", lambda e, s=ssl, rsl=rsl: e.tensor_tensor(out=t2f[sslot][:],
                                                                       in0=stg[rsl][:],
                                                                       in1=s, op=ALU.mult),
                   reads=[b_stg[rsl], b_css], writes=[b_t2[sslot]])
            sch.op("vector", lambda e: e.tensor_tensor(out=stg[sslot][:], in0=t1f[sslot][:],
                                                       in1=t2f[sslot][:], op=ALU.add),
                   reads=[b_t1[sslot], b_t2[sslot]], writes=[st])
        if pname == "q":
            col0 = half * HALF + tc * 512
            sch.op("scalar", lambda e, o=o_global, c=col0: e.dma_start(
                out=q_d[o:o + 128, c:c + 512], in_=stg[sslot][:]),
                reads=[st], writes=[DB["q"]], slot=SL[f"st_q{sslot}"])
            h = o_global // 128
            if not B_QNORM:
                return
            sch.op("vector", lambda e: e.tensor_mul(out=qsq[:], in0=stg[sslot][:],
                                                    in1=stg[sslot][:]),
                   reads=[st], writes=[b_qsq])
            sch.op("tensor", lambda e: e.matmul(out=plB[0:1, :], lhsT=sb_o1f[:], rhs=qsq[:],
                                                start=True, stop=True),
                   reads=[CB["o1f"], b_qsq], writes=[ps["plB"]])
            sch.op("scalar", lambda e: e.copy(out=qsq[0:1, :], in_=plB[0:1, :]),
                   reads=[ps["plB"]], writes=[b_qsq])
            sch.op("vector", lambda e: e.reduce_max(out=sb_qn[0:1, 40:41], in_=qsq[0:1, :],
                                                    axis=AX.X),
                   reads=[b_qsq], writes=[B_qn])
            sch.op("vector", lambda e, h=h: e.tensor_tensor(out=sb_qn[0:1, h:h + 1],
                                                            in0=sb_qn[0:1, h:h + 1],
                                                            in1=sb_qn[0:1, 40:41], op=ALU.max),
                   reads=[B_qn], writes=[B_qn])
        else:
            c0 = slot_col * 512
            sch.op("scalar", lambda e, o=o_global, c=c0: e.dma_start(
                out=k_d[o:o + 128, c:c + 512], in_=stg[sslot][:]),
                reads=[st], writes=[DB["k"]], slot=SL[f"st_k{sslot}"])

    def phase_B(half, own):
        slot_base = (0 if own else 4) + 2 * half
        lo = half * HALF
        key = 0 if own else 1
        sch.op("sync", lambda e: e.dma_start(out=cs_cos[:], in_=cs_in[(key, "c")][:, lo:lo + HALF]),
               writes=[b_csc], slot=SL["cs"])
        sch.op("sync", lambda e: e.dma_start(out=cs_sin[:], in_=cs_in[(key, "s")][:, lo:lo + HALF]),
               writes=[b_css], slot=SL["cs"])
        b_csc.wtok = b_css.wtok = ("cs", sch.cnt["cs"])

        projs = ([("q", wqb, DB["wqb"], D // W_OC)] if own and "q" in B_PROJS else [])
        if "k" in B_PROJS:
            projs.append(("k", wkb, DB["wkb"], KVD // W_OC))
        if "v" in B_PROJS:
            projs.append(("v", wvb, DB["wvb"], KVD // W_OC))
        it = 0
        for pname, wsrc, wdb, nchunk in projs:
            for j in range(nchunk):
                ws = it % 2
                it += 1
                sch.op("sync", lambda e, j=j, ws=ws, wsrc=wsrc: e.dma_start(
                    out=wbuf[ws][:], in_=wsrc[j]),
                    reads=[wdb], writes=[b_wbuf[ws]], slot=SL[f"w{ws}"])
                if pname == "v":
                    for tt in range(NT_HALF):
                        sslot = tt % 2
                        pbuf = ps["pA0"] if sslot == 0 else ps["pA1"]
                        pt = psA[:, 0:512] if sslot == 0 else psA[:, 512:1024]
                        def f_vm(e, tt=tt, pt=pt, ws=ws):
                            for ft in range(32):
                                ii = e.matmul(
                                    out=pt[:, 0:W_OC],
                                    lhsT=y1fm[:, ft * HALF + tt * 128: ft * HALF + (tt + 1) * 128],
                                    rhs=wbuf[ws][:, ft * W_OC:(ft + 1) * W_OC],
                                    start=(ft == 0), stop=(ft == 31))
                            return ii
                        sch.op("tensor", f_vm, reads=[b_wbuf[ws], b_y1fm], writes=[pbuf])
                        st = b_stg[2 + sslot]
                        sch.op("scalar", lambda e, pt=pt, sslot=sslot: e.copy(
                            out=stg[2 + sslot][:, 0:W_OC], in_=pt[:, 0:W_OC]),
                            reads=[pbuf], writes=[st])
                        grow = (slot_base + tt // 4) * 512 + (tt % 4) * 128
                        sch.op("scalar", lambda e, g=grow, j=j, sslot=sslot: e.dma_start(
                            out=v_d[g:g + 128, j * W_OC:(j + 1) * W_OC],
                            in_=stg[2 + sslot][:, 0:W_OC]),
                            reads=[st], writes=[DB["v"]], slot=SL[f"st_v{sslot}"])
                else:
                    for o1 in range(W_OC // 128):
                        o_global = j * W_OC + o1 * 128
                        for tc in range(2):
                            sslot = (o1 * 2 + tc) % 2
                            pbuf = ps["pA0"] if sslot == 0 else ps["pA1"]
                            pt = psA[:, 0:512] if sslot == 0 else psA[:, 512:1024]
                            def f_mm(e, o1=o1, tc=tc, pt=pt, ws=ws):
                                for ft in range(32):
                                    ii = e.matmul(
                                        out=pt,
                                        lhsT=wbuf[ws][:, ft * W_OC + o1 * 128: ft * W_OC + (o1 + 1) * 128],
                                        rhs=y1fm[:, ft * HALF + tc * 512: ft * HALF + (tc + 1) * 512],
                                        start=(ft == 0), stop=(ft == 31))
                                return ii
                            sch.op("tensor", f_mm, reads=[b_wbuf[ws], b_y1fm], writes=[pbuf])
                            rope_store(pname, o_global, slot_base + tc, tc, pbuf, pt, half, sslot)

    last_toks = []
    for half in range(2):
        phase_A(x_own, DB["xo"], half, own=True)
        if "B" in phases:
            phase_B(half, own=True)
    if "B" in phases:
        for half in range(2):
            phase_A(x_part, DB["xp"], half, own=False)
            phase_B(half, own=False)
    if "C" not in phases:
        for tt in range(OWN // 128):
            tk = sch.op("sync", lambda e, tt=tt: e.dma_start(
                out=out_hbm[tt * 128:(tt + 1) * 128, :],
                in_=y1_d[tt * 128:(tt + 1) * 128, :]),
                reads=[DB["y1"]], writes=[DB["out"]], slot=SL["st_out"])
            last_toks.append(tk)
        if "B" in phases:
            last_toks.append(DB["q"].wtok)
            last_toks.append(DB["k"].wtok)
            last_toks.append(DB["v"].wtok)
        sch.finish(last_toks)
        return _finish_block(nc, sch)

    ovc = Overlay()
    K_sb, b_K = ovc.t("K_sb", [128, NKV * S], BF16)
    V_sb, b_V = ovc.t("V_sb", [128, (S // 128) * KVD], BF16)
    qtl, b_qtl = [], []
    for i in range(2):
        t, b = ovc.t(f"qtl{i}", [128, 512], BF16)
        qtl.append(t); b_qtl.append(b)
    exb, b_exb = [], []
    for i in range(2):
        t, b = ovc.t(f"exb{i}", [128, 1536], BF16)
        exb.append(t); b_exb.append(b)
    maskb, b_maskb = ovc.t("maskb", [128, 4096], BF16)
    rlb, b_rlb = ovc.t("rlb", [128, 512], F32)
    rlr, b_rlr = ovc.t("rlr", [1, 512], F32)
    aost, b_aost = [], []
    for i in range(2):
        t, b = ovc.t(f"aost{i}", [128, 512], BF16)
        aost.append(t); b_aost.append(b)
    ksq, b_ksq = ovc.t("ksq", [128, 4096], F32)
    avf, b_avf = ovc.t("avf", [128, 512], F32)

    psc = psum_phase(["pA0", "pA1", "pB0", "pB1", "pT0", "pT1", "pav", "plB"])

    def f_kld(e, sem):
        for kv in range(NKV):
            e.dma_start(out=K_sb[:, kv * S:(kv + 1) * S],
                        in_=k_d[kv * 128:(kv + 1) * 128, :].rearrange("d pos -> d pos")
                        ).then_inc(sem, 16)
    sch.op("sync", f_kld, reads=[DB["k"]], writes=[b_K], slot=SL["kvld"],
           n_dma=NKV, self_inc=True)

    def f_vld(e, sem):
        for g in range(4):
            e.dma_start(
                out=V_sb[:, g * 8 * KVD:(g + 1) * 8 * KVD].rearrange(
                    "p (pt c) -> p pt c", pt=8),
                in_=v_d[g * 1024:(g + 1) * 1024, :].rearrange(
                    "(pt p) c -> p pt c", p=128)).then_inc(sem, 16)
    sch.op("sync", f_vld, reads=[DB["v"]], writes=[b_V], slot=SL["kvld"],
           n_dma=4, self_inc=True)
    b_K.wtok = b_V.wtok = ("kvld", sch.cnt["kvld"])

    for kv in range(NKV):
        sch.op("vector", lambda e, kv=kv: e.tensor_mul(
            out=ksq[:], in0=K_sb[:, kv * S:(kv + 1) * S], in1=K_sb[:, kv * S:(kv + 1) * S]),
            reads=[b_K], writes=[b_ksq])
        for c8 in range(8):
            sch.op("tensor", lambda e, c8=c8: e.matmul(
                out=plB[0:1, :], lhsT=sb_o1f[:], rhs=ksq[:, c8 * 512:(c8 + 1) * 512],
                start=True, stop=True),
                reads=[CB["o1f"], b_ksq], writes=[psc["plB"]])
            sch.op("scalar", lambda e: e.copy(out=rlr[:], in_=plB[0:1, :]),
                   reads=[psc["plB"]], writes=[b_rlr])
            sch.op("vector", lambda e: e.reduce_max(out=sb_qn[0:1, 40:41], in_=rlr[:],
                                                    axis=AX.X),
                   reads=[b_rlr], writes=[B_qn])
            sch.op("vector", lambda e, kv=kv: e.tensor_tensor(
                out=sb_qn[0:1, 32 + kv:33 + kv], in0=sb_qn[0:1, 32 + kv:33 + kv],
                in1=sb_qn[0:1, 40:41], op=ALU.max),
                reads=[B_qn], writes=[B_qn])

    for h in range(NH):
        kv = h // 4
        sch.op("vector", lambda e, h=h, kv=kv: e.tensor_tensor(
            out=sb_qn[0:1, 41:42], in0=sb_qn[0:1, h:h + 1],
            in1=sb_qn[0:1, 32 + kv:33 + kv], op=ALU.mult),
            reads=[B_qn], writes=[B_qn])
        sch.op("scalar", lambda e: e.activation(out=sb_qn[0:1, 43:44], in_=sb_qn[0:1, 41:42],
                                                func=AF.Ln),
               reads=[B_qn], writes=[B_qn])
        sch.op("scalar", lambda e: e.activation(out=sb_qn[0:1, 42:43], in_=sb_qn[0:1, 43:44],
                                                func=AF.Exp, scale=0.5),
               reads=[B_qn], writes=[B_qn])
        sch.op("vector", lambda e, h=h: e.tensor_scalar(
            out=sb_qn[0:1, 64 + h:65 + h], in0=sb_qn[0:1, 42:43],
            scalar1=-RSCALE, scalar2=None, op0=ALU.mult),
            reads=[B_qn], writes=[B_qn])
    sch.op("tensor", lambda e: e.matmul(out=pav[:, 0:NH], lhsT=sb_orf[:],
                                        rhs=sb_qn[0:1, 64:64 + NH], start=True, stop=True),
           reads=[CB["orf"], B_qn], writes=[psc["pav"]])
    sch.op("scalar", lambda e: e.copy(out=sb_bias[:], in_=pav[:, 0:NH]),
           reads=[psc["pav"]], writes=[B_bias])

    for p in range(NQC):
        sch.op("sync", lambda e, p=p: e.dma_start(out=maskb[:], in_=masks_in[p]),
               reads=[DB["masks"]], writes=[b_maskb], slot=SL["mld"])
        tiles = [(sl, j0) for sl in SLOTS_P[p] for j0 in range(4)]
        nk = len(tiles)
        for h in range(NH):
            kv = h // 4
            qs = h % 2
            sch.op("sync", lambda e, p=p, h=h, qs=qs: e.dma_start(
                out=qtl[qs][:], in_=q_d[h * 128:(h + 1) * 128, p * 512:(p + 1) * 512]),
                reads=[DB["q"]], writes=[b_qtl[qs]], slot=SL[f"qld{qs}"])

            groups = [tiles[i:i + 2] for i in range(0, nk, 2)]

            def emit_scores(gi):
                grp = groups[gi]
                pb = [psc["pA0"], psc["pA1"]] if gi % 2 == 0 else [psc["pB0"], psc["pB1"]]
                pt = psA if gi % 2 == 0 else psB
                def f(e, grp=grp, pt=pt, qs=qs, kv=kv):
                    for i, (sl, j0) in enumerate(grp):
                        pos = sl * 4 + j0
                        ii = e.matmul(out=pt[:, i * 512:(i + 1) * 512],
                                      lhsT=K_sb[:, kv * S + pos * 128: kv * S + (pos + 1) * 128],
                                      rhs=qtl[qs][:], start=True, stop=True,
                                      skip_group_check=True)
                    return ii
                sch.op("tensor", f, reads=[b_K, b_qtl[qs]], writes=pb)

            def emit_rest(gi, first_tile_idx):
                grp = groups[gi]
                pb = [psc["pA0"], psc["pA1"]] if gi % 2 == 0 else [psc["pB0"], psc["pB1"]]
                pt = psA if gi % 2 == 0 else psB
                eb = b_exb[gi % 2]
                et = exb[gi % 2]
                n = len(grp)
                sch.op("scalar", lambda e, pt=pt, et=et, n=n, h=h: e.activation(
                    out=et[:, 0:n * 512], in_=pt[:, 0:n * 512], func=AF.Exp,
                    bias=sb_bias[:, h:h + 1], scale=RSCALE),
                    reads=pb + [B_bias], writes=[eb])
                for i, (sl, j0) in enumerate(grp):
                    if sl == p or sl == p + 4:
                        mi = 0 if sl == p else 1
                        mc = mi * 2048 + j0 * 512
                        sch.op("vector", lambda e, i=i, mc=mc, et=et: e.tensor_tensor(
                            out=et[:, i * 512:(i + 1) * 512], in0=et[:, i * 512:(i + 1) * 512],
                            in1=maskb[:, mc:mc + 512], op=ALU.mult),
                            reads=[eb, b_maskb], writes=[eb])
                def f_avl(e, grp=grp, et=et, kv=kv, fti=first_tile_idx):
                    for i, (sl, j0) in enumerate(grp):
                        ti = fti + i
                        pos = sl * 4 + j0
                        e.matmul(out=pav[:],
                                 lhsT=V_sb[:, pos * KVD + kv * 128: pos * KVD + (kv + 1) * 128],
                                 rhs=et[:, i * 512:(i + 1) * 512],
                                 start=(ti == 0), stop=(ti == nk - 1), skip_group_check=True)
                        ii = e.matmul(out=plB[:], lhsT=sb_osq[:],
                                      rhs=et[:, i * 512:(i + 1) * 512],
                                      start=(ti == 0), stop=(ti == nk - 1),
                                      skip_group_check=True)
                    return ii
                sch.op("tensor", f_avl, reads=[eb, b_V, CB["osq"]],
                       writes=[psc["pav"], psc["plB"]])

            emit_scores(0)
            ti0 = 0
            for gi in range(1, len(groups)):
                emit_scores(gi)
                emit_rest(gi - 1, ti0)
                ti0 += len(groups[gi - 1])
            emit_rest(len(groups) - 1, ti0)

            sch.op("scalar", lambda e: e.activation(out=rlb[:], in_=plB[:], func=AF.Ln),
                   reads=[psc["plB"]], writes=[b_rlb])
            sch.op("scalar", lambda e: e.activation(out=rlb[:], in_=rlb[:], func=AF.Exp,
                                                    scale=-1.0),
                   reads=[b_rlb], writes=[b_rlb])
            ao_s = h % 2
            sch.op("scalar", lambda e: e.copy(out=avf[:], in_=pav[:]),
                   reads=[psc["pav"]], writes=[b_avf])
            sch.op("vector", lambda e, ao_s=ao_s: e.tensor_tensor(
                out=aost[ao_s][:], in0=avf[:], in1=rlb[:], op=ALU.mult),
                reads=[b_avf, b_rlb], writes=[b_aost[ao_s]])
            sch.op("scalar", lambda e, p=p, h=h, ao_s=ao_s: e.dma_start(
                out=ao_d[h * 128:(h + 1) * 128, p * 512:(p + 1) * 512], in_=aost[ao_s][:]),
                reads=[b_aost[ao_s]], writes=[DB["ao"]], slot=SL[f"st_ao{ao_s}"])

    if "D" not in phases:
        for tt in range(OWN // 128):
            tk = sch.op("sync", lambda e, tt=tt: e.dma_start(
                out=out_hbm[tt * 128:(tt + 1) * 128, :],
                in_=y1_d[tt * 128:(tt + 1) * 128, :]),
                reads=[DB["y1"]], writes=[DB["out"]], slot=SL["st_out"])
        sch.finish([tk, DB["ao"].wtok])
        return _finish_block(nc, sch)

    ovd = Overlay()
    attn_sb, b_attn = ovd.t("attn_sb", [128, 32 * OWN], BF16)
    wobuf, b_wob = [], []
    for i in range(2):
        t, b = ovd.t(f"wob{i}", [128, 32 * WO_OC], BF16)
        wobuf.append(t); b_wob.append(b)
    y1s, b_y1s = [], []
    for i in range(2):
        t, b = ovd.t(f"y1s{i}", [128, WO_OC], F32)
        y1s.append(t); b_y1s.append(b)
    y2s, b_y2s = [], []
    for i in range(2):
        t, b = ovd.t(f"y2s{i}", [128, WO_OC], F32)
        y2s.append(t); b_y2s.append(b)
    y2f, b_y2f = [], []
    for i in range(2):
        t, b = ovd.t(f"y2f{i}", [128, WO_OC], F32)
        y2f.append(t); b_y2f.append(b)

    psd = psum_phase(["pA0", "pA1", "pB0", "pB1", "pT0", "pT1", "pav", "plB"])

    def f_ald(e, sem):
        for g in range(4):
            e.dma_start(
                out=attn_sb[:, g * 8 * OWN:(g + 1) * 8 * OWN].rearrange(
                    "p (hc q) -> p hc q", hc=8),
                in_=ao_d[g * 1024:(g + 1) * 1024, :].rearrange(
                    "(hc p) q -> p hc q", p=128)).then_inc(sem, 16)
    sch.op("sync", f_ald, reads=[DB["ao"]], writes=[b_attn], slot=SL["attnld"],
           n_dma=4, self_inc=True)

    for oc in range(D // WO_OC):
        ws = oc % 2
        sch.op("sync", lambda e, oc=oc, ws=ws: e.dma_start(out=wobuf[ws][:], in_=wob[oc]),
               reads=[DB["wob"]], writes=[b_wob[ws]], slot=SL[f"wo{ws}"])
        for tt in range(OWN // 128):
            sl2 = tt % 2
            pbuf = psd["pA0"] if sl2 == 0 else psd["pA1"]
            pt = psA[:, 0:WO_OC] if sl2 == 0 else psA[:, 512:512 + WO_OC]
            def f_wo(e, tt=tt, ws=ws, pt=pt):
                for hc in range(32):
                    ii = e.matmul(out=pt,
                                  lhsT=attn_sb[:, hc * OWN + tt * 128: hc * OWN + (tt + 1) * 128],
                                  rhs=wobuf[ws][:, hc * WO_OC:(hc + 1) * WO_OC],
                                  start=(hc == 0), stop=(hc == 31))
                return ii
            sch.op("tensor", f_wo, reads=[b_wob[ws], b_attn], writes=[pbuf])
            sch.op("sync", lambda e, tt=tt, oc=oc, sl2=sl2: e.dma_start(
                out=y1s[sl2][:], in_=y1_d[tt * 128:(tt + 1) * 128, oc * WO_OC:(oc + 1) * WO_OC]),
                reads=[DB["y1"]], writes=[b_y1s[sl2]], slot=SL[f"y1s{sl2}"])
            sch.op("scalar", lambda e, sl2=sl2, pt=pt: e.copy(out=y2f[sl2][:], in_=pt),
                   reads=[pbuf], writes=[b_y2f[sl2]])
            sch.op("vector", lambda e, sl2=sl2: e.tensor_tensor(
                out=y2s[sl2][:], in0=y2f[sl2][:], in1=y1s[sl2][:], op=ALU.add),
                reads=[b_y2f[sl2], b_y1s[sl2]], writes=[b_y2s[sl2]])
            sch.op("scalar", lambda e, tt=tt, oc=oc, sl2=sl2: e.dma_start(
                out=y2_d[tt * 128:(tt + 1) * 128, oc * WO_OC:(oc + 1) * WO_OC],
                in_=y2s[sl2][:]),
                reads=[b_y2s[sl2]], writes=[DB["y2"]], slot=SL[f"st_y2{sl2}"])

    if "E" not in phases:
        for tt in range(OWN // 128):
            tk = sch.op("sync", lambda e, tt=tt: e.dma_start(
                out=out_hbm[tt * 128:(tt + 1) * 128, :],
                in_=y2_d[tt * 128:(tt + 1) * 128, :]),
                reads=[DB["y2"]], writes=[DB["out"]], slot=SL["st_out"])
        sch.finish([tk])
        return _finish_block(nc, sch)

    ove = Overlay()
    y2T, b_y2T = [], []
    for i in range(2):
        t, b = ove.t(f"y2T{i}", [128, D], F32)
        y2T.append(t); b_y2T.append(b)
    y3t, b_y3t = ove.t("y3t", [128, D], F32)
    lntE, b_lntE = ove.t("lntE", [128, D], BF16)
    xfmE, b_xfmE = ove.t("xfmE", [128, D], BF16)
    fcE = FFNCtx(ove, "E")

    pse = psum_phase(["pA0", "pA1", "pB0", "pB1", "pT0", "pT1", "pav", "plB"])
    ps.update(pse)

    out_toks = []

    def phase_E():
        for tt in range(OWN // 128):
            ysl = tt % 2
            row0 = tt * 128
            sch.op("sync", lambda e, row0=row0, ysl=ysl: e.dma_start(
                out=y2T[ysl][:], in_=y2_d[row0:row0 + 128, :]),
                reads=[DB["y2"]], writes=[b_y2T[ysl]], slot=SL[f"y2l{ysl}"])
            emit_ln(fcE, b_y2T[ysl], y2T[ysl], b_lntE, lntE)
            emit_T(b_lntE, lntE, b_xfmE, xfmE, lambda ft: ft * 128, "lnT0")
            emit_ffn(fcE, b_y2T[ysl], y2T[ysl], b_y3t, y3t, xfmE, b_xfmE)
            tk = sch.op("sync", lambda e, row0=row0: e.dma_start(
                out=out_hbm[row0:row0 + 128, :], in_=y3t[:]),
                reads=[b_y3t], writes=[DB["out"]], slot=SL["st_out"])
            out_toks.append(tk)

    phase_E()
    sch.finish(out_toks[-1:])
    return _finish_block(nc, sch)


def _finish_block(nc, sch):
    with nc.Block() as block:
        @block.sync
        def _(e):
            for f in sch.ops["sync"]:
                f(e)

        @block.tensor
        def _(e):
            for f in sch.ops["tensor"]:
                f(e)

        @block.vector
        def _(e):
            for f in sch.ops["vector"]:
                f(e)

        @block.scalar
        def _(e):
            for f in sch.ops["scalar"]:
                f(e)

        @block.gpsimd
        def _(e):
            for f in sch.ops["gpsimd"]:
                f(e)

    return nc



def _own_rows(hg):
    qcs = OWN_QCS[hg]
    return np.concatenate([np.arange(g * 512, (g + 1) * 512) for g in qcs])


def make_in_maps(x, Wq, Wk, Wv, Wo, A, Bm, Wr, br):
    wq_t = _tile_lhsT(np.ascontiguousarray(Wq.T), W_OC)
    wk_t = _tile_lhsT(np.ascontiguousarray(Wk.T), W_OC)
    wv_t = _tile_lhsT(np.ascontiguousarray(Wv.T), W_OC)
    wo_t = _tile_lhsT(np.ascontiguousarray(Wo.T), WO_OC)
    acat = np.ascontiguousarray(
        _tile_lhsT(np.ascontiguousarray(A.transpose(2, 0, 1).reshape(D, 128)), 128)[0])
    bcat = np.ascontiguousarray(Bm.transpose(0, 2, 1).reshape(128, D))
    wrt = np.ascontiguousarray(
        _tile_lhsT(np.ascontiguousarray(Wr.T), 8)[0])
    brr = np.ascontiguousarray(br.reshape(1, NE))

    in_maps = []
    for c in range(8):
        b, hg = c // 2, c % 2
        rows_o = _own_rows(hg)
        rows_p = _own_rows(1 - hg)
        cos_o, sin_o = _rope_tables(rows_o)
        cos_p, sin_p = _rope_tables(rows_p)
        in_maps.append({
            "x_own": np.ascontiguousarray(x[b][rows_o]),
            "x_part": np.ascontiguousarray(x[b][rows_p]),
            "wq": wq_t, "wk": wk_t, "wv": wv_t, "wo": wo_t,
            "acat": acat, "bcat": bcat, "wrt": wrt, "br": brr,
            "cos_o": cos_o, "sin_o": sin_o, "cos_p": cos_p, "sin_p": sin_p,
            "masks": _masks_for(hg),
        })
    return in_maps


def kernel(L0_attn_hidden_states, L0_attn_Wq, L0_attn_Wk, L0_attn_Wv, L0_attn_Wo,
           L0_ffn_A, L0_ffn_B, L0_ffn_Wr, L0_ffn_br):
    x = np.asarray(L0_attn_hidden_states, dtype=np.float32)
    in_maps = make_in_maps(x,
                           np.asarray(L0_attn_Wq, np.float32),
                           np.asarray(L0_attn_Wk, np.float32),
                           np.asarray(L0_attn_Wv, np.float32),
                           np.asarray(L0_attn_Wo, np.float32),
                           np.asarray(L0_ffn_A, np.float32),
                           np.asarray(L0_ffn_B, np.float32),
                           np.asarray(L0_ffn_Wr, np.float32),
                           np.asarray(L0_ffn_br, np.float32))
    nc = build_program()
    res = run_bass_kernel_spmd(nc, in_maps, core_ids=list(range(8)))
    out = np.empty((NB, S, D), dtype=np.float32)
    for c in range(8):
        b, hg = c // 2, c % 2
        out[b][_own_rows(hg)] = res.results[c]["out"]
    return out


# version 06
# speedup vs baseline: 1.2116x; 1.0506x over previous
import math

import numpy as np
import ml_dtypes

import concourse.bass as bass
import concourse.mybir as mybir
from concourse.bass_utils import run_bass_kernel_spmd

F32 = mybir.dt.float32
BF16 = mybir.dt.bfloat16
AF = mybir.ActivationFunctionType
ALU = mybir.AluOpType
AX = mybir.AxisListType

D = 4096
S = 4096
NB = 4
NH = 32
NKV = 8
HD = 128
KVD = NKV * HD
NE = 8
OWN = 2048
HALF = 1024
NQC = 4
NT_HALF = 8
RSCALE = float(1.0 / math.sqrt(HD))
LN_EPS = 1e-5

OWN_QCS = ([0, 3, 4, 7], [1, 2, 5, 6])
NK_UNIF = [8, 16, 24, 32]
SLOTS_P = [[0, 4], [0, 1, 4, 5], [0, 1, 2, 4, 5, 6], [0, 1, 2, 3, 4, 5, 6, 7]]

W_OC = 256
WO_OC = 256

B_PROJS = ("q", "k", "v")
B_ROPE = True
B_QNORM = True
A_TRANSPOSE = True
SBUF_TOP = 229344



def _tile_lhsT(wt, oc):
    f, o = wt.shape
    return np.ascontiguousarray(
        wt.reshape(f // 128, 128, o // oc, oc).transpose(2, 1, 0, 3).reshape(o // oc, 128, -1)
    )


def _rope_tables(positions):
    pos = np.asarray(positions, dtype=np.float64)[None, :]
    half = np.arange(64, dtype=np.float64)
    freqs = np.exp(-2.0 * half * (math.log(10000.0) / HD))[:, None]
    ang = pos * freqs
    cos = np.concatenate([np.cos(ang), np.cos(ang)], axis=0).astype(np.float32)
    sin = np.concatenate([np.sin(ang), np.sin(ang)], axis=0).astype(np.float32)
    return np.ascontiguousarray(cos), np.ascontiguousarray(sin)


def _masks_for(hg):
    own = OWN_QCS[hg]
    part = OWN_QCS[1 - hg]
    g_order = own + part
    out = np.zeros((NQC, 128, 4096), dtype=np.float32)
    kp = np.arange(128)[:, None]
    qf = np.arange(512)[None, :]
    for p in range(NQC):
        gq = own[p]
        for mi, slot in enumerate((p, p + 4)):
            g = g_order[slot]
            for j0 in range(4):
                if g < gq:
                    m = np.ones((128, 512), np.float32)
                elif g > gq:
                    m = np.zeros((128, 512), np.float32)
                else:
                    m = ((kp + j0 * 128) <= qf).astype(np.float32)
                out[p, :, mi * 2048 + j0 * 512: mi * 2048 + (j0 + 1) * 512] = m
    return np.ascontiguousarray(out.astype(ml_dtypes.bfloat16))


def _rope_perm_T():
    p = np.zeros((128, 128), np.float32)
    for dd in range(64):
        p[dd, dd + 64] = -1.0
        p[dd + 64, dd] = 1.0
    return np.ascontiguousarray(p.T.astype(ml_dtypes.bfloat16))



class Buf:
    __slots__ = ("name", "wtok", "rtoks")

    def __init__(self, name, carry=()):
        self.name = name
        self.wtok = None
        self.rtoks = list(carry)


class Sched:
    ENGINES = ("tensor", "vector", "scalar", "gpsimd", "sync")

    def __init__(self, nc):
        self.nc = nc
        self.ops = {e: [] for e in self.ENGINES}
        self.sem = {}
        self.cnt = {}
        self.waited = {e: {} for e in self.ENGINES}
        for e in ("tensor", "vector", "scalar", "gpsimd"):
            self.sem[e] = nc.alloc_semaphore(f"sem_{e}")
            self.cnt[e] = 0
        self._sn = 0

    def dma_slot(self, name):
        self.sem[name] = self.nc.alloc_semaphore(f"ds_{name}")
        self.cnt[name] = 0
        return name

    def _wait(self, eng, tok):
        if tok is None:
            return
        if eng == "tensor" and tok[0] == "tensor":
            return
        prev = self.waited[eng].get(tok[0], 0)
        if tok[1] <= prev:
            return
        self.waited[eng][tok[0]] = tok[1]
        sem = self.sem[tok[0]]
        val = tok[1]
        self.ops[eng].append(lambda e, s=sem, v=val: e.wait_ge(s, v))

    def op(self, eng, fn, reads=(), writes=(), slot=None, n_dma=1, self_inc=False):
        for b in reads:
            self._wait(eng, b.wtok)
        for b in writes:
            self._wait(eng, b.wtok)
            for t in b.rtoks:
                self._wait(eng, t)
        if slot is None:
            key, inc = eng, 1
            self.cnt[key] += 1
        else:
            key, inc = slot, 16
            self.cnt[key] += 16 * n_dma
        val = self.cnt[key]
        sem = self.sem[key]
        if self_inc:
            self.ops[eng].append(lambda e, f=fn, s=sem: f(e, s))
        else:
            self.ops[eng].append(lambda e, f=fn, s=sem, i=inc: f(e).then_inc(s, i))
        tok = (key, val)
        for b in reads:
            b.rtoks.append(tok)
        for b in writes:
            b.wtok = tok
            b.rtoks = []
        return tok

    def barrier_tokens(self, bufs):
        toks = []
        for b in bufs:
            if b.wtok is not None:
                toks.append(b.wtok)
            toks.extend(b.rtoks)
        return toks

    def finish(self, toks):
        for t in toks:
            self._wait("sync", t)



def build_program(phases="ABCDE"):
    nc = bass.Bass("TRN2", target_bir_lowering=False)

    def dram_in(name, shape, dtype=F32):
        return nc.dram_tensor(name, list(shape), dtype, kind="ExternalInput").ap()

    x_own = dram_in("x_own", (OWN, D))
    x_part = dram_in("x_part", (OWN, D))
    wq = dram_in("wq", (D // W_OC, 128, (D // 128) * W_OC))
    wk = dram_in("wk", (KVD // W_OC, 128, (D // 128) * W_OC))
    wv = dram_in("wv", (KVD // W_OC, 128, (D // 128) * W_OC))
    wo = dram_in("wo", (D // WO_OC, 128, (D // 128) * WO_OC))
    acat = dram_in("acat", (128, (D // 128) * 128))
    bcat = dram_in("bcat", (128, D))
    wrt = dram_in("wrt", (128, (D // 128) * 8))
    br_in = dram_in("br", (1, NE))
    cs_in = {(0, "c"): dram_in("cos_o", (128, OWN)),
             (0, "s"): dram_in("sin_o", (128, OWN)),
             (1, "c"): dram_in("cos_p", (128, OWN)),
             (1, "s"): dram_in("sin_p", (128, OWN))}
    masks_in = dram_in("masks", (NQC, 128, 4096), BF16)

    out_hbm = nc.dram_tensor("out", [OWN, D], F32, kind="ExternalOutput").ap()

    y1_d = nc.dram_tensor("y1_d", [OWN, D], F32, kind="Internal").ap()
    wqb = nc.dram_tensor("wqb", [D // W_OC, 128, (D // 128) * W_OC], BF16, kind="Internal").ap()
    wkb = nc.dram_tensor("wkb", [KVD // W_OC, 128, (D // 128) * W_OC], BF16, kind="Internal").ap()
    wvb = nc.dram_tensor("wvb", [KVD // W_OC, 128, (D // 128) * W_OC], BF16, kind="Internal").ap()
    wob = nc.dram_tensor("wob", [D // WO_OC, 128, (D // 128) * WO_OC], BF16, kind="Internal").ap()
    y2_d = nc.dram_tensor("y2_d", [OWN, D], F32, kind="Internal").ap()
    q_d = nc.dram_tensor("q_d", [NH * HD, OWN], BF16, kind="Internal").ap()
    k_d = nc.dram_tensor("k_d", [KVD, S], BF16, kind="Internal").ap()
    v_d = nc.dram_tensor("v_d", [S, KVD], BF16, kind="Internal").ap()
    ao_d = nc.dram_tensor("ao_d", [NH * HD, OWN], BF16, kind="Internal").ap()

    identf = nc.inline_tensor(np.eye(128, dtype=np.float32), "identf")
    identb = nc.inline_tensor(np.eye(128, dtype=ml_dtypes.bfloat16), "identb")
    ropePT = nc.inline_tensor(_rope_perm_T(), "ropePT")
    e8 = np.zeros((8, 128), np.float32)
    for h in range(8):
        e8[h, h * 16:(h + 1) * 16] = 1.0
    e8c = nc.inline_tensor(e8, "e8c")
    ones_col_b = nc.inline_tensor(np.ones((128, 1), dtype=ml_dtypes.bfloat16), "ones_col_b")
    ones_sq_b = nc.inline_tensor(np.ones((128, 128), dtype=ml_dtypes.bfloat16), "ones_sq_b")
    ones_col_f = nc.inline_tensor(np.ones((128, 1), dtype=np.float32), "ones_col_f")
    ones_row_f = nc.inline_tensor(np.ones((1, 128), dtype=np.float32), "ones_row_f")

    sb_idn = nc.alloc_sbuf_tensor("sb_idn", [128, 128], F32)
    sb_idnb = nc.alloc_sbuf_tensor("sb_idnb", [128, 128], BF16)
    sb_pT = nc.alloc_sbuf_tensor("sb_pT", [128, 128], BF16)
    sb_e8 = nc.alloc_sbuf_tensor("sb_e8", [8, 128], F32)
    sb_o1b = nc.alloc_sbuf_tensor("sb_o1b", [128, 1], BF16)
    sb_osq = nc.alloc_sbuf_tensor("sb_osq", [128, 128], BF16)
    sb_o1f = nc.alloc_sbuf_tensor("sb_o1f", [128, 1], F32)
    sb_orf = nc.alloc_sbuf_tensor("sb_orf", [1, 128], F32)
    sb_acat = nc.alloc_sbuf_tensor("sb_acat", [128, (D // 128) * 128], BF16)
    sb_bcat = nc.alloc_sbuf_tensor("sb_bcat", [128, D], BF16)
    sb_wrt = nc.alloc_sbuf_tensor("sb_wrt", [128, (D // 128) * 8], BF16)
    sb_brr = nc.alloc_sbuf_tensor("sb_brr", [1, NE], F32)
    sb_brb = nc.alloc_sbuf_tensor("sb_brb", [128, NE], F32)
    sb_qn = nc.alloc_sbuf_tensor("sb_qn", [1, 128], F32)
    sb_bias = nc.alloc_sbuf_tensor("sb_bias", [128, NH], F32)

    ov_base = (nc.sbuf_base + 31) & ~31

    psA = nc.alloc_psum_tensor("psA", [128, 1024], F32)
    psT = nc.alloc_psum_tensor("psT", [128, 2048], BF16)
    psB = nc.alloc_psum_tensor("psB", [128, 1024], F32)
    pav = nc.alloc_psum_tensor("pav", [128, 512], F32)
    plB = nc.alloc_psum_tensor("plB", [128, 512], F32)

    sch = Sched(nc)
    SL = {k: sch.dma_slot(k) for k in (
        "const", "ffnw", "x0", "x1", "lnT0", "lnT1", "w0", "w1", "cs",
        "st_y1", "st_q0", "st_q1", "st_k0", "st_k1", "st_v0", "st_v1",
        "kvld", "attnld", "qld0", "qld1", "mld",
        "wc0", "wc1", "wc2", "wc3", "wc4", "wc5", "wc6", "wc7",
        "st_ao0", "st_ao1", "wo0", "wo1", "y1s0", "y1s1",
        "st_y20", "st_y21", "y2l0", "y2l1", "st_out")}

    overlay_bufs = []

    class Overlay:
        def __init__(self):
            self.off = ov_base
            self.carry = sch.barrier_tokens(overlay_bufs)
            overlay_bufs.clear()

        def t(self, name, shape, dtype):
            itm = {F32: 4, BF16: 2}[dtype]
            t = nc.alloc_sbuf_tensor_at(name, list(shape), dtype, offset=self.off)
            self.off += (shape[1] * itm + 31) & ~31
            assert self.off <= SBUF_TOP, f"overlay overflow at {name}: {self.off}"
            b = Buf(name, self.carry)
            overlay_bufs.append(b)
            return t, b

    ps_bufs = []

    def psum_phase(names):
        carry = sch.barrier_tokens(ps_bufs)
        d = {n: Buf(n, carry) for n in names}
        ps_bufs.clear()
        ps_bufs.extend(d.values())
        return d

    CB = {}
    for name, sb, src in (("idn", sb_idn, identf), ("idnb", sb_idnb, identb),
                          ("pT", sb_pT, ropePT),
                          ("e8", sb_e8, e8c), ("o1b", sb_o1b, ones_col_b), ("osq", sb_osq, ones_sq_b),
                          ("o1f", sb_o1f, ones_col_f), ("orf", sb_orf, ones_row_f)):
        CB[name] = Buf(name)
        sch.op("sync", lambda e, sb=sb, src=src: e.dma_start(out=sb[:], in_=src.ap()),
               writes=[CB[name]], slot=SL["const"])
    CB["brr"] = Buf("brr")
    sch.op("sync", lambda e: e.dma_start(out=sb_brr[:], in_=br_in),
           writes=[CB["brr"]], slot=SL["const"])
    _const_tok = ("const", sch.cnt["const"])
    for _n in ("idn", "idnb", "pT", "e8", "o1b", "osq", "o1f", "orf", "brr"):
        CB[_n].wtok = _const_tok
    for name, sb, src in (("acat", sb_acat, acat), ("bcat", sb_bcat, bcat),
                          ("wrt", sb_wrt, wrt)):
        CB[name] = Buf(name)
        sch.op("gpsimd", lambda e, sb=sb, src=src: e.dma_start(out=sb[:], in_=src),
               writes=[CB[name]], slot=SL["ffnw"])
    _ffnw_tok = ("ffnw", sch.cnt["ffnw"])
    for _n in ("acat", "bcat", "wrt"):
        CB[_n].wtok = _ffnw_tok


    ps = psum_phase(["pA0", "pA1", "pB0", "pB1", "pT0", "pT1", "pav", "plB"])
    CB["brb"] = Buf("brb")
    sch.op("tensor", lambda e: e.matmul(out=psA[:, 0:NE], lhsT=sb_orf[:], rhs=sb_brr[:],
                                        start=True, stop=True),
           reads=[CB["orf"], CB["brr"]], writes=[ps["pA0"]])
    sch.op("scalar", lambda e: e.copy(out=sb_brb[:], in_=psA[:, 0:NE]),
           reads=[ps["pA0"]], writes=[CB["brb"]])
    B_qn = Buf("qn")
    sch.op("vector", lambda e: e.memset(sb_qn[:], 0.0), writes=[B_qn])
    B_bias = Buf("bias")

    DB = {n: Buf(n) for n in ("xo", "xp", "y1", "y2", "q", "k", "v", "ao", "out",
                              "wq", "wk", "wv", "wo", "cs", "masks")}
    for bname, sname, srcw, dstw, nch in (("wqb", "wq", wq, wqb, D // W_OC),
                                          ("wkb", "wk", wk, wkb, KVD // W_OC),
                                          ("wvb", "wv", wv, wvb, KVD // W_OC),
                                          ("wob", "wo", wo, wob, D // WO_OC)):
        DB[bname] = Buf(bname)
    _wit = 0
    _wconv_plan = []
    for bname, sname, srcw, dstw, nch in (("wqb", "wq", wq, wqb, D // W_OC),
                                          ("wkb", "wk", wk, wkb, KVD // W_OC),
                                          ("wvb", "wv", wv, wvb, KVD // W_OC),
                                          ("wob", "wo", wo, wob, D // WO_OC)):
        _wconv_plan.append((bname, sname, srcw, dstw, nch))

    ov = Overlay()
    y1fm, b_y1fm = ov.t("y1fm", [128, 32 * HALF], BF16)
    xT, b_xT = [], []
    for i in range(2):
        t, b = ov.t(f"xT{i}", [128, D], F32)
        xT.append(t); b_xT.append(b)
    y1t, b_y1t = ov.t("y1t", [128, D], F32)
    lnt, b_lnt = ov.t("lnt", [128, D], BF16)
    xfm, b_xfm = ov.t("xfm", [128, D], BF16)
    wbuf, b_wbuf = [], []
    for i in range(2):
        t, b = ov.t(f"wbuf{i}", [128, 32 * W_OC], BF16)
        wbuf.append(t); b_wbuf.append(b)
    _wgate = [Buf("wg0"), Buf("wg1")]

    _wslot_of = {"wqb": ("wc0", "wc1"), "wkb": ("wc2", "wc3"),
                 "wvb": ("wc4", "wc5"), "wob": ("wc6", "wc7")}

    def emit_wconv(plan):
        for bname, sname, srcw, dstw, nch in plan:
            gates = [Buf(bname + "g0"), Buf(bname + "g1")]
            for j in range(nch):
                ws = j % 2
                sch.op("gpsimd", lambda e, j=j, srcw=srcw, dstw=dstw: e.dma_start(
                    out=dstw[j], in_=srcw[j]),
                    reads=[DB[sname]], writes=[gates[ws]],
                    slot=SL[_wslot_of[bname][ws]])
            tk = sch.op("gpsimd", lambda e: e.nop(), reads=[gates[0], gates[1]])
            DB[bname].wtok = tk

    emit_wconv([p for p in _wconv_plan if p[0] != "wob"])

    cs_cos, b_csc = ov.t("cs_cos", [128, HALF], F32)
    cs_sin, b_css = ov.t("cs_sin", [128, HALF], F32)
    stg, b_stg = [], []
    for i in range(4):
        t, b = ov.t(f"stg{i}", [128, 512], BF16)
        stg.append(t); b_stg.append(b)
    t1f, b_t1 = [], []
    t2f, b_t2 = [], []
    for i in range(2):
        t, b = ov.t(f"t1f{i}", [128, 512], F32)
        t1f.append(t); b_t1.append(b)
        t, b = ov.t(f"t2f{i}", [128, 512], F32)
        t2f.append(t); b_t2.append(b)
    qsq, b_qsq = ov.t("qsq", [128, 512], F32)

    class FFNCtx:
        def __init__(self, ovl, tag):
            self.stats, self.b_stats = ovl.t(f"stats{tag}", [128, 48], F32)
            self.lnsc, self.b_lnsc = ovl.t(f"lnsc{tag}", [128, 8], F32)
            self.ffns, self.b_ffns = ovl.t(f"ffns{tag}", [128, 32], F32)
            self.scrT, self.b_scrT = ovl.t(f"scrT{tag}", [8, 128], F32)
            self.scb, self.b_scb = ovl.t(f"scb{tag}", [128, 128], F32)
            self.lows, self.b_lows = ovl.t(f"lows{tag}", [128, 128], BF16)
            self.lowf, self.b_lowf = ovl.t(f"lowf{tag}", [128, 128], F32)
            self.ffo, self.b_ffo = ovl.t(f"ffo{tag}", [128, 512], F32)

    fc = FFNCtx(ov, "A")

    def emit_ln(fx, src_buf, src_t, dst_buf, dst_t):
        def f_stats(e):
            for g in range(8):
                ii = e.bn_stats(out=fx.stats[:, g * 6:(g + 1) * 6],
                                in_=src_t[:, g * 512:(g + 1) * 512])
            return ii
        sch.op("vector", f_stats, reads=[src_buf], writes=[fx.b_stats])
        sch.op("vector", lambda e: e.bn_aggr(out=fx.lnsc[:, 0:2], in_=fx.stats[:]),
               reads=[fx.b_stats], writes=[fx.b_lnsc])
        sch.op("vector", lambda e: e.tensor_scalar(out=fx.lnsc[:, 2:3], in0=fx.lnsc[:, 1:2],
                                                   scalar1=LN_EPS, scalar2=None, op0=ALU.add),
               reads=[fx.b_lnsc], writes=[fx.b_lnsc])
        sch.op("scalar", lambda e: e.activation(out=fx.lnsc[:, 4:5], in_=fx.lnsc[:, 2:3],
                                                func=AF.Ln),
               reads=[fx.b_lnsc], writes=[fx.b_lnsc])
        sch.op("scalar", lambda e: e.activation(out=fx.lnsc[:, 3:4], in_=fx.lnsc[:, 4:5],
                                                func=AF.Exp, scale=-0.5),
               reads=[fx.b_lnsc], writes=[fx.b_lnsc])
        sch.op("vector", lambda e: e.tensor_scalar(out=dst_t[:], in0=src_t[:],
                                                   scalar1=fx.lnsc[:, 0:1],
                                                   scalar2=fx.lnsc[:, 3:4],
                                                   op0=ALU.subtract, op1=ALU.mult),
               reads=[src_buf, fx.b_lnsc], writes=[dst_buf])

    def emit_T(src_buf, src_t, dst_buf, dst_t, colf, slotname):
        for g in range(8):
            par = g % 2
            pb = ps[f"pT{par}"]
            def f_tq(e, g=g, par=par):
                for k4 in range(4):
                    ft = g * 4 + k4
                    pt = psT[:, par * 1024 + k4 * 128: par * 1024 + (k4 + 1) * 128]
                    ii = e.transpose(out=pt, in_=src_t[:, ft * 128:(ft + 1) * 128],
                                     identity=sb_idnb[:])
                return ii
            sch.op("tensor", f_tq, reads=[src_buf, CB["idnb"]], writes=[pb])
            ft0 = g * 4
            stride = colf(ft0 + 1) - colf(ft0)
            if stride == 128:
                dview = dst_t[:, colf(ft0):colf(ft0) + 512]
                sview = psT[:, par * 1024: par * 1024 + 512]
            else:
                dview = dst_t[:].rearrange("p (f c) -> p f c", c=stride)[
                    :, colf(ft0) // stride: colf(ft0) // stride + 4,
                    (colf(ft0) % stride):(colf(ft0) % stride) + 128]
                sview = psT[:, par * 1024: par * 1024 + 512].rearrange(
                    "p (f c) -> p f c", c=128)
            sch.op("scalar", lambda e, dview=dview, sview=sview: e.copy(
                out=dview, in_=sview),
                reads=[ps[f"pT{par}"]], writes=[dst_buf])

    def emit_ffn(fx, xbuf, x_t, outbuf, out_t, xf_t, b_xf):
        def f_rout(e):
            for ft in range(32):
                ii = e.matmul(out=psB[:, 512:512 + NE],
                              lhsT=xf_t[:, ft * 128:(ft + 1) * 128],
                              rhs=sb_wrt[:, ft * 8:(ft + 1) * 8],
                              start=(ft == 0), stop=(ft == 31))
            return ii
        sch.op("tensor", f_rout, reads=[b_xf, CB["wrt"]], writes=[ps["pB1"]])
        sch.op("scalar", lambda e: e.copy(out=fx.ffns[:, 24:32], in_=psB[:, 512:512 + NE]),
               reads=[ps["pB1"]], writes=[fx.b_ffns])
        sch.op("vector", lambda e: e.tensor_tensor(out=fx.ffns[:, 0:NE], in0=fx.ffns[:, 24:32],
                                                   in1=sb_brb[:], op=ALU.add),
               reads=[fx.b_ffns, CB["brb"]], writes=[fx.b_ffns])
        sch.op("vector", lambda e: e.reduce_max(out=fx.ffns[:, 16:17], in_=fx.ffns[:, 0:NE],
                                                axis=AX.X, negate=True),
               reads=[fx.b_ffns], writes=[fx.b_ffns])
        sch.op("scalar", lambda e: e.activation(out=fx.ffns[:, 8:16], in_=fx.ffns[:, 0:NE],
                                                func=AF.Exp, bias=fx.ffns[:, 16:17],
                                                accum_out=fx.ffns[:, 17:18]),
               reads=[fx.b_ffns], writes=[fx.b_ffns])
        sch.op("vector", lambda e: e.reciprocal(out=fx.ffns[:, 18:19], in_=fx.ffns[:, 17:18]),
               reads=[fx.b_ffns], writes=[fx.b_ffns])
        sch.op("vector", lambda e: e.tensor_scalar(out=fx.ffns[:, 0:NE], in0=fx.ffns[:, 8:16],
                                                   scalar1=fx.ffns[:, 18:19], scalar2=8.0,
                                                   op0=ALU.mult, op1=ALU.mult),
               reads=[fx.b_ffns], writes=[fx.b_ffns])
        sch.op("tensor", lambda e: e.transpose(out=psB[0:8, 512:640], in_=fx.ffns[:, 0:NE],
                                               identity=sb_idn[:]),
               reads=[fx.b_ffns, CB["idn"]], writes=[ps["pB1"]])
        sch.op("scalar", lambda e: e.copy(out=fx.scrT[:], in_=psB[0:8, 512:640]),
               reads=[ps["pB1"]], writes=[fx.b_scrT])
        sch.op("tensor", lambda e: e.matmul(out=psB[:, 640:768], lhsT=sb_e8[:], rhs=fx.scrT[:],
                                            start=True, stop=True),
               reads=[CB["e8"], fx.b_scrT], writes=[ps["pB1"]])
        sch.op("scalar", lambda e: e.copy(out=fx.scb[:], in_=psB[:, 640:768]),
               reads=[ps["pB1"]], writes=[fx.b_scb])
        def f_low(e):
            for ft in range(32):
                ii = e.matmul(out=psB[:, 768:896],
                              lhsT=sb_acat[:, ft * 128:(ft + 1) * 128],
                              rhs=xf_t[:, ft * 128:(ft + 1) * 128],
                              start=(ft == 0), stop=(ft == 31))
            return ii
        sch.op("tensor", f_low, reads=[b_xf, CB["acat"]], writes=[ps["pB1"]])
        sch.op("scalar", lambda e: e.copy(out=fx.lowf[:], in_=psB[:, 768:896]),
               reads=[ps["pB1"]], writes=[fx.b_lowf])
        sch.op("vector", lambda e: e.tensor_tensor(out=fx.lows[:], in0=fx.lowf[:],
                                                   in1=fx.scb[:], op=ALU.mult),
               reads=[fx.b_lowf, fx.b_scb], writes=[fx.b_lows])
        for oc in range(8):
            pbuf = ps["pB0"]
            pt = psB[:, 0:512]
            sch.op("tensor", lambda e, oc=oc, pt=pt: e.matmul(
                out=pt, lhsT=fx.lows[:], rhs=sb_bcat[:, oc * 512:(oc + 1) * 512],
                start=True, stop=True),
                reads=[fx.b_lows, CB["bcat"]], writes=[pbuf])
            sch.op("scalar", lambda e, pt=pt: e.copy(out=fx.ffo[:], in_=pt),
                   reads=[pbuf], writes=[fx.b_ffo])
            sch.op("vector", lambda e, oc=oc: e.tensor_tensor(
                out=out_t[:, oc * 512:(oc + 1) * 512], in0=fx.ffo[:],
                in1=x_t[:, oc * 512:(oc + 1) * 512], op=ALU.add),
                reads=[fx.b_ffo, xbuf], writes=[outbuf])

    def phase_A(x_dram, b_xdram, half, own):
        for ti in range(NT_HALF):
            xs = ti % 2
            row0 = half * HALF + ti * 128
            sch.op("sync", lambda e, row0=row0, xs=xs: e.dma_start(
                out=xT[xs][:], in_=x_dram[row0:row0 + 128, :]),
                reads=[b_xdram], writes=[b_xT[xs]], slot=SL[f"x{xs}"])
            emit_ln(fc, b_xT[xs], xT[xs], b_lnt, lnt)
            emit_T(b_lnt, lnt, b_xfm, xfm, lambda ft: ft * 128, "lnT0")
            emit_ffn(fc, b_xT[xs], xT[xs], b_y1t, y1t, xfm, b_xfm)
            if own:
                sch.op("scalar", lambda e, row0=row0: e.dma_start(
                    out=y1_d[row0:row0 + 128, :], in_=y1t[:]),
                    reads=[b_y1t], writes=[DB["y1"]], slot=SL["st_y1"])
            emit_ln(fc, b_y1t, y1t, b_lnt, lnt)
            emit_T(b_lnt, lnt, b_y1fm, y1fm, lambda ft, ti=ti: ft * HALF + ti * 128, "lnT1")

    def rope_store(pname, o_global, slot_col, tc, pbuf, pt, half, sslot):
        st = b_stg[sslot]
        sch.op("scalar", lambda e: e.copy(out=stg[sslot][:], in_=pt),
               reads=[pbuf], writes=[st])
        if B_ROPE:
            csl = cs_cos[:, tc * 512:(tc + 1) * 512]
            ssl = cs_sin[:, tc * 512:(tc + 1) * 512]
            rsl = 2 + sslot
            sch.op("tensor", lambda e: e.matmul(out=pav[:], lhsT=sb_pT[:], rhs=stg[sslot][:],
                                                start=True, stop=True),
                   reads=[CB["pT"], st], writes=[ps["pav"]])
            sch.op("scalar", lambda e, rsl=rsl: e.copy(out=stg[rsl][:], in_=pav[:]),
                   reads=[ps["pav"]], writes=[b_stg[rsl]])
            sch.op("vector", lambda e, c=csl: e.tensor_tensor(out=t1f[sslot][:],
                                                              in0=stg[sslot][:],
                                                              in1=c, op=ALU.mult),
                   reads=[st, b_csc], writes=[b_t1[sslot]])
            sch.op("vector", lambda e, s=ssl, rsl=rsl: e.tensor_tensor(out=t2f[sslot][:],
                                                                       in0=stg[rsl][:],
                                                                       in1=s, op=ALU.mult),
                   reads=[b_stg[rsl], b_css], writes=[b_t2[sslot]])
            sch.op("vector", lambda e: e.tensor_tensor(out=stg[sslot][:], in0=t1f[sslot][:],
                                                       in1=t2f[sslot][:], op=ALU.add),
                   reads=[b_t1[sslot], b_t2[sslot]], writes=[st])
        if pname == "q":
            col0 = half * HALF + tc * 512
            sch.op("scalar", lambda e, o=o_global, c=col0: e.dma_start(
                out=q_d[o:o + 128, c:c + 512], in_=stg[sslot][:]),
                reads=[st], writes=[DB["q"]], slot=SL[f"st_q{sslot}"])
            h = o_global // 128
            if not B_QNORM:
                return
            sch.op("vector", lambda e: e.tensor_mul(out=qsq[:], in0=stg[sslot][:],
                                                    in1=stg[sslot][:]),
                   reads=[st], writes=[b_qsq])
            sch.op("tensor", lambda e: e.matmul(out=plB[0:1, :], lhsT=sb_o1f[:], rhs=qsq[:],
                                                start=True, stop=True),
                   reads=[CB["o1f"], b_qsq], writes=[ps["plB"]])
            sch.op("scalar", lambda e: e.copy(out=qsq[0:1, :], in_=plB[0:1, :]),
                   reads=[ps["plB"]], writes=[b_qsq])
            sch.op("vector", lambda e: e.reduce_max(out=sb_qn[0:1, 40:41], in_=qsq[0:1, :],
                                                    axis=AX.X),
                   reads=[b_qsq], writes=[B_qn])
            sch.op("vector", lambda e, h=h: e.tensor_tensor(out=sb_qn[0:1, h:h + 1],
                                                            in0=sb_qn[0:1, h:h + 1],
                                                            in1=sb_qn[0:1, 40:41], op=ALU.max),
                   reads=[B_qn], writes=[B_qn])
        else:
            c0 = slot_col * 512
            sch.op("scalar", lambda e, o=o_global, c=c0: e.dma_start(
                out=k_d[o:o + 128, c:c + 512], in_=stg[sslot][:]),
                reads=[st], writes=[DB["k"]], slot=SL[f"st_k{sslot}"])

    def phase_B(half, own):
        slot_base = (0 if own else 4) + 2 * half
        lo = half * HALF
        key = 0 if own else 1
        sch.op("sync", lambda e: e.dma_start(out=cs_cos[:], in_=cs_in[(key, "c")][:, lo:lo + HALF]),
               writes=[b_csc], slot=SL["cs"])
        sch.op("sync", lambda e: e.dma_start(out=cs_sin[:], in_=cs_in[(key, "s")][:, lo:lo + HALF]),
               writes=[b_css], slot=SL["cs"])
        b_csc.wtok = b_css.wtok = ("cs", sch.cnt["cs"])

        projs = ([("q", wqb, DB["wqb"], D // W_OC)] if own and "q" in B_PROJS else [])
        if "k" in B_PROJS:
            projs.append(("k", wkb, DB["wkb"], KVD // W_OC))
        if "v" in B_PROJS:
            projs.append(("v", wvb, DB["wvb"], KVD // W_OC))
        it = 0
        for pname, wsrc, wdb, nchunk in projs:
            for j in range(nchunk):
                ws = it % 2
                it += 1
                sch.op("sync", lambda e, j=j, ws=ws, wsrc=wsrc: e.dma_start(
                    out=wbuf[ws][:], in_=wsrc[j]),
                    reads=[wdb], writes=[b_wbuf[ws]], slot=SL[f"w{ws}"])
                if pname == "v":
                    for tt in range(NT_HALF):
                        sslot = tt % 2
                        pbuf = ps["pA0"] if sslot == 0 else ps["pA1"]
                        pt = psA[:, 0:512] if sslot == 0 else psA[:, 512:1024]
                        def f_vm(e, tt=tt, pt=pt, ws=ws):
                            for ft in range(32):
                                ii = e.matmul(
                                    out=pt[:, 0:W_OC],
                                    lhsT=y1fm[:, ft * HALF + tt * 128: ft * HALF + (tt + 1) * 128],
                                    rhs=wbuf[ws][:, ft * W_OC:(ft + 1) * W_OC],
                                    start=(ft == 0), stop=(ft == 31))
                            return ii
                        sch.op("tensor", f_vm, reads=[b_wbuf[ws], b_y1fm], writes=[pbuf])
                        st = b_stg[2 + sslot]
                        sch.op("scalar", lambda e, pt=pt, sslot=sslot: e.copy(
                            out=stg[2 + sslot][:, 0:W_OC], in_=pt[:, 0:W_OC]),
                            reads=[pbuf], writes=[st])
                        grow = (slot_base + tt // 4) * 512 + (tt % 4) * 128
                        sch.op("scalar", lambda e, g=grow, j=j, sslot=sslot: e.dma_start(
                            out=v_d[g:g + 128, j * W_OC:(j + 1) * W_OC],
                            in_=stg[2 + sslot][:, 0:W_OC]),
                            reads=[st], writes=[DB["v"]], slot=SL[f"st_v{sslot}"])
                else:
                    for o1 in range(W_OC // 128):
                        o_global = j * W_OC + o1 * 128
                        for tc in range(2):
                            sslot = (o1 * 2 + tc) % 2
                            pbuf = ps["pA0"] if sslot == 0 else ps["pA1"]
                            pt = psA[:, 0:512] if sslot == 0 else psA[:, 512:1024]
                            def f_mm(e, o1=o1, tc=tc, pt=pt, ws=ws):
                                for ft in range(32):
                                    ii = e.matmul(
                                        out=pt,
                                        lhsT=wbuf[ws][:, ft * W_OC + o1 * 128: ft * W_OC + (o1 + 1) * 128],
                                        rhs=y1fm[:, ft * HALF + tc * 512: ft * HALF + (tc + 1) * 512],
                                        start=(ft == 0), stop=(ft == 31))
                                return ii
                            sch.op("tensor", f_mm, reads=[b_wbuf[ws], b_y1fm], writes=[pbuf])
                            rope_store(pname, o_global, slot_base + tc, tc, pbuf, pt, half, sslot)

    last_toks = []
    for half in range(2):
        phase_A(x_own, DB["xo"], half, own=True)
        if "B" in phases:
            phase_B(half, own=True)
    if "B" in phases:
        for half in range(2):
            phase_A(x_part, DB["xp"], half, own=False)
            phase_B(half, own=False)
    if "C" not in phases:
        for tt in range(OWN // 128):
            tk = sch.op("sync", lambda e, tt=tt: e.dma_start(
                out=out_hbm[tt * 128:(tt + 1) * 128, :],
                in_=y1_d[tt * 128:(tt + 1) * 128, :]),
                reads=[DB["y1"]], writes=[DB["out"]], slot=SL["st_out"])
            last_toks.append(tk)
        if "B" in phases:
            last_toks.append(DB["q"].wtok)
            last_toks.append(DB["k"].wtok)
            last_toks.append(DB["v"].wtok)
        sch.finish(last_toks)
        return _finish_block(nc, sch)

    emit_wconv([p for p in _wconv_plan if p[0] == "wob"])

    ovc = Overlay()
    K_sb, b_K = ovc.t("K_sb", [128, NKV * S], BF16)
    V_sb, b_V = ovc.t("V_sb", [128, (S // 128) * KVD], BF16)
    qtl, b_qtl = [], []
    for i in range(2):
        t, b = ovc.t(f"qtl{i}", [128, 512], BF16)
        qtl.append(t); b_qtl.append(b)
    exb, b_exb = [], []
    for i in range(2):
        t, b = ovc.t(f"exb{i}", [128, 1536], BF16)
        exb.append(t); b_exb.append(b)
    maskb, b_maskb = ovc.t("maskb", [128, 4096], BF16)
    rlb, b_rlb = ovc.t("rlb", [128, 512], F32)
    rlr, b_rlr = ovc.t("rlr", [1, 512], F32)
    aost, b_aost = [], []
    for i in range(2):
        t, b = ovc.t(f"aost{i}", [128, 512], BF16)
        aost.append(t); b_aost.append(b)
    ksq, b_ksq = ovc.t("ksq", [128, 4096], F32)
    avf, b_avf = ovc.t("avf", [128, 512], F32)

    psc = psum_phase(["pA0", "pA1", "pB0", "pB1", "pT0", "pT1", "pav", "plB"])

    def f_kld(e, sem):
        for kv in range(NKV):
            e.dma_start(out=K_sb[:, kv * S:(kv + 1) * S],
                        in_=k_d[kv * 128:(kv + 1) * 128, :].rearrange("d pos -> d pos")
                        ).then_inc(sem, 16)
    sch.op("sync", f_kld, reads=[DB["k"]], writes=[b_K], slot=SL["kvld"],
           n_dma=NKV, self_inc=True)

    def f_vld(e, sem):
        for g in range(4):
            e.dma_start(
                out=V_sb[:, g * 8 * KVD:(g + 1) * 8 * KVD].rearrange(
                    "p (pt c) -> p pt c", pt=8),
                in_=v_d[g * 1024:(g + 1) * 1024, :].rearrange(
                    "(pt p) c -> p pt c", p=128)).then_inc(sem, 16)
    sch.op("sync", f_vld, reads=[DB["v"]], writes=[b_V], slot=SL["kvld"],
           n_dma=4, self_inc=True)
    b_K.wtok = b_V.wtok = ("kvld", sch.cnt["kvld"])

    for kv in range(NKV):
        sch.op("vector", lambda e, kv=kv: e.tensor_mul(
            out=ksq[:], in0=K_sb[:, kv * S:(kv + 1) * S], in1=K_sb[:, kv * S:(kv + 1) * S]),
            reads=[b_K], writes=[b_ksq])
        for c8 in range(8):
            sch.op("tensor", lambda e, c8=c8: e.matmul(
                out=plB[0:1, :], lhsT=sb_o1f[:], rhs=ksq[:, c8 * 512:(c8 + 1) * 512],
                start=True, stop=True),
                reads=[CB["o1f"], b_ksq], writes=[psc["plB"]])
            sch.op("scalar", lambda e: e.copy(out=rlr[:], in_=plB[0:1, :]),
                   reads=[psc["plB"]], writes=[b_rlr])
            sch.op("vector", lambda e: e.reduce_max(out=sb_qn[0:1, 40:41], in_=rlr[:],
                                                    axis=AX.X),
                   reads=[b_rlr], writes=[B_qn])
            sch.op("vector", lambda e, kv=kv: e.tensor_tensor(
                out=sb_qn[0:1, 32 + kv:33 + kv], in0=sb_qn[0:1, 32 + kv:33 + kv],
                in1=sb_qn[0:1, 40:41], op=ALU.max),
                reads=[B_qn], writes=[B_qn])

    for h in range(NH):
        kv = h // 4
        sch.op("vector", lambda e, h=h, kv=kv: e.tensor_tensor(
            out=sb_qn[0:1, 41:42], in0=sb_qn[0:1, h:h + 1],
            in1=sb_qn[0:1, 32 + kv:33 + kv], op=ALU.mult),
            reads=[B_qn], writes=[B_qn])
        sch.op("scalar", lambda e: e.activation(out=sb_qn[0:1, 43:44], in_=sb_qn[0:1, 41:42],
                                                func=AF.Ln),
               reads=[B_qn], writes=[B_qn])
        sch.op("scalar", lambda e: e.activation(out=sb_qn[0:1, 42:43], in_=sb_qn[0:1, 43:44],
                                                func=AF.Exp, scale=0.5),
               reads=[B_qn], writes=[B_qn])
        sch.op("vector", lambda e, h=h: e.tensor_scalar(
            out=sb_qn[0:1, 64 + h:65 + h], in0=sb_qn[0:1, 42:43],
            scalar1=-RSCALE, scalar2=None, op0=ALU.mult),
            reads=[B_qn], writes=[B_qn])
    sch.op("tensor", lambda e: e.matmul(out=pav[:, 0:NH], lhsT=sb_orf[:],
                                        rhs=sb_qn[0:1, 64:64 + NH], start=True, stop=True),
           reads=[CB["orf"], B_qn], writes=[psc["pav"]])
    sch.op("scalar", lambda e: e.copy(out=sb_bias[:], in_=pav[:, 0:NH]),
           reads=[psc["pav"]], writes=[B_bias])

    for p in range(NQC):
        sch.op("sync", lambda e, p=p: e.dma_start(out=maskb[:], in_=masks_in[p]),
               reads=[DB["masks"]], writes=[b_maskb], slot=SL["mld"])
        tiles = [(sl, j0) for sl in SLOTS_P[p] for j0 in range(4)]
        nk = len(tiles)
        for h in range(NH):
            kv = h // 4
            qs = h % 2
            sch.op("sync", lambda e, p=p, h=h, qs=qs: e.dma_start(
                out=qtl[qs][:], in_=q_d[h * 128:(h + 1) * 128, p * 512:(p + 1) * 512]),
                reads=[DB["q"]], writes=[b_qtl[qs]], slot=SL[f"qld{qs}"])

            groups = [tiles[i:i + 2] for i in range(0, nk, 2)]

            def emit_scores(gi):
                grp = groups[gi]
                pb = [psc["pA0"], psc["pA1"]] if gi % 2 == 0 else [psc["pB0"], psc["pB1"]]
                pt = psA if gi % 2 == 0 else psB
                def f(e, grp=grp, pt=pt, qs=qs, kv=kv):
                    for i, (sl, j0) in enumerate(grp):
                        pos = sl * 4 + j0
                        ii = e.matmul(out=pt[:, i * 512:(i + 1) * 512],
                                      lhsT=K_sb[:, kv * S + pos * 128: kv * S + (pos + 1) * 128],
                                      rhs=qtl[qs][:], start=True, stop=True,
                                      skip_group_check=True)
                    return ii
                sch.op("tensor", f, reads=[b_K, b_qtl[qs]], writes=pb)

            def emit_rest(gi, first_tile_idx):
                grp = groups[gi]
                pb = [psc["pA0"], psc["pA1"]] if gi % 2 == 0 else [psc["pB0"], psc["pB1"]]
                pt = psA if gi % 2 == 0 else psB
                eb = b_exb[gi % 2]
                et = exb[gi % 2]
                n = len(grp)
                sch.op("scalar", lambda e, pt=pt, et=et, n=n, h=h: e.activation(
                    out=et[:, 0:n * 512], in_=pt[:, 0:n * 512], func=AF.Exp,
                    bias=sb_bias[:, h:h + 1], scale=RSCALE),
                    reads=pb + [B_bias], writes=[eb])
                for i, (sl, j0) in enumerate(grp):
                    if sl == p or sl == p + 4:
                        mi = 0 if sl == p else 1
                        mc = mi * 2048 + j0 * 512
                        sch.op("vector", lambda e, i=i, mc=mc, et=et: e.tensor_tensor(
                            out=et[:, i * 512:(i + 1) * 512], in0=et[:, i * 512:(i + 1) * 512],
                            in1=maskb[:, mc:mc + 512], op=ALU.mult),
                            reads=[eb, b_maskb], writes=[eb])
                def f_avl(e, grp=grp, et=et, kv=kv, fti=first_tile_idx):
                    for i, (sl, j0) in enumerate(grp):
                        ti = fti + i
                        pos = sl * 4 + j0
                        e.matmul(out=pav[:],
                                 lhsT=V_sb[:, pos * KVD + kv * 128: pos * KVD + (kv + 1) * 128],
                                 rhs=et[:, i * 512:(i + 1) * 512],
                                 start=(ti == 0), stop=(ti == nk - 1), skip_group_check=True)
                        ii = e.matmul(out=plB[:], lhsT=sb_osq[:],
                                      rhs=et[:, i * 512:(i + 1) * 512],
                                      start=(ti == 0), stop=(ti == nk - 1),
                                      skip_group_check=True)
                    return ii
                sch.op("tensor", f_avl, reads=[eb, b_V, CB["osq"]],
                       writes=[psc["pav"], psc["plB"]])

            emit_scores(0)
            ti0 = 0
            for gi in range(1, len(groups)):
                emit_scores(gi)
                emit_rest(gi - 1, ti0)
                ti0 += len(groups[gi - 1])
            emit_rest(len(groups) - 1, ti0)

            sch.op("scalar", lambda e: e.activation(out=rlb[:], in_=plB[:], func=AF.Ln),
                   reads=[psc["plB"]], writes=[b_rlb])
            sch.op("scalar", lambda e: e.activation(out=rlb[:], in_=rlb[:], func=AF.Exp,
                                                    scale=-1.0),
                   reads=[b_rlb], writes=[b_rlb])
            ao_s = h % 2
            sch.op("scalar", lambda e: e.copy(out=avf[:], in_=pav[:]),
                   reads=[psc["pav"]], writes=[b_avf])
            sch.op("vector", lambda e, ao_s=ao_s: e.tensor_tensor(
                out=aost[ao_s][:], in0=avf[:], in1=rlb[:], op=ALU.mult),
                reads=[b_avf, b_rlb], writes=[b_aost[ao_s]])
            sch.op("scalar", lambda e, p=p, h=h, ao_s=ao_s: e.dma_start(
                out=ao_d[h * 128:(h + 1) * 128, p * 512:(p + 1) * 512], in_=aost[ao_s][:]),
                reads=[b_aost[ao_s]], writes=[DB["ao"]], slot=SL[f"st_ao{ao_s}"])

    if "D" not in phases:
        for tt in range(OWN // 128):
            tk = sch.op("sync", lambda e, tt=tt: e.dma_start(
                out=out_hbm[tt * 128:(tt + 1) * 128, :],
                in_=y1_d[tt * 128:(tt + 1) * 128, :]),
                reads=[DB["y1"]], writes=[DB["out"]], slot=SL["st_out"])
        sch.finish([tk, DB["ao"].wtok])
        return _finish_block(nc, sch)

    ovd = Overlay()
    attn_sb, b_attn = ovd.t("attn_sb", [128, 32 * OWN], BF16)
    wobuf, b_wob = [], []
    for i in range(2):
        t, b = ovd.t(f"wob{i}", [128, 32 * WO_OC], BF16)
        wobuf.append(t); b_wob.append(b)
    y1s, b_y1s = [], []
    for i in range(2):
        t, b = ovd.t(f"y1s{i}", [128, WO_OC], F32)
        y1s.append(t); b_y1s.append(b)
    y2s, b_y2s = [], []
    for i in range(2):
        t, b = ovd.t(f"y2s{i}", [128, WO_OC], F32)
        y2s.append(t); b_y2s.append(b)
    y2f, b_y2f = [], []
    for i in range(2):
        t, b = ovd.t(f"y2f{i}", [128, WO_OC], F32)
        y2f.append(t); b_y2f.append(b)

    psd = psum_phase(["pA0", "pA1", "pB0", "pB1", "pT0", "pT1", "pav", "plB"])

    def f_ald(e, sem):
        for g in range(4):
            e.dma_start(
                out=attn_sb[:, g * 8 * OWN:(g + 1) * 8 * OWN].rearrange(
                    "p (hc q) -> p hc q", hc=8),
                in_=ao_d[g * 1024:(g + 1) * 1024, :].rearrange(
                    "(hc p) q -> p hc q", p=128)).then_inc(sem, 16)
    sch.op("sync", f_ald, reads=[DB["ao"]], writes=[b_attn], slot=SL["attnld"],
           n_dma=4, self_inc=True)

    for oc in range(D // WO_OC):
        ws = oc % 2
        sch.op("sync", lambda e, oc=oc, ws=ws: e.dma_start(out=wobuf[ws][:], in_=wob[oc]),
               reads=[DB["wob"]], writes=[b_wob[ws]], slot=SL[f"wo{ws}"])
        for tt in range(OWN // 128):
            sl2 = tt % 2
            pbuf = psd["pA0"] if sl2 == 0 else psd["pA1"]
            pt = psA[:, 0:WO_OC] if sl2 == 0 else psA[:, 512:512 + WO_OC]
            def f_wo(e, tt=tt, ws=ws, pt=pt):
                for hc in range(32):
                    ii = e.matmul(out=pt,
                                  lhsT=attn_sb[:, hc * OWN + tt * 128: hc * OWN + (tt + 1) * 128],
                                  rhs=wobuf[ws][:, hc * WO_OC:(hc + 1) * WO_OC],
                                  start=(hc == 0), stop=(hc == 31))
                return ii
            sch.op("tensor", f_wo, reads=[b_wob[ws], b_attn], writes=[pbuf])
            sch.op("sync", lambda e, tt=tt, oc=oc, sl2=sl2: e.dma_start(
                out=y1s[sl2][:], in_=y1_d[tt * 128:(tt + 1) * 128, oc * WO_OC:(oc + 1) * WO_OC]),
                reads=[DB["y1"]], writes=[b_y1s[sl2]], slot=SL[f"y1s{sl2}"])
            sch.op("scalar", lambda e, sl2=sl2, pt=pt: e.copy(out=y2f[sl2][:], in_=pt),
                   reads=[pbuf], writes=[b_y2f[sl2]])
            sch.op("vector", lambda e, sl2=sl2: e.tensor_tensor(
                out=y2s[sl2][:], in0=y2f[sl2][:], in1=y1s[sl2][:], op=ALU.add),
                reads=[b_y2f[sl2], b_y1s[sl2]], writes=[b_y2s[sl2]])
            sch.op("scalar", lambda e, tt=tt, oc=oc, sl2=sl2: e.dma_start(
                out=y2_d[tt * 128:(tt + 1) * 128, oc * WO_OC:(oc + 1) * WO_OC],
                in_=y2s[sl2][:]),
                reads=[b_y2s[sl2]], writes=[DB["y2"]], slot=SL[f"st_y2{sl2}"])

    if "E" not in phases:
        for tt in range(OWN // 128):
            tk = sch.op("sync", lambda e, tt=tt: e.dma_start(
                out=out_hbm[tt * 128:(tt + 1) * 128, :],
                in_=y2_d[tt * 128:(tt + 1) * 128, :]),
                reads=[DB["y2"]], writes=[DB["out"]], slot=SL["st_out"])
        sch.finish([tk])
        return _finish_block(nc, sch)

    ove = Overlay()
    y2T, b_y2T = [], []
    for i in range(2):
        t, b = ove.t(f"y2T{i}", [128, D], F32)
        y2T.append(t); b_y2T.append(b)
    y3t, b_y3t = [], []
    lntE, b_lntE = [], []
    xfmE, b_xfmE = [], []
    fcE = []
    for i in range(2):
        t, b = ove.t(f"y3t{i}", [128, D], F32)
        y3t.append(t); b_y3t.append(b)
        t, b = ove.t(f"lntE{i}", [128, D], BF16)
        lntE.append(t); b_lntE.append(b)
        t, b = ove.t(f"xfmE{i}", [128, D], BF16)
        xfmE.append(t); b_xfmE.append(b)
        fcE.append(FFNCtx(ove, f"E{i}"))

    pse = psum_phase(["pA0", "pA1", "pB0", "pB1", "pT0", "pT1", "pav", "plB"])
    ps.update(pse)

    out_toks = []

    def phase_E():
        for tt in range(OWN // 128):
            ysl = tt % 2
            row0 = tt * 128
            sch.op("sync", lambda e, row0=row0, ysl=ysl: e.dma_start(
                out=y2T[ysl][:], in_=y2_d[row0:row0 + 128, :]),
                reads=[DB["y2"]], writes=[b_y2T[ysl]], slot=SL[f"y2l{ysl}"])
            emit_ln(fcE[ysl], b_y2T[ysl], y2T[ysl], b_lntE[ysl], lntE[ysl])
            emit_T(b_lntE[ysl], lntE[ysl], b_xfmE[ysl], xfmE[ysl],
                   lambda ft: ft * 128, "lnT0" if ysl == 0 else "lnT1")
            emit_ffn(fcE[ysl], b_y2T[ysl], y2T[ysl], b_y3t[ysl], y3t[ysl],
                     xfmE[ysl], b_xfmE[ysl])
            tk = sch.op("sync", lambda e, row0=row0, ysl=ysl: e.dma_start(
                out=out_hbm[row0:row0 + 128, :], in_=y3t[ysl][:]),
                reads=[b_y3t[ysl]], writes=[DB["out"]], slot=SL["st_out"])
            out_toks.append(tk)

    phase_E()
    sch.finish(out_toks[-1:])
    return _finish_block(nc, sch)


def _finish_block(nc, sch):
    with nc.Block() as block:
        @block.sync
        def _(e):
            for f in sch.ops["sync"]:
                f(e)

        @block.tensor
        def _(e):
            for f in sch.ops["tensor"]:
                f(e)

        @block.vector
        def _(e):
            for f in sch.ops["vector"]:
                f(e)

        @block.scalar
        def _(e):
            for f in sch.ops["scalar"]:
                f(e)

        @block.gpsimd
        def _(e):
            for f in sch.ops["gpsimd"]:
                f(e)

    return nc



def _own_rows(hg):
    qcs = OWN_QCS[hg]
    return np.concatenate([np.arange(g * 512, (g + 1) * 512) for g in qcs])


def make_in_maps(x, Wq, Wk, Wv, Wo, A, Bm, Wr, br):
    wq_t = _tile_lhsT(np.ascontiguousarray(Wq.T), W_OC)
    wk_t = _tile_lhsT(np.ascontiguousarray(Wk.T), W_OC)
    wv_t = _tile_lhsT(np.ascontiguousarray(Wv.T), W_OC)
    wo_t = _tile_lhsT(np.ascontiguousarray(Wo.T), WO_OC)
    acat = np.ascontiguousarray(
        _tile_lhsT(np.ascontiguousarray(A.transpose(2, 0, 1).reshape(D, 128)), 128)[0])
    bcat = np.ascontiguousarray(Bm.transpose(0, 2, 1).reshape(128, D))
    wrt = np.ascontiguousarray(
        _tile_lhsT(np.ascontiguousarray(Wr.T), 8)[0])
    brr = np.ascontiguousarray(br.reshape(1, NE))

    in_maps = []
    for c in range(8):
        b, hg = c // 2, c % 2
        rows_o = _own_rows(hg)
        rows_p = _own_rows(1 - hg)
        cos_o, sin_o = _rope_tables(rows_o)
        cos_p, sin_p = _rope_tables(rows_p)
        in_maps.append({
            "x_own": np.ascontiguousarray(x[b][rows_o]),
            "x_part": np.ascontiguousarray(x[b][rows_p]),
            "wq": wq_t, "wk": wk_t, "wv": wv_t, "wo": wo_t,
            "acat": acat, "bcat": bcat, "wrt": wrt, "br": brr,
            "cos_o": cos_o, "sin_o": sin_o, "cos_p": cos_p, "sin_p": sin_p,
            "masks": _masks_for(hg),
        })
    return in_maps


def kernel(L0_attn_hidden_states, L0_attn_Wq, L0_attn_Wk, L0_attn_Wv, L0_attn_Wo,
           L0_ffn_A, L0_ffn_B, L0_ffn_Wr, L0_ffn_br):
    x = np.asarray(L0_attn_hidden_states, dtype=np.float32)
    in_maps = make_in_maps(x,
                           np.asarray(L0_attn_Wq, np.float32),
                           np.asarray(L0_attn_Wk, np.float32),
                           np.asarray(L0_attn_Wv, np.float32),
                           np.asarray(L0_attn_Wo, np.float32),
                           np.asarray(L0_ffn_A, np.float32),
                           np.asarray(L0_ffn_B, np.float32),
                           np.asarray(L0_ffn_Wr, np.float32),
                           np.asarray(L0_ffn_br, np.float32))
    nc = build_program()
    res = run_bass_kernel_spmd(nc, in_maps, core_ids=list(range(8)))
    out = np.empty((NB, S, D), dtype=np.float32)
    for c in range(8):
        b, hg = c // 2, c % 2
        out[b][_own_rows(hg)] = res.results[c]["out"]
    return out
